# version 05
# baseline (speedup 1.0000x reference)
import math
import os
import numpy as np
import ml_dtypes
from contextlib import ExitStack

import concourse.bass as bass
import concourse.mybir as mybir
from concourse.bass_utils import run_bass_kernel_spmd

F32 = mybir.dt.float32
BF16 = mybir.dt.bfloat16
I32 = mybir.dt.int32
ALU = mybir.AluOpType
AF = mybir.ActivationFunctionType
AX = mybir.AxisListType

NCORES = 8
B = 2
S = 4096
D = 2048
SL = S // NCORES
T = B * SL
NT = T // 128
NF = D // 128
H = 16
DK = 128
HID = 8192
GN16 = 1.0 / 16.0
SCL = 1.0 / math.sqrt(DK)
LN2 = math.log(2.0)

H_LOOP = int(os.environ.get("K_HLOOP", "16"))
HC_LOOP = int(os.environ.get("K_HCLOOP", "16"))
RACEDET = os.environ.get("K_RACEDET", "0") == "1"

ENGS = ("sync", "scalar", "vector", "tensor", "gpsimd")
V, Sc, Te, Gp, Sy = "vector", "scalar", "tensor", "gpsimd", "sync"


class Stream:
    def __init__(self, name, sem):
        self.name = name
        self.sem = sem
        self.count = 0
        self.items = []
        self.waited = {}

    def covered(self, ev):
        sem, val = ev
        return self.waited.get(id(sem), -1) >= val


class Prog:
    def __init__(self, nc, stack):
        self.nc = nc
        self.stack = stack
        self.streams = {}
        for e in ENGS:
            sem = stack.enter_context(nc.semaphore(f"sem_{e}"))
            self.streams[e] = Stream(e, sem)
        self.cc_sem = stack.enter_context(nc.semaphore("sem_cc"))
        self.cc_count = 0
        self.flows = {}

    def flow(self, name):
        if name not in self.flows:
            sem = self.stack.enter_context(self.nc.semaphore("f_" + name))
            self.flows[name] = [sem, 0]
        return self.flows[name]

    def op(self, eng, emit, deps=(), inc=1, inc_sem=None):
        st = self.streams[eng]
        need = {}
        for ev in deps:
            if ev is None:
                continue
            if not st.covered(ev):
                sem, val = ev
                k = id(sem)
                if k not in need or need[k][1] < val:
                    need[k] = (sem, val)
        waits = []
        for sem, val in need.values():
            waits.append((sem, val))
            st.waited[id(sem)] = max(st.waited.get(id(sem), -1), val)
        if inc_sem is None:
            st.count += inc
            st.items.append((waits, emit, (st.sem, inc)))
            return (st.sem, st.count)
        st.items.append((waits, emit, (inc_sem[0], inc)))
        return (inc_sem[0], inc_sem[1])

    def collective(self, kind, alu, ins, outs, deps=()):
        self.cc_count += 1

        def emit(e):
            return e.collective_compute(kind, alu,
                                        replica_groups=[list(range(NCORES))],
                                        ins=ins, outs=outs)
        return self.op(Gp, emit, deps, inc=1, inc_sem=(self.cc_sem, self.cc_count))

    def barrier(self, include_cc=True):
        evs = [(st.sem, st.count) for st in self.streams.values() if st.count > 0]
        evs += [(f[0], f[1]) for f in self.flows.values() if f[1] > 0]
        if include_cc and self.cc_count:
            evs.append((self.cc_sem, self.cc_count))
        for st in self.streams.values():
            for ev in evs:
                if not st.covered(ev):
                    sem, val = ev
                    st.items.append(([(sem, val)], None, None))
                    st.waited[id(sem)] = max(st.waited.get(id(sem), -1), val)

    def finalize(self, block):
        def mk(st):
            def body(eng):
                for waits, emit, inc in st.items:
                    for sem, val in waits:
                        eng.wait_ge(sem, val)
                    if emit is not None:
                        ins = emit(eng)
                        if inc is not None:
                            ins.then_inc(inc[0], inc[1])
            return body
        for e in ENGS:
            getattr(block, e)(mk(self.streams[e]))


def build_program(taps=()):
    nc = bass.Bass("TRN2", target_bir_lowering=False,
                   detect_race_conditions=RACEDET)

    def din(name, shape, dt=F32):
        return nc.dram_tensor(name, list(shape), dt, kind="ExternalInput").ap()

    x_in = din("x", (T, D))
    wq_d = din("wq", (D, D)); wk_d = din("wk", (D, D)); wv_d = din("wv", (D, D))
    wg_d = din("wg", (D, D)); wo_d = din("wo", (D, D))
    wgf_d = din("wgf", (HID, D)); wuf_d = din("wuf", (HID, D))
    wdf_d = din("wdf", (D, HID))
    wqkv_d = din("wqkv", (3072, D)); wproj_d = din("wproj", (D, D))
    w1g_d = din("w1g_sh", (HID // NCORES, D))
    w1u_d = din("w1u_sh", (HID // NCORES, D))
    w1d_d = din("w1d_sh", (D // NCORES, HID))
    tri_d = din("tri", (128, 128), BF16)
    ident_d = din("ident", (128, 128), BF16)
    fmask_d = din("fmask", (128, NCORES))
    fmaski_d = din("fmaski", (128, NCORES))

    y_out = nc.dram_tensor("y", [T, D], F32, kind="ExternalOutput").ap()
    tap_t = {}

    def tap(name, shape, dt):
        if name in taps:
            tap_t[name] = nc.dram_tensor("tap_" + name, list(shape), dt,
                                         kind="ExternalOutput").ap()
            return tap_t[name]
        return None

    ccg_in = nc.dram_tensor("ccg_in", [2 * H, 128, 129], F32).ap()
    ccg_out = nc.dram_tensor("ccg_out", [NCORES, 2 * H, 128, 129], F32).ap()
    q1g_in = nc.dram_tensor("q1g_in", [HID // NCORES, D], BF16).ap()
    q1g_out = nc.dram_tensor("q1g_out", [HID, D], BF16).ap()
    q1u_in = nc.dram_tensor("q1u_in", [HID // NCORES, D], BF16).ap()
    q1u_out = nc.dram_tensor("q1u_out", [HID, D], BF16).ap()
    q1d_in = nc.dram_tensor("q1d_in", [D // NCORES, HID], BF16).ap()
    q1d_out = nc.dram_tensor("q1d_out", [D, HID], BF16).ap()
    sc_q = nc.dram_tensor("sc_q", [H, DK, T], BF16).ap()
    sc_o = nc.dram_tensor("sc_o", [H, DK, T], BF16).ap()
    sc_xn = nc.dram_tensor("sc_xn", [128, NF * T], BF16).ap()
    sc_xq = nc.dram_tensor("sc_xq", [128, NF * T], BF16).ap()

    gstack = ExitStack()
    with gstack:
        p = Prog(nc, gstack)

        def sb(st, *a, **k):
            return st.enter_context(nc.sbuf_tensor(*a, **k))

        def ps_(st, *a, **k):
            return st.enter_context(nc.psum_tensor(*a, **k))

        xres = sb(gstack, "xres", [128, NT * D], F32)
        trit = sb(gstack, "trit", [128, 128], BF16)
        idt = sb(gstack, "idt", [128, 128], BF16)
        cols = sb(gstack, "cols", [128, 256], F32)
        sinv_cols = sb(gstack, "sinv_cols", [128, NT], F32)
        acols = sb(gstack, "acols", [128, H * 8], F32)
        fmsk = sb(gstack, "fmsk", [128, NCORES], F32)
        fmski = sb(gstack, "fmski", [128, NCORES], F32)
        zcol = sb(gstack, "zcol", [128, 1], F32)
        mxc = sb(gstack, "mxc", [128, 3 * 64], F32)
        xres_last = {}
        ccol = sb(gstack, "ccol", [128, 16], F32)
        CONSTS = {}

        def ts_(eng, out, in0, s1, op0, s2=None, op1=None, deps=(), accum=None):
            if op1 is None:
                return p.op(eng, lambda e: e.tensor_scalar(
                    out=out, in0=in0, scalar1=s1, scalar2=None, op0=op0,
                    accum_out=accum), deps)
            return p.op(eng, lambda e: e.tensor_scalar(
                out=out, in0=in0, scalar1=s1, scalar2=s2, op0=op0, op1=op1,
                accum_out=accum), deps)

        def tt_(eng, out, in0, in1, alu, deps=()):
            return p.op(eng, lambda e: e.tensor_tensor(out=out, in0=in0,
                                                       in1=in1, op=alu), deps)

        def stt_(eng, out, in0, scalar, in1, op0, op1, deps=()):
            return p.op(eng, lambda e: e.scalar_tensor_tensor(
                out=out, in0=in0, scalar=scalar, in1=in1, op0=op0, op1=op1),
                deps)

        def act_(out, in_, func, bias=0.0, scale=1.0, accum=None, deps=()):
            return p.op(Sc, lambda e: e.activation(out=out, in_=in_, func=func,
                                                   bias=bias, scale=scale,
                                                   accum_out=accum), deps)

        def cp_(eng, out, in_, deps=()):
            return p.op(eng, lambda e: e.tensor_copy(out=out, in_=in_), deps)

        def red_(out, in_, alu, deps=(), absval=None, negate=None):
            return p.op(V, lambda e: e.tensor_reduce(
                out=out, in_=in_, axis=AX.X, op=alu,
                apply_absolute_value=absval, negate=negate), deps)

        def mm_(out, lhsT, rhs, start, stop, deps=()):
            return p.op(Te, lambda e: e.matmul(out=out, lhsT=lhsT, rhs=rhs,
                                               start=start, stop=stop), deps)

        def tr_(out, in_, deps=()):
            return p.op(Te, lambda e: e.transpose(out=out, in_=in_,
                                                  identity=idt[:]), deps)

        def scan_(out, d0, deps=()):
            return p.op(V, lambda e: e.tensor_tensor_scan(
                out=out, data0=d0, data1=d0, initial=0.0, op0=ALU.add,
                op1=ALU.bypass), deps)

        def ms_(eng, ap_, val, deps=()):
            return p.op(eng, lambda e: e.memset(ap_, val), deps)

        def recip_(out, in_, deps=()):
            return p.op(V, lambda e: e.reciprocal(out=out, in_=in_), deps)

        def dma(eng, out, in_, deps=(), transpose=False, flow="ld"):
            f = p.flow(flow)
            f[1] += 16
            return p.op(eng, lambda e: e.dma_start(out=out, in_=in_,
                                                   transpose=transpose), deps,
                        inc=16, inc_sem=(f[0], f[1]))

        def rtile(apx):
            return apx.rearrange("(t p) c -> t p c", p=128)

        def cc(v):
            if v not in CONSTS:
                i = len(CONSTS)
                ev = ms_(V, ccol[:, i:i + 1], float(v))
                CONSTS[v] = (ccol[:, i:i + 1], ev)
            return CONSTS[v][0]

        def ccev(v):
            cc(v)
            return CONSTS[v][1]

        def repl32(dst_flat, src_cols, nb, deps):
            d3 = dst_flat.rearrange("p (b e) -> p b e", e=32)
            ev = cp_(V, d3[:, :, 0:1],
                     src_cols.rearrange("p (b o) -> p b o", o=1), deps)
            k = 1
            while k < 32:
                kk = min(k, 32 - k)
                ev = cp_(V, d3[:, :, k:k + kk], d3[:, :, 0:kk], [ev])
                k += kk
            return ev

        def mx_scales(src_blocks, nb, iw, deps):
            amx = mxc[:, 0:nb]
            lg = mxc[:, 64:64 + nb]
            fg = mxc[:, 128:128 + nb]
            e1 = red_(amx, src_blocks, ALU.max, deps, absval=True)
            e2 = ts_(V, amx, amx, 1e-30, ALU.max, deps=[e1])
            e3 = act_(lg, amx, AF.Ln, deps=[e2])
            e4 = ts_(V, lg, lg, 1.0 / LN2, ALU.mult, 1e-9, ALU.add, deps=[e3])
            e5 = cp_(V, iw[:, 0:nb], lg, [e4])
            e6 = stt_(V, fg, iw[:, 0:nb], 0.0, lg, ALU.bypass, ALU.is_gt, [e5])
            e7 = tt_(V, lg, iw[:, 0:nb], fg, ALU.subtract, [e6])
            e8 = ts_(V, lg, lg, -2.0, ALU.add, deps=[e7])
            e9 = ts_(V, lg, lg, 127.0, ALU.min, -127.0, ALU.max, deps=[e8])
            er = act_(fg, lg, AF.Exp, scale=cc(-LN2), deps=[e9, ccev(-LN2)])
            ers = act_(lg, lg, AF.Exp, scale=cc(LN2), deps=[er, ccev(LN2)])
            return fg, lg, er, ers

        def mx_quant(in_f32, out_bf, W, base_inv, maxn, scr, deps,
                     out_free=(), in_more=()):
            nb = W // 32
            fg, lg, er, ers = mx_scales(
                in_f32.rearrange("p (b e) -> p b e", e=32), nb, scr["I"],
                list(deps))
            erp = repl32(scr["R"][:, 0:W], fg, nb, [er])
            ey = tt_(V, scr["A"][:, 0:W], in_f32, scr["R"][:, 0:W], ALU.mult,
                     [erp] + list(deps) + list(in_more))
            ersp = repl32(scr["R2"][:, 0:W], lg, nb, [ers])
            ea = act_(scr["S1"][:, 0:W], scr["A"][:, 0:W], AF.Abs, deps=[ey])
            es1 = ts_(V, scr["S2"][:, 0:W], scr["S1"][:, 0:W], 2.0, ALU.is_ge,
                      deps=[ea])
            es2 = ts_(V, scr["S1"][:, 0:W], scr["S1"][:, 0:W], 4.0, ALU.is_ge,
                      deps=[ea, es1])
            eu1 = ts_(V, out_bf[:, 0:W], scr["S2"][:, 0:W], -0.5, ALU.mult,
                      1.0, ALU.add, deps=[es1] + list(out_free))
            ev2 = tt_(V, scr["A"][:, 0:W], scr["A"][:, 0:W], out_bf[:, 0:W],
                      ALU.mult, [eu1])
            eu2 = ts_(V, out_bf[:, 0:W], scr["S1"][:, 0:W], -0.5, ALU.mult,
                      1.0, ALU.add, deps=[es2, ev2])
            ev3 = tt_(V, scr["A"][:, 0:W], scr["A"][:, 0:W], out_bf[:, 0:W],
                      ALU.mult, [eu2])
            ev4 = ts_(V, scr["A"][:, 0:W], scr["A"][:, 0:W], base_inv, ALU.mult,
                      0.5, ALU.add, deps=[ev3])
            eI = cp_(V, scr["I"][:, 0:W], scr["A"][:, 0:W], [ev4])
            eG = stt_(V, scr["R"][:, 0:W], scr["I"][:, 0:W], 0.0,
                      scr["A"][:, 0:W], ALU.bypass, ALU.is_gt, [eI])
            eK = tt_(V, out_bf[:, 0:W], scr["I"][:, 0:W], scr["R"][:, 0:W],
                     ALU.subtract, [eG])
            ew1 = ts_(V, scr["S2"][:, 0:W], scr["S2"][:, 0:W], 1.0, ALU.add,
                      deps=[eK])
            ew2 = stt_(V, scr["S1"][:, 0:W], scr["S1"][:, 0:W], 1.0,
                       scr["S2"][:, 0:W], ALU.add, ALU.mult, [ew1])
            ew3 = ts_(V, scr["S1"][:, 0:W], scr["S1"][:, 0:W], 1.0 / base_inv,
                      ALU.mult, deps=[ew2])
            eq = tt_(V, out_bf[:, 0:W], out_bf[:, 0:W], scr["S1"][:, 0:W],
                     ALU.mult, [ew3])
            ecl = ts_(V, out_bf[:, 0:W], out_bf[:, 0:W], maxn, ALU.min, -maxn,
                      ALU.max, deps=[eq])
            eo = tt_(V, out_bf[:, 0:W], out_bf[:, 0:W], scr["R2"][:, 0:W],
                     ALU.mult, [ecl, ersp])
            return eo, ey

        def mx_quant_gp(in_f32, out_bf, W, base_inv, maxn, scr, deps,
                        out_free=()):
            nb = W // 32
            fg, lg, er, ers = mx_scales(
                in_f32.rearrange("p (b e) -> p b e", e=32), nb, scr["I"],
                list(deps))
            erp = repl32g(scr["R"][:, 0:W], fg, nb, [er])
            ey = tt_(Gp, scr["A"][:, 0:W], in_f32, scr["R"][:, 0:W], ALU.mult,
                     [erp] + list(deps))
            ersp = repl32g(scr["R2"][:, 0:W], lg, nb, [ers])
            esq = tt_(Gp, scr["R"][:, 0:W], scr["A"][:, 0:W],
                      scr["A"][:, 0:W], ALU.mult, [ey])
            es1 = ts_(Gp, scr["S2"][:, 0:W], scr["R"][:, 0:W], 4.0, ALU.is_ge,
                      deps=[esq])
            es2 = ts_(Gp, scr["S1"][:, 0:W], scr["R"][:, 0:W], 16.0, ALU.is_ge,
                      deps=[esq, es1])
            eu1 = ts_(Gp, out_bf[:, 0:W], scr["S2"][:, 0:W], -0.5, ALU.mult,
                      1.0, ALU.add, deps=[es1] + list(out_free))
            ev2 = tt_(Gp, scr["A"][:, 0:W], scr["A"][:, 0:W], out_bf[:, 0:W],
                      ALU.mult, [eu1])
            eu2 = ts_(Gp, out_bf[:, 0:W], scr["S1"][:, 0:W], -0.5, ALU.mult,
                      1.0, ALU.add, deps=[es2, ev2])
            ev3 = tt_(Gp, scr["A"][:, 0:W], scr["A"][:, 0:W], out_bf[:, 0:W],
                      ALU.mult, [eu2])
            ev4 = ts_(Gp, scr["A"][:, 0:W], scr["A"][:, 0:W], base_inv,
                      ALU.mult, 0.5, ALU.add, deps=[ev3])
            eI = cp_(Gp, scr["I"][:, 0:W], scr["A"][:, 0:W], [ev4])
            eF = cp_(Gp, scr["R"][:, 0:W], scr["I"][:, 0:W], [eI])
            eD = tt_(Gp, scr["A"][:, 0:W], scr["R"][:, 0:W], scr["A"][:, 0:W],
                     ALU.subtract, [eF])
            eG = ts_(Gp, scr["A"][:, 0:W], scr["A"][:, 0:W], 0.0, ALU.is_gt,
                     deps=[eD])
            eK = tt_(Gp, out_bf[:, 0:W], scr["R"][:, 0:W], scr["A"][:, 0:W],
                     ALU.subtract, [eG])
            ew1 = ts_(Gp, scr["S2"][:, 0:W], scr["S2"][:, 0:W], 1.0, ALU.add,
                      deps=[eK])
            ea1 = ts_(Gp, scr["S1"][:, 0:W], scr["S1"][:, 0:W], 1.0, ALU.add,
                      deps=[eK])
            ew2 = tt_(Gp, scr["S1"][:, 0:W], scr["S1"][:, 0:W],
                      scr["S2"][:, 0:W], ALU.mult, [ew1, ea1])
            ew3 = ts_(Gp, scr["S1"][:, 0:W], scr["S1"][:, 0:W], 1.0 / base_inv,
                      ALU.mult, deps=[ew2])
            eq = tt_(Gp, out_bf[:, 0:W], out_bf[:, 0:W], scr["S1"][:, 0:W],
                     ALU.mult, [ew3])
            ecl = ts_(Gp, out_bf[:, 0:W], out_bf[:, 0:W], maxn, ALU.min, -maxn,
                      ALU.max, deps=[eq])
            eo = tt_(Gp, out_bf[:, 0:W], out_bf[:, 0:W], scr["R2"][:, 0:W],
                     ALU.mult, [ecl, ersp])
            return eo, ey

        def repl32g(dst_flat, src_cols, nb, deps):
            d3 = dst_flat.rearrange("p (b e) -> p b e", e=32)
            ev = cp_(Gp, d3[:, :, 0:1],
                     src_cols.rearrange("p (b o) -> p b o", o=1), deps)
            k = 1
            while k < 32:
                kk = min(k, 32 - k)
                ev = cp_(Gp, d3[:, :, k:k + kk], d3[:, :, 0:kk], [ev])
                k += kk
            return ev

        ev_tri = dma(Sy, trit[:], tri_d, flow="cst")
        ev_id = dma(Sy, idt[:], ident_d, flow="cst")
        ev_fm = dma(Sy, fmsk[:], fmask_d, flow="cst")
        ev_fmi = dma(Sy, fmski[:], fmaski_d, flow="cst")
        ev_cst = ev_fmi
        ev_z = ms_(V, zcol[:], 0.0)
        evx = []
        x_t = rtile(x_in)
        for t in range(NT):
            evx.append(dma(Sy, xres[:, t * D:(t + 1) * D], x_t[t]))
        ev_xall = evx[-1]

        ph12 = ExitStack()
        xn_fm = sb(ph12, "xn_fm", [128, NF * T], BF16, is_transpose_dest=True)
        pool = ExitStack()
        pA = sb(pool, "pA", [128, D], F32)
        pB = sb(pool, "pB", [128, D], F32)
        pC = sb(pool, "pC", [128, D], F32)
        pI = sb(pool, "pI", [128, D], I32)
        pBF = sb(pool, "pBF", [128, D], BF16)
        pBF2 = sb(pool, "pBF2", [128, D], BF16)
        pR2 = sb(pool, "pR2", [128, D], BF16)
        pO = sb(pool, "pO", [128, D], BF16)
        SCR = {"A": pA, "R": pC, "I": pI, "S1": pBF, "S2": pBF2, "R2": pR2}

        ph1 = ExitStack()
        xq_fm1 = sb(ph1, "xq_fm1", [128, NF * T], BF16, is_transpose_dest=True)

        def ln_cols(src_tile, base, eps, deps, dump, ddep):
            c = lambda i: cols[:, base + i:base + i + 1]
            e1 = red_(c(0), src_tile, ALU.add, deps)
            e2 = act_(dump[:], src_tile, AF.Square, accum=c(1),
                      deps=list(deps) + [ddep])
            e3 = ts_(V, c(2), c(0), 1.0 / D, ALU.mult, deps=[e1])
            e4 = ts_(V, c(3), c(1), 1.0 / D, ALU.mult, deps=[e2])
            e5 = tt_(V, c(4), c(2), c(2), ALU.mult, [e3])
            e6 = tt_(V, c(4), c(3), c(4), ALU.subtract, [e4, e5])
            e7 = act_(c(5), c(4), AF.Sqrt, bias=cc(eps), deps=[e6, ccev(eps)])
            e8 = recip_(c(5), c(5), [e7])
            e9 = stt_(V, c(6), c(2), -1.0, c(5), ALU.mult, ALU.mult, [e8])
            return c(5), c(6), e9

        ev_xnfm = []
        ev_xqfm1 = []
        pv_saf = pv_g = pv_tq = pv_bf = None
        for t in range(NT):
            xt = xres[:, t * D:(t + 1) * D]
            base = t * 24
            rc, nbc, e9 = ln_cols(xt, base, 1e-5, [ev_xall], pBF2, pv_tq)
            e_xn = act_(pA[:], xt, AF.Identity, bias=nbc, scale=rc,
                        deps=[e9, ev_xall, pv_saf])
            e_bf = cp_(V, pBF[:], pA[:], [e_xn, pv_tq])
            e_tp = dma(Sy, xn_fm.rearrange("p (f w) -> p f w", f=NF)[
                           :, :, t * 128:(t + 1) * 128],
                       pBF.rearrange("p (f q) -> p f q", q=128), [e_bf],
                       transpose=True, flow="t1")
            ev_xnfm.append(e_tp)
            rc2, nbc2, e9b = ln_cols(pA[:], base + 8, 1e-8, [e_xn], pBF2, None)
            pv_saf = e9b
            e_xnr = act_(pB[:], pA[:], AF.Identity, bias=nbc2, scale=rc2,
                         deps=[e9b, e_xn, pv_g])
            cmax = cols[:, base + 16:base + 17]
            csc = cols[:, base + 17:base + 18]
            e_mx = red_(cmax, pB[:], ALU.max, [e_xnr], absval=True)
            e_r = recip_(csc, cmax, [e_mx])
            e_s = ts_(V, csc, csc, 127.0, ALU.mult, deps=[e_r])
            e_si = recip_(sinv_cols[:, t:t + 1], csc, [e_s])
            e_v = ts_(V, pB[:], pB[:], csc, ALU.mult, 0.5, ALU.add,
                      deps=[e_s, e_xnr])
            e_i = cp_(V, pI[:], pB[:], [e_v])
            e_g = stt_(V, pC[:], pI[:], 0.0, pB[:], ALU.bypass, ALU.is_gt,
                       [e_i])
            pv_g = e_g
            e_k = tt_(V, pBF[:], pI[:], pC[:], ALU.subtract, [e_g, e_tp])
            e_cl = ts_(V, pBF[:], pBF[:], 127.0, ALU.min, -128.0, ALU.max,
                       deps=[e_k])
            e_tq = dma(Sy, xq_fm1.rearrange("p (f w) -> p f w", f=NF)[
                           :, :, t * 128:(t + 1) * 128],
                       pBF.rearrange("p (f q) -> p f q", q=128), [e_cl],
                       transpose=True, flow="t2")
            pv_tq = e_tq
            ev_xqfm1.append(e_tq)
        e_spxq = dma(Sc, sc_xq, xq_fm1[:], ev_xqfm1, flow="sp1")
        ph1.close()

        if tap("xnfm", (128, NF * T), BF16) is not None:
            dma(Sc, tap_t["xnfm"], xn_fm[:], ev_xnfm, flow="tap")
        if tap("sinv", (128, NT), F32) is not None:
            dma(Sc, tap_t["sinv"], sinv_cols[:], [e_si], flow="tap")

        phw = ExitStack()
        pL = sb(phw, "pL", [128, D], F32)
        ph1_done = [e_spxq, ev_xnfm[-1], e_si]
        ev_qg, ev_qu, ev_qd = [], [], []
        pv_ey = None
        pv_store = None
        for (src, dst, evlist) in ((w1g_d, q1g_in, ev_qg),
                                   (w1u_d, q1u_in, ev_qu)):
            src_t = rtile(src)
            dst_t = rtile(dst)
            for t in range(HID // NCORES // 128):
                el = dma(Sy, pL[:], src_t[t], [pv_ey] + ph1_done, flow="wlA")
                eo, pv_ey = mx_quant_gp(pL[:], pO, D, 2.0, 6.0, SCR,
                                        [el] + ph1_done, [pv_store])
                pv_store = dma(Sc, dst_t[t], pO[:], [eo], flow="st" + dst_t.tensor.name)
                evlist.append(pv_store)
        src_t = rtile(w1d_d)
        dst_t = rtile(q1d_in)
        for t in range(D // NCORES // 128):
            for cchunk in range(4):
                el = dma(Sy, pL[:], src_t[t][:, cchunk * D:(cchunk + 1) * D],
                         [pv_ey] + ph1_done, flow="wlA")
                eo, pv_ey = mx_quant_gp(pL[:], pO, D, 2.0, 6.0, SCR,
                                        [el] + ph1_done, [pv_store])
                pv_store = dma(Sc, dst_t[t][:, cchunk * D:(cchunk + 1) * D],
                               pO[:], [eo], flow="stD")
            ev_qd.append(pv_store)
        cc_qg = p.collective("AllGather", ALU.bypass, [q1g_in.opt()],
                             [q1g_out.opt()], ev_qg)
        cc_qu = p.collective("AllGather", ALU.bypass, [q1u_in.opt()],
                             [q1u_out.opt()], ev_qu)
        cc_qd = p.collective("AllGather", ALU.bypass, [q1d_in.opt()],
                             [q1d_out.opt()], ev_qd)
        if tap("qwg", (HID, D), BF16) is not None:
            dma(Sc, tap_t["qwg"], q1g_out, [cc_qg], flow="tap")
        phw.close()
        pool.close()
        p.barrier(include_cc=False)

        ph2 = ExitStack()
        gkl = sb(ph2, "gkl", [128, T], F32)
        An = sb(ph2, "An", [128, T], F32)
        gt1 = sb(ph2, "gt1", [128, 512], F32)
        qh = sb(ph2, "qh", [128, T], BF16)
        vh = sb(ph2, "vh", [128, T], BF16)
        oh = sb(ph2, "oh", [128, T], BF16)
        ktil = sb(ph2, "ktil", [128, 128], BF16)
        khat = sb(ph2, "khat", [128, 128], BF16)
        kth = sb(ph2, "kth", [128, 128], BF16)
        eqt = sb(ph2, "eqt", [128, 128], BF16)
        ekt = sb(ph2, "ekt", [128, 128], BF16)
        pbar = sb(ph2, "pbar", [128, 128], BF16)
        sbf = sb(ph2, "sbf", [128, 128], BF16)
        ssb = sb(ph2, "ssb", [128, 2 * 129], F32)
        wsl = sb(ph2, "wsl", [128, NF * 128], F32)
        wslb = [sb(ph2, f"wslb{i}", [128, NF * 128], BF16) for i in range(2)]
        gcolA = sb(ph2, "gcolA", [128, 8], F32)
        gcolB = sb(ph2, "gcolB", [128, 8], F32)
        gcolD = sb(ph2, "gcolD", [128, 8], F32)
        PS = [ps_(ph2, f"ps{i}", [128, 512], F32) for i in range(7)]
        PT = ps_(ph2, "pst", [128, 128], BF16)

        ev_ccg_w = []
        prev_tail = None
        prev_cast = None
        prev_sq = prev_so = None
        prev_ccg = None
        prev_dc = [None] * 8
        prev_scan_sread = None
        pv_kth = None
        for h in range(H_LOOP):
            def wslice(wd, buf, extra=()):
                nonlocal prev_cast
                e = dma(Sy, wsl.rearrange("p (f q) -> p f q", q=128),
                        wd.rearrange("(f p) o -> p f o", p=128)[
                            :, :, h * 128:(h + 1) * 128],
                        [prev_cast], flow="wsl")
                e2 = cp_(V, buf[:], wsl[:], [e] + list(extra))
                prev_cast = e2
                return e2

            e_wg = wslice(wg_d, wslb[0], [prev_tail])
            ev_scan = [None, None]
            for half in range(2):
                for kc in range(NF):
                    e_mm = mm_(PS[half][:],
                               wslb[0][:, kc * 128:(kc + 1) * 128],
                               xn_fm[:, kc * T + half * 512:
                                     kc * T + (half + 1) * 512],
                               start=(kc == 0), stop=(kc == NF - 1),
                               deps=([e_wg, prev_tail] + ev_xnfm)
                               if kc == 0 else ())
                e_c = ts_(V, gt1[:], PS[half][:], -87.0, ALU.max,
                          deps=[e_mm, prev_scan_sread])
                e_e = act_(gt1[:], gt1[:], AF.Exp, scale=cc(-1.0),
                           deps=[e_c, ccev(-1.0)])
                e_u = ts_(V, gt1[:], gt1[:], 1.0, ALU.add, deps=[e_e])
                e_l = act_(gkl[:, half * 512:(half + 1) * 512], gt1[:], AF.Ln,
                           deps=[e_u])
                ev_scan[half] = scan_(An[:, half * 512:(half + 1) * 512],
                                      gkl[:, half * 512:(half + 1) * 512],
                                      [e_l])
            ev_bc = [None] * 8
            for c in range(8):
                aoff_src = zcol[:] if c % 4 == 0 else \
                    An[:, (c // 4) * 512 + (c % 4) * 128 - 1:
                       (c // 4) * 512 + (c % 4) * 128]
                e_b1 = ts_(V, gcolA[:, c:c + 1], aoff_src, GN16, ALU.mult,
                           deps=[ev_scan[c // 4], ev_z, prev_dc[c]])
                e_b2 = ts_(V, gcolB[:, c:c + 1], aoff_src, -GN16, ALU.mult,
                           deps=[ev_scan[c // 4], prev_dc[c]])
                ev_bc[c] = (e_b1, e_b2)
            ev_ac = []
            for c in range(8):
                dst = acols[:, h * 8 + c:h * 8 + c + 1]
                if c % 4 == 0:
                    ev_ac.append(cp_(V, dst, zcol[:], [ev_z]))
                else:
                    cs = (c // 4) * 512 + (c % 4) * 128
                    ev_ac.append(cp_(V, dst, An[:, cs - 1:cs],
                                     [ev_scan[c // 4]]))
            e_wq = wslice(wq_d, wslb[1], [prev_tail])
            for half in range(2):
                for kc in range(NF):
                    e_mmq = mm_(PS[2 + half][:],
                                wslb[1][:, kc * 128:(kc + 1) * 128],
                                xn_fm[:, kc * T + half * 512:
                                      kc * T + (half + 1) * 512],
                                start=(kc == 0), stop=(kc == NF - 1),
                                deps=[e_wq, prev_tail] if kc == 0 else ())
            ev_qmm = e_mmq
            e_wk = wslice(wk_d, wslb[0], [e_mm])
            for half in range(2):
                for kc in range(NF):
                    e_mmk = mm_(PS[4 + half][:],
                                wslb[0][:, kc * 128:(kc + 1) * 128],
                                xn_fm[:, kc * T + half * 512:
                                      kc * T + (half + 1) * 512],
                                start=(kc == 0), stop=(kc == NF - 1),
                                deps=[e_wk, prev_tail] if kc == 0 else ())
            ev_kmm = e_mmk
            e_wv = wslice(wv_d, wslb[1], [ev_qmm])
            ev_vev = [None] * NT
            for tt in range(NT):
                sl6 = PS[6][:, (tt % 4) * 128:(tt % 4) * 128 + 128]
                dep0 = [e_wv, prev_tail]
                if tt > 0:
                    dep0.append(ev_vev[tt - 1])
                for kc in range(NF):
                    e_mmv = mm_(sl6,
                                xn_fm[:, kc * T + tt * 128:
                                      kc * T + (tt + 1) * 128],
                                wslb[1][:, kc * 128:(kc + 1) * 128],
                                start=(kc == 0), stop=(kc == NF - 1),
                                deps=dep0 if kc == 0 else ())
                ev_vev[tt] = cp_(V, vh[:, tt * 128:(tt + 1) * 128], sl6,
                                 [e_mmv])
            e_z0 = ms_(V, ssb[:], 0.0, [prev_ccg])
            ev_oev = [None] * 8
            e_qt_last = None
            pv_P = pv_tr = pv_D = pv_o1 = pv_oi = None
            pv_qt = pv_kt = None
            e_su = None
            e_dcr = None
            for c in range(8):
                b = c // 4
                cs = c * 128
                first_of_b = (c % 4 == 0)
                e_b1, e_b2 = ev_bc[c]
                e_eq = act_(eqt[:], An[:, cs:cs + 128], AF.Exp, scale=cc(-GN16),
                            bias=gcolA[:, c:c + 1],
                            deps=[e_b1, ev_scan[b], ccev(-GN16), pv_qt])
                e_ek = act_(ekt[:], An[:, cs:cs + 128], AF.Exp, scale=cc(GN16),
                            bias=gcolB[:, c:c + 1],
                            deps=[e_b2, ev_scan[b], ccev(GN16), pv_kt])
                qps = PS[2 + b][:, (c % 4) * 128:(c % 4) * 128 + 128]
                e_qt = stt_(V, qh[:, cs:cs + 128], eqt[:], SCL, qps,
                            ALU.mult, ALU.mult,
                            [e_eq, ev_qmm, prev_sq if c == 0 else None])
                e_qt_last = e_qt
                pv_qt = e_qt
                kps = PS[4 + b][:, (c % 4) * 128:(c % 4) * 128 + 128]
                e_kt = tt_(V, ktil[:], kps, ekt[:], ALU.mult,
                           [e_ek, ev_kmm, pv_P])
                pv_kt = e_kt
                e_dc = act_(gcolD[:, c:c + 1], An[:, cs + 127:cs + 128],
                            AF.Exp, scale=cc(-GN16), bias=gcolA[:, c:c + 1],
                            deps=[e_b1, ev_scan[b], prev_dc[c]])
                prev_dc[c] = e_dc
                e_kh = ts_(V, khat[:], ktil[:], gcolD[:, c:c + 1], ALU.mult,
                           deps=[e_kt, e_dc, pv_tr])
                e_tr = tr_(PT[:], khat[:],
                           [e_kh, ev_cst] + ([pv_kth] if pv_kth else []))
                pv_tr = e_tr
                e_kth = cp_(V, kth[:], PT[:], [e_tr, pv_D])
                pv_kth = e_kth
                e_P = mm_(PS[0][:, 0:128], ktil[:], qh[:, cs:cs + 128],
                          start=True, stop=True,
                          deps=[e_qt, e_kt,
                                ev_oev[c - 1] if c else prev_tail])
                pv_P = e_P
                e_pb = tt_(V, pbar[:], PS[0][:, 0:128], trit[:], ALU.mult,
                           [e_P, ev_cst, pv_o1])
                e_sb = cp_(V, sbf[:], ssb[:, b * 129:b * 129 + 128],
                           [e_z0, pv_oi, e_su])
                ops = PS[1][:, (c % 4) * 128:(c % 4) * 128 + 128]
                e_o1 = mm_(ops, vh[:, cs:cs + 128], pbar[:],
                           start=True, stop=first_of_b,
                           deps=[e_pb, ev_vev[c],
                                 ev_oev[c - 1] if c else prev_tail])
                pv_o1 = e_o1
                if not first_of_b:
                    e_o1 = mm_(ops, sbf[:], qh[:, cs:cs + 128],
                               start=False, stop=True, deps=[e_sb, e_qt])
                    pv_oi = e_o1
                e_oe = cp_(V, oh[:, cs:cs + 128], ops,
                           [e_o1, prev_so if c == 0 else None])
                ev_oev[c] = e_oe
                e_D = mm_(PS[6][:, 128:256], kth[:], vh[:, cs:cs + 128],
                          start=True, stop=True,
                          deps=[e_kth, ev_vev[c],
                                e_su if c else ev_vev[NT - 1]])
                pv_D = e_D
                e_su = stt_(V, ssb[:, b * 129:b * 129 + 128],
                            ssb[:, b * 129:b * 129 + 128],
                            gcolD[:, c:c + 1], PS[6][:, 128:256],
                            ALU.mult, ALU.add, [e_D, e_sb, e_dc])
                if c % 4 == 3:
                    e_dcr = act_(ssb[:, b * 129 + 128:b * 129 + 129],
                                 An[:, b * 512 + 511:b * 512 + 512], AF.Exp,
                                 scale=cc(-GN16), deps=[ev_scan[b], e_su])
            prev_scan_sread = e_dcr
            prev_sq = dma(Sc, sc_q[h], qh[:], [e_qt_last], flow="sq")
            prev_so = dma(Sc, sc_o[h], oh[:], ev_oev, flow="so")
            e_c0 = dma(Sc, ccg_in[h], ssb[:, 0:129], [e_su, e_dcr], flow="ccw")
            e_c1 = dma(Sc, ccg_in[H + h], ssb[:, 129:258], [e_su, e_dcr], flow="ccw")
            prev_ccg = e_c1
            ev_ccg_w += [prev_sq, prev_so, e_c0, e_c1]
            prev_tail = ev_oev[7]

        if "qh2" in taps:
            tap("qh2", (128, T), BF16)
            dma(Sc, tap_t["qh2"], qh[:], [prev_sq], flow="tap")
        if "vh2" in taps:
            tap("vh2", (128, T), BF16)
            dma(Sc, tap_t["vh2"], vh[:], ev_vev, flow="tap")
        if "oh2" in taps:
            tap("oh2", (128, T), BF16)
            dma(Sc, tap_t["oh2"], oh[:], ev_oev, flow="tap")
        if "An2" in taps:
            tap("An2", (128, T), F32)
            dma(Sc, tap_t["An2"], An[:], [prev_scan_sread, ev_scan[0], ev_scan[1]], flow="tap")
        if "ssb2" in taps:
            tap("ssb2", (128, 258), F32)
            dma(Sc, tap_t["ssb2"], ssb[:], [e_su, e_dcr], flow="tap")

        cc_g = p.collective("AllGather", ALU.bypass, [ccg_in.opt()],
                            [ccg_out.opt()], ev_ccg_w)
        ph2.close()
        ph12.close()
        p.barrier()

        ph3 = ExitStack()
        sfold = sb(ph3, "sfold", [128, 2 * H * 128], F32)
        o_all = sb(ph3, "o_all", [128, H * T], BF16)
        qh3 = sb(ph3, "qh3", [128, T], BF16)
        oh3 = sb(ph3, "oh3", [128, T], BF16)
        sbf3 = sb(ph3, "sbf3", [128, 128], BF16)
        PS3 = [ps_(ph3, f"ps3_{i}", [128, 512], F32) for i in range(4)]

        ph3a = ExitStack()
        tjl = sb(ph3a, "tjl", [128, 2 * H * 129], F32)
        e_fold = ms_(V, sfold[:], 0.0)
        fold_q = [e_fold] * (2 * H)
        for j in range(NCORES):
            e_lj = dma(Sy, tjl.rearrange("p (q c) -> p q c", c=129),
                       ccg_out[j].rearrange("q p c -> p q c"),
                       [cc_g, e_fold], flow="tj")
            e_m1 = ts_(V, tjl[:], tjl[:], fmsk[:, j:j + 1], ALU.mult,
                       deps=[e_lj, ev_cst])
            e_m2 = ts_(V, tjl.rearrange("p (q c) -> p q c", c=129)[
                           :, :, 128:129],
                       tjl.rearrange("p (q c) -> p q c", c=129)[
                           :, :, 128:129],
                       fmski[:, j:j + 1], ALU.add, deps=[e_m1, ev_cst])
            for q in range(2 * H):
                e_fold = stt_(V, sfold[:, q * 128:(q + 1) * 128],
                              sfold[:, q * 128:(q + 1) * 128],
                              tjl[:, q * 129 + 128:q * 129 + 129],
                              tjl[:, q * 129:q * 129 + 128],
                              ALU.mult, ALU.add, [e_m2, fold_q[q]])
                fold_q[q] = e_fold
        ph3a.close()

        ev_oall = []
        if H_LOOP < H:
            ev_oall.append(ms_(V, o_all[:], 0.0))
        pv_cor = None
        for h in range(H_LOOP):
            e_lq = dma(Sy, qh3[:], sc_q[h],
                       [e_fold, ev_oall[-1] if ev_oall else None], flow="lq")
            e_lo = dma(Sy, oh3[:], sc_o[h],
                       [e_fold, ev_oall[-1] if ev_oall else None], flow="lo")
            for c in range(8):
                b = c // 4
                qp = b * H + h
                e_bk = act_(cols[:, 200:201],
                            acols[:, h * 8 + c:h * 8 + c + 1],
                            AF.Exp, scale=cc(-GN16),
                            deps=[e_lq, ev_oall[-1] if ev_oall else None])
                e_sk = ts_(V, sbf3[:], sfold[:, qp * 128:(qp + 1) * 128],
                           cols[:, 200:201], ALU.mult,
                           deps=[e_bk, e_fold, pv_cor])
                e_cor = mm_(PS3[0][:, (c % 4) * 128:(c % 4) * 128 + 128],
                            sbf3[:], qh3[:, c * 128:(c + 1) * 128],
                            start=True, stop=True,
                            deps=[e_sk, e_lq,
                                  ev_oall[-1] if ev_oall else None])
                pv_cor = e_cor
                e_up = tt_(V, o_all[:, (h * 8 + c) * 128:
                                    (h * 8 + c + 1) * 128],
                           oh3[:, c * 128:(c + 1) * 128],
                           PS3[0][:, (c % 4) * 128:(c % 4) * 128 + 128],
                           ALU.add, [e_cor, e_lo])
                ev_oall.append(e_up)
        if tap("o", (128, H * T), BF16) is not None:
            dma(Sc, tap_t["o"], o_all[:], ev_oall, flow="tap")

        ph3b = ExitStack()
        wof = sb(ph3b, "wof", [128, NF * 512], F32)
        wofb = sb(ph3b, "wofb", [128, NF * 512], BF16)
        ev_wo = []
        pv_cw = None
        for oc in range(4):
            e_lw = dma(Sy, wof.rearrange("p (f q) -> p f q", q=512),
                       wo_d.rearrange("(f p) o -> p f o", p=128)[
                           :, :, oc * 512:(oc + 1) * 512],
                       [pv_cw, e_fold], flow="wo")
            e_cw = cp_(V, wofb[:], wof[:],
                       [e_lw, ev_wo[-1] if ev_wo else None])
            pv_cw = e_cw
            for tt in range(NT):
                bank = PS3[1 + tt % 2]
                dep0 = [e_cw] + ev_oall + \
                    ([ev_wo[-2]] if len(ev_wo) >= 2 else [])
                for hh in range(H):
                    e_mm = mm_(bank[:],
                               o_all[:, (hh * 8 + tt) * 128:
                                     (hh * 8 + tt + 1) * 128],
                               wofb[:, hh * 512:(hh + 1) * 512],
                               start=(hh == 0), stop=(hh == H - 1),
                               deps=dep0 if hh == 0 else ())
                e_ad = tt_(V, xres[:, tt * D + oc * 512:
                                   tt * D + (oc + 1) * 512],
                           xres[:, tt * D + oc * 512:
                                tt * D + (oc + 1) * 512],
                           bank[:], ALU.add, [e_mm, xres_last.get((tt, oc))])
                xres_last[(tt, oc)] = e_ad
                ev_wo.append(e_ad)
        ph3b.close()
        ph3.close()
        p.barrier()

        def ffn_pass(layer, xq_b, ev_act_ready, stk):
            L = f"l{layer}"
            if layer == 0:
                wld = sb(stk, L + "wld", [128, 1024], F32)
                wldd = sb(stk, L + "wldd", [128, 512], F32)
            else:
                wld = wldd = None
            wld_state = {}
            if layer == 0:
                wsg_g = sb(stk, L + "wsg_gu", [128, 1024], BF16)
                wsg_u = wsg_g
                wsg_d = sb(stk, L + "wsg_d", [128, 512], BF16)
            else:
                wsg_g = wsg_u = wsg_d = None
            wfm_g = sb(stk, L + "wfm_g", [128, NF * 512], BF16,
                       is_transpose_dest=True)
            wfm_u = sb(stk, L + "wfm_u", [128, NF * 512], BF16,
                       is_transpose_dest=True)
            wdfm = [sb(stk, L + f"wdfm{i}", [128, 4 * D], BF16,
                       is_transpose_dest=True) for i in range(2)]
            gsl = sb(stk, L + "gsl", [128, NT * 512], BF16)
            usl = sb(stk, L + "usl", [128, NT * 512], BF16)
            pfm = [sb(stk, L + f"pfm{i}", [128, 4 * T], BF16,
                      is_transpose_dest=True) for i in range(2)]
            PS5 = [ps_(stk, L + f"ps5_{i}", [128, 512], F32) for i in range(8)]
            if layer == 1:
                iq = {
                    "A": sb(stk, "iqA", [128, 512], F32),
                    "R": sb(stk, "iqR", [128, 512], F32),
                    "I": sb(stk, "iqI", [128, 512], I32),
                    "S1": sb(stk, "iqS1", [128, 512], BF16),
                    "S2": sb(stk, "iqS2", [128, 512], BF16),
                    "R2": sb(stk, "iqR2", [128, 512], BF16),
                }

            NHC_ = HC_LOOP

            def prep(stage, hc, wdram, quant_src, wld_ap, wsg_ap, dst_fm,
                     ntile, wchunk, state, fsuf=""):
                evs = []
                halves = 2 if (quant_src is None and wchunk == D) else 1
                wc = wchunk // halves
                for jj in range(ntile * halves):
                    j, hh = jj // halves, jj % halves
                    if wchunk == D:
                        srcap = rtile(wdram if quant_src is None else
                                      quant_src)[hc * ntile + j]
                        if halves == 2:
                            srcap = srcap[:, hh * 1024:(hh + 1) * 1024]
                    else:
                        srcap = rtile(wdram if quant_src is None else
                                      quant_src)[j][:, hc * 512:(hc + 1) * 512]
                    if quant_src is None:
                        lstate = wld_state if wld_ap is not wldd else state
                        e1 = dma(Sy, wld_ap[:, 0:wc], srcap,
                                 [lstate.get("sign")], flow=L + stage + "w")
                        e2 = act_(wsg_ap[:, 0:wc], wld_ap[:, 0:wc],
                                  AF.Sign,
                                  deps=[e1, evs[-1] if evs else
                                        state.get("tr"),
                                        lstate.get("trr")])
                        state["sign"] = e2
                        lstate["sign"] = e2
                    else:
                        nblk = wc // 128
                        fv = dst_fm.rearrange("p (f w) -> p f w",
                                              f=wchunk // 128)
                        e3 = dma(Sy, fv[:, hh * nblk:(hh + 1) * nblk,
                                        j * 128:(j + 1) * 128],
                                 srcap.rearrange("p (f q) -> p f q", q=128),
                                 [state.get("use")] if jj == 0 else (),
                                 transpose=True, flow=L + stage + "t" + fsuf)
                        evs.append(e3)
                        continue
                    nblk = wc // 128
                    fv = dst_fm.rearrange("p (f w) -> p f w", f=wchunk // 128)
                    e3 = dma(Sc, fv[:, hh * nblk:(hh + 1) * nblk,
                                    j * 128:(j + 1) * 128],
                             wsg_ap[:, 0:wc].rearrange(
                                 "p (f q) -> p f q", q=128),
                             [e2] + ([state.get("use")] if jj == 0 else []),
                             transpose=True, flow=L + stage + "t" + fsuf)
                    evs.append(e3)
                    if wld_ap is not wldd:
                        lstate["trr"] = e3
                state["tr"] = evs[-1]
                return evs

            def down_block(dn_evd, dn_evp, buf, dn_w):
                for tt in range(NT):
                    for oc in range(4):
                        bank = PS5[4 + (tt * 4 + oc) % 4]
                        dep0 = dn_evd + dn_evp +                             ([ev_dn[-4]] if len(ev_dn) >= 4 else [])
                        for ks in range(4):
                            e_mm = mm_(bank[:],
                                       buf[:, ks * T + tt * 128:
                                           ks * T + (tt + 1) * 128],
                                       dn_w[:, ks * D + oc * 512:
                                            ks * D + (oc + 1) * 512],
                                       start=(ks == 0), stop=(ks == 3),
                                       deps=dep0 if ks == 0 else ())
                        e_ad = tt_(V, xres[:, tt * D + oc * 512:
                                           tt * D + (oc + 1) * 512],
                                   xres[:, tt * D + oc * 512:
                                        tt * D + (oc + 1) * 512],
                                   bank[:], ALU.add,
                                   [e_mm, xres_last.get((tt, oc))])
                        xres_last[(tt, oc)] = e_ad
                        ev_dn.append(e_ad)

            st_g = {}
            st_u = {}
            st_d = {}
            ev_gev = [None] * NT
            ev_uev = [None] * NT
            ev_dn = []
            pend_down = None
            prev_ptr = None

            for hc in range(NHC_):
                st_g["use"] = ev_gev[-1]
                evg = prep("g", hc, wgf_d if layer == 0 else None,
                           None if layer == 0 else q1g_out,
                           wld if layer == 0 else None,
                           wsg_g, wfm_g, 4, D, st_g)
                st_u["use"] = ev_uev[-1]
                evu = prep("u", hc, wuf_d if layer == 0 else None,
                           None if layer == 0 else q1u_out,
                           wld if layer == 0 else None,
                           wsg_u, wfm_u, 4, D, st_u)
                for tt in range(NT):
                    bank = PS5[tt % 2]
                    dep0 = evg + list(ev_act_ready) +                         [ev_gev[tt], ev_gev[tt - 2] if tt >= 2 else None]
                    for kc in range(NF):
                        e_mm = mm_(bank[:],
                                   xq_b[:, kc * T + tt * 128:
                                        kc * T + (tt + 1) * 128],
                                   wfm_g[:, kc * 512:(kc + 1) * 512],
                                   start=(kc == 0), stop=(kc == NF - 1),
                                   deps=dep0 if kc == 0 else ())
                    if layer == 0:
                        e_ev = act_(gsl[:, tt * 512:(tt + 1) * 512], bank[:],
                                    AF.Silu, scale=sinv_cols[:, tt:tt + 1],
                                    deps=[e_mm, prev_ptr])
                    else:
                        e_ev = act_(gsl[:, tt * 512:(tt + 1) * 512], bank[:],
                                    AF.Silu, deps=[e_mm, prev_ptr])
                    ev_gev[tt] = e_ev
                ev_p = [None] * NT
                pbuf = pfm[hc % 2]
                for tt in range(NT):
                    bank = PS5[2 + tt % 2]
                    dep0 = evu + [ev_uev[tt],
                                  ev_uev[tt - 2] if tt >= 2 else None]
                    for kc in range(NF):
                        e_mm = mm_(bank[:],
                                   xq_b[:, kc * T + tt * 128:
                                        kc * T + (tt + 1) * 128],
                                   wfm_u[:, kc * 512:(kc + 1) * 512],
                                   start=(kc == 0), stop=(kc == NF - 1),
                                   deps=dep0 if kc == 0 else ())
                    if layer == 0:
                        e_ev = act_(usl[:, tt * 512:(tt + 1) * 512], bank[:],
                                    AF.Identity,
                                    scale=sinv_cols[:, tt:tt + 1],
                                    deps=[e_mm])
                    else:
                        e_ev = act_(usl[:, tt * 512:(tt + 1) * 512], bank[:],
                                    AF.Identity, deps=[e_mm])
                    ev_uev[tt] = e_ev
                    sl = gsl[:, tt * 512:(tt + 1) * 512]
                    e_p = tt_(V, sl, sl, usl[:, tt * 512:(tt + 1) * 512],
                              ALU.mult, [ev_uev[tt], ev_gev[tt]])
                    if layer == 1:
                        e_p, _ = mx_quant(sl, sl, 512, 8.0, 7.5, iq, [e_p])
                    e_tp = dma(Sc, pbuf.rearrange("p (hs w) -> p hs w", hs=4)[
                                   :, :, tt * 128:(tt + 1) * 128],
                               sl.rearrange("p (hs q) -> p hs q", q=128),
                               [e_p, ev_dn[-1] if ev_dn else None],
                               transpose=True, flow=L + "pt" + str(hc % 2))
                    ev_p[tt] = e_tp
                st_d["use"] = ev_dn[-1] if ev_dn else None
                evd = prep("d", hc, wdf_d if layer == 0 else None,
                           None if layer == 0 else q1d_out,
                           wldd, wsg_d, wdfm[hc % 2], NF, 512, st_d,
                           fsuf=str(hc % 2))
                if pend_down is not None:
                    dn_evd, dn_evp, dn_buf, dn_w = pend_down
                    down_block(dn_evd, dn_evp, dn_buf, dn_w)
                pend_down = (evd, ev_p, pbuf, wdfm[hc % 2])
                prev_ptr = ev_p[-1]
            dn_evd, dn_evp, dn_buf, dn_w = pend_down
            down_block(dn_evd, dn_evp, dn_buf, dn_w)
            if any((L + nm) in taps for nm in
                   ("dwdfm0", "dwdfm1", "dpfm0", "dpfm1", "dgsl", "dusl")):
                p.barrier()
                for nm, buf, w in (("dwdfm0", wdfm[0], 4 * D),
                                   ("dwdfm1", wdfm[1], 4 * D),
                                   ("dpfm0", pfm[0], 4 * T),
                                   ("dpfm1", pfm[1], 4 * T),
                                   ("dgsl", gsl, NT * 512),
                                   ("dusl", usl, NT * 512)):
                    if tap(L + nm, (128, w), BF16) is not None:
                        dma(Sc, tap_t[L + nm], buf[:], flow="tap")
                p.barrier()

        ph5 = ExitStack()
        xq_fm5 = sb(ph5, "xq_fm5", [128, NF * T], BF16)
        e_rld = dma(Sy, xq_fm5[:], sc_xq, flow="r5")
        ffn_pass(0, xq_fm5, [e_rld], ph5)
        ph5.close()
        p.barrier()
        if tap("x1", (T, D), F32) is not None:
            for t in range(NT):
                dma(Sc, rtile(tap_t["x1"])[t], xres[:, t * D:(t + 1) * D], flow="tap")
            p.barrier()

        ph67 = ExitStack()
        xq_fm6 = sb(ph67, "xq_fm6", [128, NF * T], BF16,
                    is_transpose_dest=True)
        ph6 = ExitStack()
        xn_fm6 = sb(ph6, "xn_fm6", [128, NF * T], BF16, is_transpose_dest=True)
        s6 = {
            "A": sb(ph6, "s6A", [128, D], F32),
            "R": sb(ph6, "s6R", [128, D], F32),
            "I": sb(ph6, "s6I", [128, D], I32),
            "S1": sb(ph6, "s6S1", [128, D], BF16),
            "S2": sb(ph6, "s6S2", [128, D], BF16),
            "R2": sb(ph6, "s6R2", [128, D], BF16),
        }
        s6X = sb(ph6, "s6X", [128, D], F32)
        s6BF = sb(ph6, "s6BF", [128, D], BF16)
        s6Q = sb(ph6, "s6Q", [128, D], BF16)

        ev_xnfm2 = []
        ev_xqfm2 = []
        pv_eyq = pv_tq6 = pv_tp6 = None
        for t in range(NT):
            xt = xres[:, t * D:(t + 1) * D]
            base = t * 24
            rc, nbc, e9 = ln_cols(xt, base, 1e-5, (), s6BF, pv_tp6)
            e_xn = act_(s6X[:], xt, AF.Identity, bias=nbc, scale=rc,
                        deps=[e9, pv_eyq])
            e_bf = cp_(V, s6BF[:], s6X[:], [e_xn, pv_tp6])
            e_tp = dma(Sy, xn_fm6.rearrange("p (f w) -> p f w", f=NF)[
                           :, :, t * 128:(t + 1) * 128],
                       s6BF.rearrange("p (f q) -> p f q", q=128), [e_bf],
                       transpose=True, flow="t3")
            pv_tp6 = e_tp
            ev_xnfm2.append(e_tp)
            eo, ey = mx_quant(s6X[:], s6Q, D, 8.0, 7.5, s6, [e_xn],
                              out_free=[pv_tq6])
            pv_eyq = ey
            e_tq = dma(Sy, xq_fm6.rearrange("p (f w) -> p f w", f=NF)[
                           :, :, t * 128:(t + 1) * 128],
                       s6Q.rearrange("p (f q) -> p f q", q=128), [eo],
                       transpose=True, flow="t4")
            pv_tq6 = e_tq
            ev_xqfm2.append(e_tq)
        if tap("qx1", (128, NF * T), BF16) is not None:
            dma(Sc, tap_t["qx1"], xq_fm6[:], ev_xqfm2, flow="tap")
        e_spxn = dma(Sc, sc_xn, xn_fm6[:], ev_xnfm2, flow="sp2")
        ph6.close()
        p.barrier()

        ph7 = ExitStack()
        ffn_pass(1, xq_fm6, ev_xqfm2, ph7)
        ph7.close()
        ph67.close()
        p.barrier()

        ph8 = ExitStack()
        qkvt = sb(ph8, "qkvt", [128, NT * 3072], BF16)
        a8 = sb(ph8, "a8", [128, 1024], F32)
        c8 = sb(ph8, "c8", [128, 64], F32)
        wl8 = sb(ph8, "wl8", [128, 2 * D], F32)
        wb8 = sb(ph8, "wb8", [128, D], BF16)
        wfm8 = sb(ph8, "wfm8", [128, NF * 512], BF16, is_transpose_dest=True)
        afm8 = sb(ph8, "afm8", [128, NF * 128], BF16, is_transpose_dest=True)
        PS8 = [ps_(ph8, f"ps8_{i}", [128, 512], F32) for i in range(4)]

        ph8a = ExitStack()
        xn_fm8 = sb(ph8a, "xn_fm8", [128, NF * T], BF16)
        ev_xn8 = [dma(Sy, xn_fm8[:], sc_xn, flow="r8")]

        ev_qkv = []
        pv_cast8 = None
        pv_w8 = []
        for oc in range(6):
            evw = []
            for j in range(4):
                buf = wl8[:, (j % 2) * D:(j % 2) * D + D]
                e1 = dma(Sy, buf, rtile(wqkv_d)[oc * 4 + j],
                         [evw[-2] if len(evw) >= 2 else pv_cast8],
                         flow="w8" + str(j % 2))
                e2 = cp_(V, wb8[:], buf, [e1, evw[-1] if evw else pv_cast8])
                pv_cast8 = e2
                e3 = dma(Sy, wfm8.rearrange("p (f q) -> p f q", f=NF)[
                             :, :, j * 128:(j + 1) * 128],
                         wb8.rearrange("p (f q) -> p f q", q=128),
                         [e2] + (pv_w8 if j == 0 else []), transpose=True,
                         flow="w8t")
                evw.append(e3)
            pv_w8 = []
            for tt in range(NT):
                bank = PS8[tt % 2]
                dep0 = evw + ev_xn8 + \
                    ([ev_qkv[-2]] if len(ev_qkv) >= 2 else [])
                for kc in range(NF):
                    e_mm = mm_(bank[:],
                               xn_fm8[:, kc * T + tt * 128:
                                      kc * T + (tt + 1) * 128],
                               wfm8[:, kc * 512:(kc + 1) * 512],
                               start=(kc == 0), stop=(kc == NF - 1),
                               deps=dep0 if kc == 0 else ())
                e_ev = cp_(V, qkvt[:, tt * 3072 + oc * 512:
                                   tt * 3072 + (oc + 1) * 512],
                           bank[:], [e_mm])
                ev_qkv.append(e_ev)
                pv_w8.append(e_ev)
        ph8a.close()
        if tap("qkv", (T, 3072), BF16) is not None:
            for t in range(NT):
                dma(Sc, rtile(tap_t["qkv"])[t],
                    qkvt[:, t * 3072:(t + 1) * 3072], ev_qkv, flow="tap")

        ev_att2 = []
        pv_gp = None
        for tt in range(4, NT):
            src = qkvt[:, tt * 3072:(tt + 1) * 3072]
            s0 = qkvt[:, (tt - 4) * 3072:(tt - 3) * 3072]
            q1v3 = src.rearrange("p (g c) -> p g c", g=4)[:, :, 0:512] \
                .rearrange("p g (i q) -> p g i q", q=128)
            k0v = s0.rearrange("p (g c) -> p g c", g=4)[:, :, 512:640]
            k1v = src.rearrange("p (g c) -> p g c", g=4)[:, :, 512:640]
            for gp2 in range(2):
                e_m0 = p.op(Gp, lambda e, qv=q1v3[:, gp2 * 2:gp2 * 2 + 2],
                            kk=k0v[:, gp2 * 2:gp2 * 2 + 2],
                            o=a8.rearrange("p (g i q) -> p g i q", g=2, q=128):
                            e.tensor_tensor(
                                out=o, in0=qv,
                                in1=kk.unsqueeze(2).broadcast_to(
                                    (128, 2, 4, 128)),
                                op=ALU.mult), ev_qkv + [pv_gp])
                e_r0 = red_(c8[:, 16 + gp2 * 8:24 + gp2 * 8],
                            a8.rearrange("p (x q) -> p x q", q=128),
                            ALU.add, [e_m0])
                pv_gp = e_r0
            for gp2 in range(2):
                e_m1 = p.op(Gp, lambda e, qv=q1v3[:, gp2 * 2:gp2 * 2 + 2],
                            kk=k1v[:, gp2 * 2:gp2 * 2 + 2],
                            o=a8.rearrange("p (g i q) -> p g i q", g=2, q=128):
                            e.tensor_tensor(
                                out=o, in0=qv,
                                in1=kk.unsqueeze(2).broadcast_to(
                                    (128, 2, 4, 128)),
                                op=ALU.mult), [pv_gp])
                e_r1 = red_(c8[:, 32 + gp2 * 8:40 + gp2 * 8],
                            a8.rearrange("p (x q) -> p x q", q=128),
                            ALU.add, [e_m1])
                pv_gp = e_r1
            e_d = tt_(V, c8[:, 48:64], c8[:, 32:48], c8[:, 16:32],
                      ALU.subtract, [pv_gp])
            e_e = act_(c8[:, 48:64], c8[:, 48:64], AF.Exp, scale=cc(-SCL),
                       deps=[e_d, ccev(-SCL)])
            e_a = ts_(V, c8[:, 48:64], c8[:, 48:64], 1.0, ALU.add, deps=[e_e])
            e_p1 = recip_(c8[:, 48:64], c8[:, 48:64], [e_a])
            for g in range(4):
                e_t = tt_(V, a8[:, g * 128:(g + 1) * 128],
                          src[:, g * 768 + 640:g * 768 + 768],
                          s0[:, g * 768 + 640:g * 768 + 768], ALU.subtract,
                          [e_p1])
                for qi in range(4):
                    h_ = g * 4 + qi
                    dstc = src[:, g * 768 + qi * 128:g * 768 + qi * 128 + 128]
                    e_sc = ts_(V, a8[:, 512 + h_ % 4 * 128:
                                     512 + (h_ % 4 + 1) * 128],
                               a8[:, g * 128:(g + 1) * 128],
                               c8[:, 48 + h_:49 + h_], ALU.mult, deps=[e_t])
                    e_o = tt_(V, dstc,
                              a8[:, 512 + h_ % 4 * 128:
                                 512 + (h_ % 4 + 1) * 128],
                              s0[:, g * 768 + 640:g * 768 + 768], ALU.add,
                              [e_sc])
                    ev_att2.append(e_o)
            pv_gp = e_o
        ev_att0 = []
        for tt in range(4):
            src = qkvt[:, tt * 3072:(tt + 1) * 3072]
            for g in range(4):
                for qi in range(4):
                    hcol = g * 768 + qi * 128
                    ev_att0.append(cp_(V, src[:, hcol:hcol + 128],
                                       src[:, g * 768 + 640:g * 768 + 768],
                                       ev_qkv + [pv_gp]))
        ev_att = ev_att2 + ev_att0
        if tap("attn1", (T, D), BF16) is not None:
            at = tap_t["attn1"]
            for t in range(NT):
                for g in range(4):
                    dma(Sc, rtile(at)[t][:, g * 512:(g + 1) * 512],
                        qkvt[:, t * 3072 + g * 768:t * 3072 + g * 768 + 512],
                        ev_att, flow="tap")

        ev_pj = []
        pv_cast8 = None
        pv_w8 = []
        for oc in range(4):
            evw = []
            for j in range(4):
                buf = wl8[:, (j % 2) * D:(j % 2) * D + D]
                e1 = dma(Sy, buf, rtile(wproj_d)[oc * 4 + j],
                         [evw[-2] if len(evw) >= 2 else pv_cast8],
                         flow="w8" + str(j % 2))
                e2 = cp_(V, wb8[:], buf, [e1, evw[-1] if evw else pv_cast8])
                pv_cast8 = e2
                e3 = dma(Sy, wfm8.rearrange("p (f q) -> p f q", f=NF)[
                             :, :, j * 128:(j + 1) * 128],
                         wb8.rearrange("p (f q) -> p f q", q=128),
                         [e2] + (pv_w8 if j == 0 else []) +
                         (ev_qkv if oc == 0 and j == 0 else []),
                         transpose=True, flow="w8t")
                evw.append(e3)
            pv_w8 = []
            for tt in range(NT):
                e_tr2 = None
                for g in range(4):
                    attv = qkvt[:, tt * 3072 + g * 768:
                                tt * 3072 + g * 768 + 512].rearrange(
                        "p (i q) -> p i q", q=128)
                    e_tr2 = dma(Sy, afm8[:, g * 512:(g + 1) * 512].rearrange(
                                    "p (i q) -> p i q", q=128),
                                attv,
                                ev_att + [ev_pj[-1] if ev_pj else None],
                                transpose=True, flow="af")
                bank = PS8[2 + tt % 2]
                dep0 = evw + [e_tr2] + \
                    ([ev_pj[-2]] if len(ev_pj) >= 2 else [])
                for kc in range(NF):
                    e_mm = mm_(bank[:],
                               afm8[:, kc * 128:(kc + 1) * 128],
                               wfm8[:, kc * 512:(kc + 1) * 512],
                               start=(kc == 0), stop=(kc == NF - 1),
                               deps=dep0 if kc == 0 else ())
                e_ad = tt_(V, xres[:, tt * D + oc * 512:
                                   tt * D + (oc + 1) * 512],
                           xres[:, tt * D + oc * 512:
                                tt * D + (oc + 1) * 512],
                           bank[:], ALU.add, [e_mm, xres_last.get((tt, oc))])
                xres_last[(tt, oc)] = e_ad
                ev_pj.append(e_ad)
                pv_w8.append(e_ad)
        ph8.close()
        p.barrier()

        ev_st = []
        y_t = rtile(y_out)
        for t in range(NT):
            ev_st.append(dma(Sy, y_t[t], xres[:, t * D:(t + 1) * D], flow="yst"))
        all_done = list(ev_st) + [(f[0], f[1]) for f in p.flows.values()]
        p.op(Sy, lambda e: e.nop(), all_done, inc=1)
        p.op(Sc, lambda e: e.nop(), all_done, inc=1)

        with nc.Block() as block:
            p.finalize(block)

    return nc, sorted(tap_t.keys())


_cache = {}


def kernel(L0_attn_x, L0_attn_Wq, L0_attn_Wk, L0_attn_Wv, L0_attn_Wg,
           L0_attn_Wo, L0_ffn_W_g, L0_ffn_W_u, L0_ffn_W_d,
           L1_attn_weight_qkv, L1_attn_weight_proj, L1_ffn_w_gate,
           L1_ffn_w_up, L1_ffn_w_down, _taps=(), _return_raw=False):
    x = np.ascontiguousarray(np.asarray(L0_attn_x, dtype=np.float32))
    f32 = lambda a: np.ascontiguousarray(np.asarray(a, dtype=np.float32))
    wq, wk, wv, wg, wo = map(f32, (L0_attn_Wq, L0_attn_Wk, L0_attn_Wv,
                                   L0_attn_Wg, L0_attn_Wo))
    wgf, wuf, wdf = map(f32, (L0_ffn_W_g, L0_ffn_W_u, L0_ffn_W_d))
    wqkv, wproj = map(f32, (L1_attn_weight_qkv, L1_attn_weight_proj))
    w1g, w1u, w1d = map(f32, (L1_ffn_w_gate, L1_ffn_w_up, L1_ffn_w_down))

    tri = np.triu(np.ones((128, 128), np.float32)).astype(ml_dtypes.bfloat16)
    ident = np.eye(128, dtype=ml_dtypes.bfloat16)

    key = tuple(_taps)
    if key not in _cache:
        _cache[key] = build_program(_taps)
    nc, tap_names = _cache[key]

    in_maps = []
    for c in range(NCORES):
        xs = np.ascontiguousarray(x[:, c * SL:(c + 1) * SL, :].reshape(T, D))
        fm = np.zeros((128, NCORES), np.float32)
        fm[:, :c] = 1.0
        in_maps.append({
            "x": xs, "wq": wq, "wk": wk, "wv": wv, "wg": wg, "wo": wo,
            "wgf": wgf, "wuf": wuf, "wdf": wdf, "wqkv": wqkv, "wproj": wproj,
            "w1g_sh": np.ascontiguousarray(
                w1g[c * (HID // NCORES):(c + 1) * (HID // NCORES)]),
            "w1u_sh": np.ascontiguousarray(
                w1u[c * (HID // NCORES):(c + 1) * (HID // NCORES)]),
            "w1d_sh": np.ascontiguousarray(
                w1d[c * (D // NCORES):(c + 1) * (D // NCORES)]),
            "tri": tri, "ident": ident,
            "fmask": np.ascontiguousarray(fm),
            "fmaski": np.ascontiguousarray(1.0 - fm),
        })

    res = run_bass_kernel_spmd(nc, in_maps, core_ids=list(range(NCORES)))
    if _return_raw:
        return res
    out = np.empty((B, S, D), np.float32)
    for c in range(NCORES):
        out[:, c * SL:(c + 1) * SL, :] = np.asarray(
            res.results[c]["y"]).reshape(B, SL, D)
    return out


# version 06
# speedup vs baseline: 1.4600x; 1.4600x over previous
import math
import os
import numpy as np
import ml_dtypes
from contextlib import ExitStack

import concourse.bass as bass
import concourse.mybir as mybir
from concourse.bass_utils import run_bass_kernel_spmd

F32 = mybir.dt.float32
BF16 = mybir.dt.bfloat16
I32 = mybir.dt.int32
ALU = mybir.AluOpType
AF = mybir.ActivationFunctionType
AX = mybir.AxisListType

NCORES = 8
B = 2
S = 4096
D = 2048
SL = S // NCORES
T = B * SL
NT = T // 128
NF = D // 128
H = 16
DK = 128
HID = 8192
GN16 = 1.0 / 16.0
SCL = 1.0 / math.sqrt(DK)
LN2 = math.log(2.0)

H_LOOP = int(os.environ.get("K_HLOOP", "16"))
HC_LOOP = int(os.environ.get("K_HCLOOP", "16"))
RACEDET = os.environ.get("K_RACEDET", "0") == "1"

ENGS = ("sync", "scalar", "vector", "tensor", "gpsimd")
V, Sc, Te, Gp, Sy = "vector", "scalar", "tensor", "gpsimd", "sync"


class Stream:
    def __init__(self, name, sem):
        self.name = name
        self.sem = sem
        self.count = 0
        self.items = []
        self.waited = {}

    def covered(self, ev):
        sem, val = ev
        return self.waited.get(id(sem), -1) >= val


class Prog:
    def __init__(self, nc, stack):
        self.nc = nc
        self.stack = stack
        self.streams = {}
        for e in ENGS:
            sem = stack.enter_context(nc.semaphore(f"sem_{e}"))
            self.streams[e] = Stream(e, sem)
        self.cc_sem = stack.enter_context(nc.semaphore("sem_cc"))
        self.cc_count = 0
        self.flows = {}

    def flow(self, name):
        if name not in self.flows:
            sem = self.stack.enter_context(self.nc.semaphore("f_" + name))
            self.flows[name] = [sem, 0]
        return self.flows[name]

    def op(self, eng, emit, deps=(), inc=1, inc_sem=None):
        st = self.streams[eng]
        need = {}
        for ev in deps:
            if ev is None:
                continue
            if not st.covered(ev):
                sem, val = ev
                k = id(sem)
                if k not in need or need[k][1] < val:
                    need[k] = (sem, val)
        waits = []
        for sem, val in need.values():
            waits.append((sem, val))
            st.waited[id(sem)] = max(st.waited.get(id(sem), -1), val)
        if inc_sem is None:
            st.count += inc
            st.items.append((waits, emit, (st.sem, inc)))
            return (st.sem, st.count)
        st.items.append((waits, emit, (inc_sem[0], inc)))
        return (inc_sem[0], inc_sem[1])

    def collective(self, kind, alu, ins, outs, deps=()):
        self.cc_count += 1

        def emit(e):
            return e.collective_compute(kind, alu,
                                        replica_groups=[list(range(NCORES))],
                                        ins=ins, outs=outs)
        return self.op(Gp, emit, deps, inc=1, inc_sem=(self.cc_sem, self.cc_count))

    def barrier(self, include_cc=True):
        evs = [(st.sem, st.count) for st in self.streams.values() if st.count > 0]
        evs += [(f[0], f[1]) for f in self.flows.values() if f[1] > 0]
        if include_cc and self.cc_count:
            evs.append((self.cc_sem, self.cc_count))
        for st in self.streams.values():
            for ev in evs:
                if not st.covered(ev):
                    sem, val = ev
                    st.items.append(([(sem, val)], None, None))
                    st.waited[id(sem)] = max(st.waited.get(id(sem), -1), val)

    def finalize(self, block):
        def mk(st):
            def body(eng):
                for waits, emit, inc in st.items:
                    for sem, val in waits:
                        eng.wait_ge(sem, val)
                    if emit is not None:
                        ins = emit(eng)
                        if inc is not None:
                            ins.then_inc(inc[0], inc[1])
            return body
        for e in ENGS:
            getattr(block, e)(mk(self.streams[e]))


def build_program(taps=()):
    nc = bass.Bass("TRN2", target_bir_lowering=False,
                   detect_race_conditions=RACEDET)

    def din(name, shape, dt=F32):
        return nc.dram_tensor(name, list(shape), dt, kind="ExternalInput").ap()

    x_in = din("x", (T, D))
    wq_d = din("wq", (D, D)); wk_d = din("wk", (D, D)); wv_d = din("wv", (D, D))
    wg_d = din("wg", (D, D)); wo_d = din("wo", (D, D))
    wgf_d = din("wgf", (HID, D)); wuf_d = din("wuf", (HID, D))
    wdf_d = din("wdf", (D, HID))
    wqkv_d = din("wqkv", (3072, D)); wproj_d = din("wproj", (D, D))
    w1g_d = din("w1g_sh", (HID // NCORES, D))
    w1u_d = din("w1u_sh", (HID // NCORES, D))
    w1d_d = din("w1d_sh", (D // NCORES, HID))
    tri_d = din("tri", (128, 128), BF16)
    ident_d = din("ident", (128, 128), BF16)
    fmask_d = din("fmask", (128, NCORES))
    fmaski_d = din("fmaski", (128, NCORES))

    y_out = nc.dram_tensor("y", [T, D], F32, kind="ExternalOutput").ap()
    tap_t = {}

    def tap(name, shape, dt):
        if name in taps:
            tap_t[name] = nc.dram_tensor("tap_" + name, list(shape), dt,
                                         kind="ExternalOutput").ap()
            return tap_t[name]
        return None

    ccg_in = nc.dram_tensor("ccg_in", [2 * H, 128, 129], F32).ap()
    ccg_out = nc.dram_tensor("ccg_out", [NCORES, 2 * H, 128, 129], F32).ap()
    q1g_in = nc.dram_tensor("q1g_in", [HID // NCORES, D], BF16).ap()
    q1g_out = nc.dram_tensor("q1g_out", [HID, D], BF16).ap()
    q1u_in = nc.dram_tensor("q1u_in", [HID // NCORES, D], BF16).ap()
    q1u_out = nc.dram_tensor("q1u_out", [HID, D], BF16).ap()
    q1d_in = nc.dram_tensor("q1d_in", [D // NCORES, HID], BF16).ap()
    q1d_out = nc.dram_tensor("q1d_out", [D, HID], BF16).ap()
    sc_q = nc.dram_tensor("sc_q", [H, DK, T], BF16).ap()
    sc_o = nc.dram_tensor("sc_o", [H, DK, T], BF16).ap()
    sc_xn = nc.dram_tensor("sc_xn", [128, NF * T], BF16).ap()
    sc_xq = nc.dram_tensor("sc_xq", [128, NF * T], BF16).ap()

    gstack = ExitStack()
    with gstack:
        p = Prog(nc, gstack)

        def sb(st, *a, **k):
            return st.enter_context(nc.sbuf_tensor(*a, **k))

        def ps_(st, *a, **k):
            return st.enter_context(nc.psum_tensor(*a, **k))

        xres = sb(gstack, "xres", [128, NT * D], F32)
        trit = sb(gstack, "trit", [128, 128], BF16)
        idt = sb(gstack, "idt", [128, 128], BF16)
        cols = sb(gstack, "cols", [128, 256], F32)
        sinv_cols = sb(gstack, "sinv_cols", [128, NT], F32)
        acols = sb(gstack, "acols", [128, H * 8], F32)
        fmsk = sb(gstack, "fmsk", [128, NCORES], F32)
        fmski = sb(gstack, "fmski", [128, NCORES], F32)
        zcol = sb(gstack, "zcol", [128, 1], F32)
        mxc = sb(gstack, "mxc", [128, 3 * 64], F32)
        xres_last = {}
        ccol = sb(gstack, "ccol", [128, 16], F32)
        CONSTS = {}

        def ts_(eng, out, in0, s1, op0, s2=None, op1=None, deps=(), accum=None):
            if op1 is None:
                return p.op(eng, lambda e: e.tensor_scalar(
                    out=out, in0=in0, scalar1=s1, scalar2=None, op0=op0,
                    accum_out=accum), deps)
            return p.op(eng, lambda e: e.tensor_scalar(
                out=out, in0=in0, scalar1=s1, scalar2=s2, op0=op0, op1=op1,
                accum_out=accum), deps)

        def tt_(eng, out, in0, in1, alu, deps=()):
            return p.op(eng, lambda e: e.tensor_tensor(out=out, in0=in0,
                                                       in1=in1, op=alu), deps)

        def stt_(eng, out, in0, scalar, in1, op0, op1, deps=()):
            return p.op(eng, lambda e: e.scalar_tensor_tensor(
                out=out, in0=in0, scalar=scalar, in1=in1, op0=op0, op1=op1),
                deps)

        def act_(out, in_, func, bias=0.0, scale=1.0, accum=None, deps=()):
            return p.op(Sc, lambda e: e.activation(out=out, in_=in_, func=func,
                                                   bias=bias, scale=scale,
                                                   accum_out=accum), deps)

        def cp_(eng, out, in_, deps=()):
            return p.op(eng, lambda e: e.tensor_copy(out=out, in_=in_), deps)

        def red_(out, in_, alu, deps=(), absval=None, negate=None):
            return p.op(V, lambda e: e.tensor_reduce(
                out=out, in_=in_, axis=AX.X, op=alu,
                apply_absolute_value=absval, negate=negate), deps)

        def mm_(out, lhsT, rhs, start, stop, deps=()):
            return p.op(Te, lambda e: e.matmul(out=out, lhsT=lhsT, rhs=rhs,
                                               start=start, stop=stop), deps)

        def tr_(out, in_, deps=()):
            return p.op(Te, lambda e: e.transpose(out=out, in_=in_,
                                                  identity=idt[:]), deps)

        def scan_(out, d0, deps=()):
            return p.op(V, lambda e: e.tensor_tensor_scan(
                out=out, data0=d0, data1=d0, initial=0.0, op0=ALU.add,
                op1=ALU.bypass), deps)

        def ms_(eng, ap_, val, deps=()):
            return p.op(eng, lambda e: e.memset(ap_, val), deps)

        def recip_(out, in_, deps=()):
            return p.op(V, lambda e: e.reciprocal(out=out, in_=in_), deps)

        def dma(eng, out, in_, deps=(), transpose=False, flow="ld"):
            f = p.flow(flow)
            f[1] += 16
            return p.op(eng, lambda e: e.dma_start(out=out, in_=in_,
                                                   transpose=transpose), deps,
                        inc=16, inc_sem=(f[0], f[1]))

        def rtile(apx):
            return apx.rearrange("(t p) c -> t p c", p=128)

        def cc(v):
            if v not in CONSTS:
                i = len(CONSTS)
                ev = ms_(V, ccol[:, i:i + 1], float(v))
                CONSTS[v] = (ccol[:, i:i + 1], ev)
            return CONSTS[v][0]

        def ccev(v):
            cc(v)
            return CONSTS[v][1]

        def repl32(dst_flat, src_cols, nb, deps):
            d3 = dst_flat.rearrange("p (b e) -> p b e", e=32)
            ev = cp_(V, d3[:, :, 0:1],
                     src_cols.rearrange("p (b o) -> p b o", o=1), deps)
            k = 1
            while k < 32:
                kk = min(k, 32 - k)
                ev = cp_(V, d3[:, :, k:k + kk], d3[:, :, 0:kk], [ev])
                k += kk
            return ev

        def mx_scales(src_blocks, nb, iw, deps):
            amx = mxc[:, 0:nb]
            lg = mxc[:, 64:64 + nb]
            fg = mxc[:, 128:128 + nb]
            e1 = red_(amx, src_blocks, ALU.max, deps, absval=True)
            e2 = ts_(V, amx, amx, 1e-30, ALU.max, deps=[e1])
            e3 = act_(lg, amx, AF.Ln, deps=[e2])
            e4 = ts_(V, lg, lg, 1.0 / LN2, ALU.mult, 1e-9, ALU.add, deps=[e3])
            e5 = cp_(V, iw[:, 0:nb], lg, [e4])
            e6 = stt_(V, fg, iw[:, 0:nb], 0.0, lg, ALU.bypass, ALU.is_gt, [e5])
            e7 = tt_(V, lg, iw[:, 0:nb], fg, ALU.subtract, [e6])
            e8 = ts_(V, lg, lg, -2.0, ALU.add, deps=[e7])
            e9 = ts_(V, lg, lg, 127.0, ALU.min, -127.0, ALU.max, deps=[e8])
            er = act_(fg, lg, AF.Exp, scale=cc(-LN2), deps=[e9, ccev(-LN2)])
            ers = act_(lg, lg, AF.Exp, scale=cc(LN2), deps=[er, ccev(LN2)])
            return fg, lg, er, ers

        def mx_quant(in_f32, out_bf, W, base_inv, maxn, scr, deps,
                     out_free=(), in_more=()):
            nb = W // 32
            fg, lg, er, ers = mx_scales(
                in_f32.rearrange("p (b e) -> p b e", e=32), nb, scr["I"],
                list(deps))
            erp = repl32(scr["R"][:, 0:W], fg, nb, [er])
            ey = tt_(V, scr["A"][:, 0:W], in_f32, scr["R"][:, 0:W], ALU.mult,
                     [erp] + list(deps) + list(in_more))
            ersp = repl32(scr["R2"][:, 0:W], lg, nb, [ers])
            ea = act_(scr["S1"][:, 0:W], scr["A"][:, 0:W], AF.Abs, deps=[ey])
            es1 = ts_(V, scr["S2"][:, 0:W], scr["S1"][:, 0:W], 2.0, ALU.is_ge,
                      deps=[ea])
            es2 = ts_(V, scr["S1"][:, 0:W], scr["S1"][:, 0:W], 4.0, ALU.is_ge,
                      deps=[ea, es1])
            eu1 = ts_(V, out_bf[:, 0:W], scr["S2"][:, 0:W], -0.5, ALU.mult,
                      1.0, ALU.add, deps=[es1] + list(out_free))
            ev2 = tt_(V, scr["A"][:, 0:W], scr["A"][:, 0:W], out_bf[:, 0:W],
                      ALU.mult, [eu1])
            eu2 = ts_(V, out_bf[:, 0:W], scr["S1"][:, 0:W], -0.5, ALU.mult,
                      1.0, ALU.add, deps=[es2, ev2])
            ev3 = tt_(V, scr["A"][:, 0:W], scr["A"][:, 0:W], out_bf[:, 0:W],
                      ALU.mult, [eu2])
            ev4 = ts_(V, scr["A"][:, 0:W], scr["A"][:, 0:W], base_inv, ALU.mult,
                      0.5, ALU.add, deps=[ev3])
            eI = cp_(V, scr["I"][:, 0:W], scr["A"][:, 0:W], [ev4])
            eG = stt_(V, scr["R"][:, 0:W], scr["I"][:, 0:W], 0.0,
                      scr["A"][:, 0:W], ALU.bypass, ALU.is_gt, [eI])
            eK = tt_(V, out_bf[:, 0:W], scr["I"][:, 0:W], scr["R"][:, 0:W],
                     ALU.subtract, [eG])
            ew1 = ts_(V, scr["S2"][:, 0:W], scr["S2"][:, 0:W], 1.0, ALU.add,
                      deps=[eK])
            ew2 = stt_(V, scr["S1"][:, 0:W], scr["S1"][:, 0:W], 1.0,
                       scr["S2"][:, 0:W], ALU.add, ALU.mult, [ew1])
            ew3 = ts_(V, scr["S1"][:, 0:W], scr["S1"][:, 0:W], 1.0 / base_inv,
                      ALU.mult, deps=[ew2])
            eq = tt_(V, out_bf[:, 0:W], out_bf[:, 0:W], scr["S1"][:, 0:W],
                     ALU.mult, [ew3])
            ecl = ts_(V, out_bf[:, 0:W], out_bf[:, 0:W], maxn, ALU.min, -maxn,
                      ALU.max, deps=[eq])
            eo = tt_(V, out_bf[:, 0:W], out_bf[:, 0:W], scr["R2"][:, 0:W],
                     ALU.mult, [ecl, ersp])
            return eo, ey

        def mx_quant_gp(in_f32, out_bf, W, base_inv, maxn, scr, deps,
                        out_free=()):
            nb = W // 32
            fg, lg, er, ers = mx_scales(
                in_f32.rearrange("p (b e) -> p b e", e=32), nb, scr["I"],
                list(deps))
            erp = repl32g(scr["R"][:, 0:W], fg, nb, [er])
            ey = tt_(Gp, scr["A"][:, 0:W], in_f32, scr["R"][:, 0:W], ALU.mult,
                     [erp] + list(deps))
            ersp = repl32g(scr["R2"][:, 0:W], lg, nb, [ers])
            esq = tt_(Gp, scr["R"][:, 0:W], scr["A"][:, 0:W],
                      scr["A"][:, 0:W], ALU.mult, [ey])
            es1 = ts_(Gp, scr["S2"][:, 0:W], scr["R"][:, 0:W], 4.0, ALU.is_ge,
                      deps=[esq])
            es2 = ts_(Gp, scr["S1"][:, 0:W], scr["R"][:, 0:W], 16.0, ALU.is_ge,
                      deps=[esq, es1])
            eu1 = ts_(Gp, out_bf[:, 0:W], scr["S2"][:, 0:W], -0.5, ALU.mult,
                      1.0, ALU.add, deps=[es1] + list(out_free))
            ev2 = tt_(Gp, scr["A"][:, 0:W], scr["A"][:, 0:W], out_bf[:, 0:W],
                      ALU.mult, [eu1])
            eu2 = ts_(Gp, out_bf[:, 0:W], scr["S1"][:, 0:W], -0.5, ALU.mult,
                      1.0, ALU.add, deps=[es2, ev2])
            ev3 = tt_(Gp, scr["A"][:, 0:W], scr["A"][:, 0:W], out_bf[:, 0:W],
                      ALU.mult, [eu2])
            ev4 = ts_(Gp, scr["A"][:, 0:W], scr["A"][:, 0:W], base_inv,
                      ALU.mult, 0.5, ALU.add, deps=[ev3])
            eI = cp_(Gp, scr["I"][:, 0:W], scr["A"][:, 0:W], [ev4])
            eF = cp_(Gp, scr["R"][:, 0:W], scr["I"][:, 0:W], [eI])
            eD = tt_(Gp, scr["A"][:, 0:W], scr["R"][:, 0:W], scr["A"][:, 0:W],
                     ALU.subtract, [eF])
            eG = ts_(Gp, scr["A"][:, 0:W], scr["A"][:, 0:W], 0.0, ALU.is_gt,
                     deps=[eD])
            eK = tt_(Gp, out_bf[:, 0:W], scr["R"][:, 0:W], scr["A"][:, 0:W],
                     ALU.subtract, [eG])
            ew1 = ts_(Gp, scr["S2"][:, 0:W], scr["S2"][:, 0:W], 1.0, ALU.add,
                      deps=[eK])
            ea1 = ts_(Gp, scr["S1"][:, 0:W], scr["S1"][:, 0:W], 1.0, ALU.add,
                      deps=[eK])
            ew2 = tt_(Gp, scr["S1"][:, 0:W], scr["S1"][:, 0:W],
                      scr["S2"][:, 0:W], ALU.mult, [ew1, ea1])
            ew3 = ts_(Gp, scr["S1"][:, 0:W], scr["S1"][:, 0:W], 1.0 / base_inv,
                      ALU.mult, deps=[ew2])
            eq = tt_(Gp, out_bf[:, 0:W], out_bf[:, 0:W], scr["S1"][:, 0:W],
                     ALU.mult, [ew3])
            ecl = ts_(Gp, out_bf[:, 0:W], out_bf[:, 0:W], maxn, ALU.min, -maxn,
                      ALU.max, deps=[eq])
            eo = tt_(Gp, out_bf[:, 0:W], out_bf[:, 0:W], scr["R2"][:, 0:W],
                     ALU.mult, [ecl, ersp])
            return eo, ey

        def repl32g(dst_flat, src_cols, nb, deps):
            d3 = dst_flat.rearrange("p (b e) -> p b e", e=32)
            ev = cp_(Gp, d3[:, :, 0:1],
                     src_cols.rearrange("p (b o) -> p b o", o=1), deps)
            k = 1
            while k < 32:
                kk = min(k, 32 - k)
                ev = cp_(Gp, d3[:, :, k:k + kk], d3[:, :, 0:kk], [ev])
                k += kk
            return ev

        ev_tri = dma(Sy, trit[:], tri_d, flow="cst")
        ev_id = dma(Sy, idt[:], ident_d, flow="cst")
        ev_fm = dma(Sy, fmsk[:], fmask_d, flow="cst")
        ev_fmi = dma(Sy, fmski[:], fmaski_d, flow="cst")
        ev_cst = ev_fmi
        ev_z = ms_(V, zcol[:], 0.0)
        evx = []
        x_t = rtile(x_in)
        for t in range(NT):
            evx.append(dma(Sy, xres[:, t * D:(t + 1) * D], x_t[t]))
        ev_xall = evx[-1]

        ph12 = ExitStack()
        xn_fm = sb(ph12, "xn_fm", [128, NF * T], BF16, is_transpose_dest=True)
        pool = ExitStack()
        pA = sb(pool, "pA", [128, D], F32)
        pB = sb(pool, "pB", [128, D], F32)
        pC = sb(pool, "pC", [128, D], F32)
        pI = sb(pool, "pI", [128, D], I32)
        pBF = sb(pool, "pBF", [128, D], BF16)
        pBF2 = sb(pool, "pBF2", [128, D], BF16)
        pR2 = sb(pool, "pR2", [128, D], BF16)
        pO = sb(pool, "pO", [128, D], BF16)
        SCR = {"A": pA, "R": pC, "I": pI, "S1": pBF, "S2": pBF2, "R2": pR2}

        ph1 = ExitStack()
        xq_fm1 = sb(ph1, "xq_fm1", [128, NF * T], BF16, is_transpose_dest=True)

        def ln_cols(src_tile, base, eps, deps, dump, ddep):
            c = lambda i: cols[:, base + i:base + i + 1]
            e1 = red_(c(0), src_tile, ALU.add, deps)
            e2 = act_(dump[:], src_tile, AF.Square, accum=c(1),
                      deps=list(deps) + [ddep])
            e3 = ts_(V, c(2), c(0), 1.0 / D, ALU.mult, deps=[e1])
            e4 = ts_(V, c(3), c(1), 1.0 / D, ALU.mult, deps=[e2])
            e5 = tt_(V, c(4), c(2), c(2), ALU.mult, [e3])
            e6 = tt_(V, c(4), c(3), c(4), ALU.subtract, [e4, e5])
            e7 = act_(c(5), c(4), AF.Sqrt, bias=cc(eps), deps=[e6, ccev(eps)])
            e8 = recip_(c(5), c(5), [e7])
            e9 = stt_(V, c(6), c(2), -1.0, c(5), ALU.mult, ALU.mult, [e8])
            return c(5), c(6), e9

        ev_xnfm = []
        ev_xqfm1 = []
        pv_saf = pv_g = pv_tq = pv_bf = None
        for t in range(NT):
            xt = xres[:, t * D:(t + 1) * D]
            base = t * 24
            rc, nbc, e9 = ln_cols(xt, base, 1e-5, [ev_xall], pBF2, pv_tq)
            e_xn = act_(pA[:], xt, AF.Identity, bias=nbc, scale=rc,
                        deps=[e9, ev_xall, pv_saf])
            e_bf = cp_(V, pBF[:], pA[:], [e_xn, pv_tq])
            e_tp = dma(Sy, xn_fm.rearrange("p (f w) -> p f w", f=NF)[
                           :, :, t * 128:(t + 1) * 128],
                       pBF.rearrange("p (f q) -> p f q", q=128), [e_bf],
                       transpose=True, flow="t1")
            ev_xnfm.append(e_tp)
            rc2, nbc2, e9b = ln_cols(pA[:], base + 8, 1e-8, [e_xn], pBF2, None)
            pv_saf = e9b
            e_xnr = act_(pB[:], pA[:], AF.Identity, bias=nbc2, scale=rc2,
                         deps=[e9b, e_xn, pv_g])
            cmax = cols[:, base + 16:base + 17]
            csc = cols[:, base + 17:base + 18]
            e_mx = red_(cmax, pB[:], ALU.max, [e_xnr], absval=True)
            e_r = recip_(csc, cmax, [e_mx])
            e_s = ts_(V, csc, csc, 127.0, ALU.mult, deps=[e_r])
            e_si = recip_(sinv_cols[:, t:t + 1], csc, [e_s])
            e_v = ts_(V, pB[:], pB[:], csc, ALU.mult, 0.5, ALU.add,
                      deps=[e_s, e_xnr])
            e_i = cp_(V, pI[:], pB[:], [e_v])
            e_g = stt_(V, pC[:], pI[:], 0.0, pB[:], ALU.bypass, ALU.is_gt,
                       [e_i])
            pv_g = e_g
            e_k = tt_(V, pBF[:], pI[:], pC[:], ALU.subtract, [e_g, e_tp])
            e_cl = ts_(V, pBF[:], pBF[:], 127.0, ALU.min, -128.0, ALU.max,
                       deps=[e_k])
            e_tq = dma(Sy, xq_fm1.rearrange("p (f w) -> p f w", f=NF)[
                           :, :, t * 128:(t + 1) * 128],
                       pBF.rearrange("p (f q) -> p f q", q=128), [e_cl],
                       transpose=True, flow="t2")
            pv_tq = e_tq
            ev_xqfm1.append(e_tq)
        e_spxq = dma(Sc, sc_xq, xq_fm1[:], ev_xqfm1, flow="sp1")
        ph1.close()

        if tap("xnfm", (128, NF * T), BF16) is not None:
            dma(Sc, tap_t["xnfm"], xn_fm[:], ev_xnfm, flow="tap")
        if tap("sinv", (128, NT), F32) is not None:
            dma(Sc, tap_t["sinv"], sinv_cols[:], [e_si], flow="tap")

        phw = ExitStack()
        pL = sb(phw, "pL", [128, D], F32)
        ph1_done = [e_spxq, ev_xnfm[-1], e_si]
        ev_qg, ev_qu, ev_qd = [], [], []
        pv_ey = None
        pv_store = None
        for (src, dst, evlist) in ((w1g_d, q1g_in, ev_qg),
                                   (w1u_d, q1u_in, ev_qu)):
            src_t = rtile(src)
            dst_t = rtile(dst)
            for t in range(HID // NCORES // 128):
                el = dma(Sy, pL[:], src_t[t], [pv_ey] + ph1_done, flow="wlA")
                eo, pv_ey = mx_quant(pL[:], pO, D, 2.0, 6.0, SCR,
                                     [el] + ph1_done, [pv_store])
                pv_store = dma(Sc, dst_t[t], pO[:], [eo], flow="st" + dst_t.tensor.name)
                evlist.append(pv_store)
        src_t = rtile(w1d_d)
        dst_t = rtile(q1d_in)
        for t in range(D // NCORES // 128):
            for cchunk in range(4):
                el = dma(Sy, pL[:], src_t[t][:, cchunk * D:(cchunk + 1) * D],
                         [pv_ey] + ph1_done, flow="wlA")
                eo, pv_ey = mx_quant(pL[:], pO, D, 2.0, 6.0, SCR,
                                     [el] + ph1_done, [pv_store])
                pv_store = dma(Sc, dst_t[t][:, cchunk * D:(cchunk + 1) * D],
                               pO[:], [eo], flow="stD")
            ev_qd.append(pv_store)
        cc_qg = p.collective("AllGather", ALU.bypass, [q1g_in.opt()],
                             [q1g_out.opt()], ev_qg)
        cc_qu = p.collective("AllGather", ALU.bypass, [q1u_in.opt()],
                             [q1u_out.opt()], ev_qu)
        cc_qd = p.collective("AllGather", ALU.bypass, [q1d_in.opt()],
                             [q1d_out.opt()], ev_qd)
        if tap("qwg", (HID, D), BF16) is not None:
            dma(Sc, tap_t["qwg"], q1g_out, [cc_qg], flow="tap")
        phw.close()
        pool.close()
        p.barrier(include_cc=False)

        ph2 = ExitStack()
        gkl = sb(ph2, "gkl", [128, T], F32)
        An = sb(ph2, "An", [128, T], F32)
        gt1 = sb(ph2, "gt1", [128, 512], F32)
        qh = sb(ph2, "qh", [128, T], BF16)
        vh = sb(ph2, "vh", [128, T], BF16)
        oh = sb(ph2, "oh", [128, T], BF16)
        ktil = sb(ph2, "ktil", [128, 128], BF16)
        khat = sb(ph2, "khat", [128, 128], BF16)
        kth = sb(ph2, "kth", [128, 128], BF16)
        eqt = sb(ph2, "eqt", [128, 128], BF16)
        ekt = sb(ph2, "ekt", [128, 128], BF16)
        pbar = sb(ph2, "pbar", [128, 128], BF16)
        sbf = sb(ph2, "sbf", [128, 128], BF16)
        ssb = sb(ph2, "ssb", [128, 2 * 129], F32)
        wsl = sb(ph2, "wsl", [128, NF * 128], F32)
        wslb = [sb(ph2, f"wslb{i}", [128, NF * 128], BF16) for i in range(2)]
        gcolA = sb(ph2, "gcolA", [128, 8], F32)
        gcolB = sb(ph2, "gcolB", [128, 8], F32)
        gcolD = sb(ph2, "gcolD", [128, 8], F32)
        PS = [ps_(ph2, f"ps{i}", [128, 512], F32) for i in range(7)]
        PT = ps_(ph2, "pst", [128, 128], BF16)

        ev_ccg_w = []
        prev_tail = None
        prev_cast = None
        prev_sq = prev_so = None
        prev_ccg = None
        prev_dc = [None] * 8
        prev_scan_sread = None
        pv_kth = None
        for h in range(H_LOOP):
            def wslice(wd, buf, extra=()):
                nonlocal prev_cast
                e = dma(Sy, wsl.rearrange("p (f q) -> p f q", q=128),
                        wd.rearrange("(f p) o -> p f o", p=128)[
                            :, :, h * 128:(h + 1) * 128],
                        [prev_cast], flow="wsl")
                e2 = cp_(V, buf[:], wsl[:], [e] + list(extra))
                prev_cast = e2
                return e2

            e_wg = wslice(wg_d, wslb[0], [prev_tail])
            ev_scan = [None, None]
            for half in range(2):
                for kc in range(NF):
                    e_mm = mm_(PS[half][:],
                               wslb[0][:, kc * 128:(kc + 1) * 128],
                               xn_fm[:, kc * T + half * 512:
                                     kc * T + (half + 1) * 512],
                               start=(kc == 0), stop=(kc == NF - 1),
                               deps=([e_wg, prev_tail] + ev_xnfm)
                               if kc == 0 else ())
                e_c = ts_(V, gt1[:], PS[half][:], -87.0, ALU.max,
                          deps=[e_mm, prev_scan_sread])
                e_e = act_(gt1[:], gt1[:], AF.Exp, scale=cc(-1.0),
                           deps=[e_c, ccev(-1.0)])
                e_u = ts_(V, gt1[:], gt1[:], 1.0, ALU.add, deps=[e_e])
                e_l = act_(gkl[:, half * 512:(half + 1) * 512], gt1[:], AF.Ln,
                           deps=[e_u])
                ev_scan[half] = scan_(An[:, half * 512:(half + 1) * 512],
                                      gkl[:, half * 512:(half + 1) * 512],
                                      [e_l])
            ev_bc = [None] * 8
            for c in range(8):
                aoff_src = zcol[:] if c % 4 == 0 else \
                    An[:, (c // 4) * 512 + (c % 4) * 128 - 1:
                       (c // 4) * 512 + (c % 4) * 128]
                e_b1 = ts_(V, gcolA[:, c:c + 1], aoff_src, GN16, ALU.mult,
                           deps=[ev_scan[c // 4], ev_z, prev_dc[c]])
                e_b2 = ts_(V, gcolB[:, c:c + 1], aoff_src, -GN16, ALU.mult,
                           deps=[ev_scan[c // 4], prev_dc[c]])
                ev_bc[c] = (e_b1, e_b2)
            ev_ac = []
            for c in range(8):
                dst = acols[:, h * 8 + c:h * 8 + c + 1]
                if c % 4 == 0:
                    ev_ac.append(cp_(V, dst, zcol[:], [ev_z]))
                else:
                    cs = (c // 4) * 512 + (c % 4) * 128
                    ev_ac.append(cp_(V, dst, An[:, cs - 1:cs],
                                     [ev_scan[c // 4]]))
            e_wq = wslice(wq_d, wslb[1], [prev_tail])
            for half in range(2):
                for kc in range(NF):
                    e_mmq = mm_(PS[2 + half][:],
                                wslb[1][:, kc * 128:(kc + 1) * 128],
                                xn_fm[:, kc * T + half * 512:
                                      kc * T + (half + 1) * 512],
                                start=(kc == 0), stop=(kc == NF - 1),
                                deps=[e_wq, prev_tail] if kc == 0 else ())
            ev_qmm = e_mmq
            e_wk = wslice(wk_d, wslb[0], [e_mm])
            for half in range(2):
                for kc in range(NF):
                    e_mmk = mm_(PS[4 + half][:],
                                wslb[0][:, kc * 128:(kc + 1) * 128],
                                xn_fm[:, kc * T + half * 512:
                                      kc * T + (half + 1) * 512],
                                start=(kc == 0), stop=(kc == NF - 1),
                                deps=[e_wk, prev_tail] if kc == 0 else ())
            ev_kmm = e_mmk
            e_wv = wslice(wv_d, wslb[1], [ev_qmm])
            ev_vev = [None] * NT
            for tt in range(NT):
                sl6 = PS[6][:, (tt % 4) * 128:(tt % 4) * 128 + 128]
                dep0 = [e_wv, prev_tail]
                if tt > 0:
                    dep0.append(ev_vev[tt - 1])
                for kc in range(NF):
                    e_mmv = mm_(sl6,
                                xn_fm[:, kc * T + tt * 128:
                                      kc * T + (tt + 1) * 128],
                                wslb[1][:, kc * 128:(kc + 1) * 128],
                                start=(kc == 0), stop=(kc == NF - 1),
                                deps=dep0 if kc == 0 else ())
                ev_vev[tt] = cp_(V, vh[:, tt * 128:(tt + 1) * 128], sl6,
                                 [e_mmv])
            e_z0 = ms_(V, ssb[:], 0.0, [prev_ccg])
            ev_oev = [None] * 8
            e_qt_last = None
            pv_P = pv_tr = pv_D = pv_o1 = pv_oi = None
            pv_qt = pv_kt = None
            e_su = None
            e_dcr = None
            for c in range(8):
                b = c // 4
                cs = c * 128
                first_of_b = (c % 4 == 0)
                e_b1, e_b2 = ev_bc[c]
                e_eq = act_(eqt[:], An[:, cs:cs + 128], AF.Exp, scale=cc(-GN16),
                            bias=gcolA[:, c:c + 1],
                            deps=[e_b1, ev_scan[b], ccev(-GN16), pv_qt])
                e_ek = act_(ekt[:], An[:, cs:cs + 128], AF.Exp, scale=cc(GN16),
                            bias=gcolB[:, c:c + 1],
                            deps=[e_b2, ev_scan[b], ccev(GN16), pv_kt])
                qps = PS[2 + b][:, (c % 4) * 128:(c % 4) * 128 + 128]
                e_qt = stt_(V, qh[:, cs:cs + 128], eqt[:], SCL, qps,
                            ALU.mult, ALU.mult,
                            [e_eq, ev_qmm, prev_sq if c == 0 else None])
                e_qt_last = e_qt
                pv_qt = e_qt
                kps = PS[4 + b][:, (c % 4) * 128:(c % 4) * 128 + 128]
                e_kt = tt_(V, ktil[:], kps, ekt[:], ALU.mult,
                           [e_ek, ev_kmm, pv_P])
                pv_kt = e_kt
                e_dc = act_(gcolD[:, c:c + 1], An[:, cs + 127:cs + 128],
                            AF.Exp, scale=cc(-GN16), bias=gcolA[:, c:c + 1],
                            deps=[e_b1, ev_scan[b], prev_dc[c]])
                prev_dc[c] = e_dc
                e_kh = ts_(V, khat[:], ktil[:], gcolD[:, c:c + 1], ALU.mult,
                           deps=[e_kt, e_dc, pv_tr])
                e_tr = tr_(PT[:], khat[:],
                           [e_kh, ev_cst] + ([pv_kth] if pv_kth else []))
                pv_tr = e_tr
                e_kth = cp_(V, kth[:], PT[:], [e_tr, pv_D])
                pv_kth = e_kth
                e_P = mm_(PS[0][:, 0:128], ktil[:], qh[:, cs:cs + 128],
                          start=True, stop=True,
                          deps=[e_qt, e_kt,
                                ev_oev[c - 1] if c else prev_tail])
                pv_P = e_P
                e_pb = tt_(V, pbar[:], PS[0][:, 0:128], trit[:], ALU.mult,
                           [e_P, ev_cst, pv_o1])
                e_sb = cp_(V, sbf[:], ssb[:, b * 129:b * 129 + 128],
                           [e_z0, pv_oi, e_su])
                ops = PS[1][:, (c % 4) * 128:(c % 4) * 128 + 128]
                e_o1 = mm_(ops, vh[:, cs:cs + 128], pbar[:],
                           start=True, stop=first_of_b,
                           deps=[e_pb, ev_vev[c],
                                 ev_oev[c - 1] if c else prev_tail])
                pv_o1 = e_o1
                if not first_of_b:
                    e_o1 = mm_(ops, sbf[:], qh[:, cs:cs + 128],
                               start=False, stop=True, deps=[e_sb, e_qt])
                    pv_oi = e_o1
                e_oe = cp_(V, oh[:, cs:cs + 128], ops,
                           [e_o1, prev_so if c == 0 else None])
                ev_oev[c] = e_oe
                e_D = mm_(PS[6][:, 128:256], kth[:], vh[:, cs:cs + 128],
                          start=True, stop=True,
                          deps=[e_kth, ev_vev[c],
                                e_su if c else ev_vev[NT - 1]])
                pv_D = e_D
                e_su = stt_(V, ssb[:, b * 129:b * 129 + 128],
                            ssb[:, b * 129:b * 129 + 128],
                            gcolD[:, c:c + 1], PS[6][:, 128:256],
                            ALU.mult, ALU.add, [e_D, e_sb, e_dc])
                if c % 4 == 3:
                    e_dcr = act_(ssb[:, b * 129 + 128:b * 129 + 129],
                                 An[:, b * 512 + 511:b * 512 + 512], AF.Exp,
                                 scale=cc(-GN16), deps=[ev_scan[b], e_su])
            prev_scan_sread = e_dcr
            prev_sq = dma(Sc, sc_q[h], qh[:], [e_qt_last], flow="sq")
            prev_so = dma(Sc, sc_o[h], oh[:], ev_oev, flow="so")
            e_c0 = dma(Sc, ccg_in[h], ssb[:, 0:129], [e_su, e_dcr], flow="ccw")
            e_c1 = dma(Sc, ccg_in[H + h], ssb[:, 129:258], [e_su, e_dcr], flow="ccw")
            prev_ccg = e_c1
            ev_ccg_w += [prev_sq, prev_so, e_c0, e_c1]
            prev_tail = ev_oev[7]

        if "qh2" in taps:
            tap("qh2", (128, T), BF16)
            dma(Sc, tap_t["qh2"], qh[:], [prev_sq], flow="tap")
        if "vh2" in taps:
            tap("vh2", (128, T), BF16)
            dma(Sc, tap_t["vh2"], vh[:], ev_vev, flow="tap")
        if "oh2" in taps:
            tap("oh2", (128, T), BF16)
            dma(Sc, tap_t["oh2"], oh[:], ev_oev, flow="tap")
        if "An2" in taps:
            tap("An2", (128, T), F32)
            dma(Sc, tap_t["An2"], An[:], [prev_scan_sread, ev_scan[0], ev_scan[1]], flow="tap")
        if "ssb2" in taps:
            tap("ssb2", (128, 258), F32)
            dma(Sc, tap_t["ssb2"], ssb[:], [e_su, e_dcr], flow="tap")

        cc_g = p.collective("AllGather", ALU.bypass, [ccg_in.opt()],
                            [ccg_out.opt()], ev_ccg_w)
        ph2.close()
        ph12.close()
        p.barrier()

        ph3 = ExitStack()
        sfold = sb(ph3, "sfold", [128, 2 * H * 128], F32)
        o_all = sb(ph3, "o_all", [128, H * T], BF16)
        qh3 = sb(ph3, "qh3", [128, T], BF16)
        oh3 = sb(ph3, "oh3", [128, T], BF16)
        sbf3 = sb(ph3, "sbf3", [128, 128], BF16)
        PS3 = [ps_(ph3, f"ps3_{i}", [128, 512], F32) for i in range(4)]

        ph3a = ExitStack()
        tjl = sb(ph3a, "tjl", [128, 2 * H * 129], F32)
        e_fold = ms_(V, sfold[:], 0.0)
        fold_q = [e_fold] * (2 * H)
        for j in range(NCORES):
            e_lj = dma(Sy, tjl.rearrange("p (q c) -> p q c", c=129),
                       ccg_out[j].rearrange("q p c -> p q c"),
                       [cc_g, e_fold], flow="tj")
            e_m1 = ts_(V, tjl[:], tjl[:], fmsk[:, j:j + 1], ALU.mult,
                       deps=[e_lj, ev_cst])
            e_m2 = ts_(V, tjl.rearrange("p (q c) -> p q c", c=129)[
                           :, :, 128:129],
                       tjl.rearrange("p (q c) -> p q c", c=129)[
                           :, :, 128:129],
                       fmski[:, j:j + 1], ALU.add, deps=[e_m1, ev_cst])
            for q in range(2 * H):
                e_fold = stt_(V, sfold[:, q * 128:(q + 1) * 128],
                              sfold[:, q * 128:(q + 1) * 128],
                              tjl[:, q * 129 + 128:q * 129 + 129],
                              tjl[:, q * 129:q * 129 + 128],
                              ALU.mult, ALU.add, [e_m2, fold_q[q]])
                fold_q[q] = e_fold
        ph3a.close()

        ev_oall = []
        if H_LOOP < H:
            ev_oall.append(ms_(V, o_all[:], 0.0))
        pv_cor = None
        for h in range(H_LOOP):
            e_lq = dma(Sy, qh3[:], sc_q[h],
                       [e_fold, ev_oall[-1] if ev_oall else None], flow="lq")
            e_lo = dma(Sy, oh3[:], sc_o[h],
                       [e_fold, ev_oall[-1] if ev_oall else None], flow="lo")
            for c in range(8):
                b = c // 4
                qp = b * H + h
                e_bk = act_(cols[:, 200:201],
                            acols[:, h * 8 + c:h * 8 + c + 1],
                            AF.Exp, scale=cc(-GN16),
                            deps=[e_lq, ev_oall[-1] if ev_oall else None])
                e_sk = ts_(V, sbf3[:], sfold[:, qp * 128:(qp + 1) * 128],
                           cols[:, 200:201], ALU.mult,
                           deps=[e_bk, e_fold, pv_cor])
                e_cor = mm_(PS3[0][:, (c % 4) * 128:(c % 4) * 128 + 128],
                            sbf3[:], qh3[:, c * 128:(c + 1) * 128],
                            start=True, stop=True,
                            deps=[e_sk, e_lq,
                                  ev_oall[-1] if ev_oall else None])
                pv_cor = e_cor
                e_up = tt_(V, o_all[:, (h * 8 + c) * 128:
                                    (h * 8 + c + 1) * 128],
                           oh3[:, c * 128:(c + 1) * 128],
                           PS3[0][:, (c % 4) * 128:(c % 4) * 128 + 128],
                           ALU.add, [e_cor, e_lo])
                ev_oall.append(e_up)
        if tap("o", (128, H * T), BF16) is not None:
            dma(Sc, tap_t["o"], o_all[:], ev_oall, flow="tap")

        ph3b = ExitStack()
        wof = sb(ph3b, "wof", [128, NF * 512], F32)
        wofb = sb(ph3b, "wofb", [128, NF * 512], BF16)
        ev_wo = []
        pv_cw = None
        for oc in range(4):
            e_lw = dma(Sy, wof.rearrange("p (f q) -> p f q", q=512),
                       wo_d.rearrange("(f p) o -> p f o", p=128)[
                           :, :, oc * 512:(oc + 1) * 512],
                       [pv_cw, e_fold], flow="wo")
            e_cw = cp_(V, wofb[:], wof[:],
                       [e_lw, ev_wo[-1] if ev_wo else None])
            pv_cw = e_cw
            for tt in range(NT):
                bank = PS3[1 + tt % 2]
                dep0 = [e_cw] + ev_oall + \
                    ([ev_wo[-2]] if len(ev_wo) >= 2 else [])
                for hh in range(H):
                    e_mm = mm_(bank[:],
                               o_all[:, (hh * 8 + tt) * 128:
                                     (hh * 8 + tt + 1) * 128],
                               wofb[:, hh * 512:(hh + 1) * 512],
                               start=(hh == 0), stop=(hh == H - 1),
                               deps=dep0 if hh == 0 else ())
                e_ad = tt_(V, xres[:, tt * D + oc * 512:
                                   tt * D + (oc + 1) * 512],
                           xres[:, tt * D + oc * 512:
                                tt * D + (oc + 1) * 512],
                           bank[:], ALU.add, [e_mm, xres_last.get((tt, oc))])
                xres_last[(tt, oc)] = e_ad
                ev_wo.append(e_ad)
        ph3b.close()
        ph3.close()
        p.barrier()

        def ffn_pass(layer, xq_b, ev_act_ready, stk):
            L = f"l{layer}"
            if layer == 0:
                wld = sb(stk, L + "wld", [128, 1024], F32)
                wldd = sb(stk, L + "wldd", [128, 512], F32)
            else:
                wld = wldd = None
            wld_state = {}
            if layer == 0:
                wsg_g = sb(stk, L + "wsg_gu", [128, 1024], BF16)
                wsg_u = wsg_g
                wsg_d = sb(stk, L + "wsg_d", [128, 512], BF16)
            else:
                wsg_g = wsg_u = wsg_d = None
            wfm_g = sb(stk, L + "wfm_g", [128, NF * 512], BF16,
                       is_transpose_dest=True)
            wfm_u = sb(stk, L + "wfm_u", [128, NF * 512], BF16,
                       is_transpose_dest=True)
            wdfm = [sb(stk, L + f"wdfm{i}", [128, 4 * D], BF16,
                       is_transpose_dest=True) for i in range(2)]
            gsl = sb(stk, L + "gsl", [128, NT * 512], BF16)
            usl = sb(stk, L + "usl", [128, NT * 512], BF16)
            pfm = [sb(stk, L + f"pfm{i}", [128, 4 * T], BF16,
                      is_transpose_dest=True) for i in range(2)]
            PS5 = [ps_(stk, L + f"ps5_{i}", [128, 512], F32) for i in range(8)]
            if layer == 1:
                iq = {
                    "A": sb(stk, "iqA", [128, 512], F32),
                    "R": sb(stk, "iqR", [128, 512], F32),
                    "I": sb(stk, "iqI", [128, 512], I32),
                    "S1": sb(stk, "iqS1", [128, 512], BF16),
                    "S2": sb(stk, "iqS2", [128, 512], BF16),
                    "R2": sb(stk, "iqR2", [128, 512], BF16),
                }

            NHC_ = HC_LOOP

            def prep(stage, hc, wdram, quant_src, wld_ap, wsg_ap, dst_fm,
                     ntile, wchunk, state, fsuf=""):
                evs = []
                halves = 2 if (quant_src is None and wchunk == D) else 1
                wc = wchunk // halves
                for jj in range(ntile * halves):
                    j, hh = jj // halves, jj % halves
                    if wchunk == D:
                        srcap = rtile(wdram if quant_src is None else
                                      quant_src)[hc * ntile + j]
                        if halves == 2:
                            srcap = srcap[:, hh * 1024:(hh + 1) * 1024]
                    else:
                        srcap = rtile(wdram if quant_src is None else
                                      quant_src)[j][:, hc * 512:(hc + 1) * 512]
                    if quant_src is None:
                        lstate = wld_state if wld_ap is not wldd else state
                        e1 = dma(Sy, wld_ap[:, 0:wc], srcap,
                                 [lstate.get("sign")], flow=L + stage + "w")
                        e2 = act_(wsg_ap[:, 0:wc], wld_ap[:, 0:wc],
                                  AF.Sign,
                                  deps=[e1, evs[-1] if evs else
                                        state.get("tr"),
                                        lstate.get("trr")])
                        state["sign"] = e2
                        lstate["sign"] = e2
                    else:
                        nblk = wc // 128
                        fv = dst_fm.rearrange("p (f w) -> p f w",
                                              f=wchunk // 128)
                        e3 = dma(Sy, fv[:, hh * nblk:(hh + 1) * nblk,
                                        j * 128:(j + 1) * 128],
                                 srcap.rearrange("p (f q) -> p f q", q=128),
                                 [state.get("use")] if jj == 0 else (),
                                 transpose=True, flow=L + stage + "t" + fsuf)
                        evs.append(e3)
                        continue
                    nblk = wc // 128
                    fv = dst_fm.rearrange("p (f w) -> p f w", f=wchunk // 128)
                    e3 = dma(Sc, fv[:, hh * nblk:(hh + 1) * nblk,
                                    j * 128:(j + 1) * 128],
                             wsg_ap[:, 0:wc].rearrange(
                                 "p (f q) -> p f q", q=128),
                             [e2] + ([state.get("use")] if jj == 0 else []),
                             transpose=True, flow=L + stage + "t" + fsuf)
                    evs.append(e3)
                    if wld_ap is not wldd:
                        lstate["trr"] = e3
                state["tr"] = evs[-1]
                return evs

            def down_block(dn_evd, dn_evp, buf, dn_w):
                for tt in range(NT):
                    for oc in range(4):
                        bank = PS5[4 + (tt * 4 + oc) % 4]
                        dep0 = dn_evd + dn_evp +                             ([ev_dn[-4]] if len(ev_dn) >= 4 else [])
                        for ks in range(4):
                            e_mm = mm_(bank[:],
                                       buf[:, ks * T + tt * 128:
                                           ks * T + (tt + 1) * 128],
                                       dn_w[:, ks * D + oc * 512:
                                            ks * D + (oc + 1) * 512],
                                       start=(ks == 0), stop=(ks == 3),
                                       deps=dep0 if ks == 0 else ())
                        e_ad = tt_(V, xres[:, tt * D + oc * 512:
                                           tt * D + (oc + 1) * 512],
                                   xres[:, tt * D + oc * 512:
                                        tt * D + (oc + 1) * 512],
                                   bank[:], ALU.add,
                                   [e_mm, xres_last.get((tt, oc))])
                        xres_last[(tt, oc)] = e_ad
                        ev_dn.append(e_ad)

            st_g = {}
            st_u = {}
            st_d = {}
            ev_gev = [None] * NT
            ev_uev = [None] * NT
            ev_dn = []
            pend_down = None
            prev_ptr = None

            for hc in range(NHC_):
                st_g["use"] = ev_gev[-1]
                evg = prep("g", hc, wgf_d if layer == 0 else None,
                           None if layer == 0 else q1g_out,
                           wld if layer == 0 else None,
                           wsg_g, wfm_g, 4, D, st_g)
                st_u["use"] = ev_uev[-1]
                evu = prep("u", hc, wuf_d if layer == 0 else None,
                           None if layer == 0 else q1u_out,
                           wld if layer == 0 else None,
                           wsg_u, wfm_u, 4, D, st_u)
                for tt in range(NT):
                    bank = PS5[tt % 2]
                    dep0 = evg + list(ev_act_ready) +                         [ev_gev[tt], ev_gev[tt - 2] if tt >= 2 else None]
                    for kc in range(NF):
                        e_mm = mm_(bank[:],
                                   xq_b[:, kc * T + tt * 128:
                                        kc * T + (tt + 1) * 128],
                                   wfm_g[:, kc * 512:(kc + 1) * 512],
                                   start=(kc == 0), stop=(kc == NF - 1),
                                   deps=dep0 if kc == 0 else ())
                    if layer == 0:
                        e_ev = act_(gsl[:, tt * 512:(tt + 1) * 512], bank[:],
                                    AF.Silu, scale=sinv_cols[:, tt:tt + 1],
                                    deps=[e_mm, prev_ptr])
                    else:
                        e_ev = act_(gsl[:, tt * 512:(tt + 1) * 512], bank[:],
                                    AF.Silu, deps=[e_mm, prev_ptr])
                    ev_gev[tt] = e_ev
                ev_p = [None] * NT
                pbuf = pfm[hc % 2]
                for tt in range(NT):
                    bank = PS5[2 + tt % 2]
                    dep0 = evu + [ev_uev[tt],
                                  ev_uev[tt - 2] if tt >= 2 else None]
                    for kc in range(NF):
                        e_mm = mm_(bank[:],
                                   xq_b[:, kc * T + tt * 128:
                                        kc * T + (tt + 1) * 128],
                                   wfm_u[:, kc * 512:(kc + 1) * 512],
                                   start=(kc == 0), stop=(kc == NF - 1),
                                   deps=dep0 if kc == 0 else ())
                    if layer == 0:
                        e_ev = act_(usl[:, tt * 512:(tt + 1) * 512], bank[:],
                                    AF.Identity,
                                    scale=sinv_cols[:, tt:tt + 1],
                                    deps=[e_mm])
                    else:
                        e_ev = act_(usl[:, tt * 512:(tt + 1) * 512], bank[:],
                                    AF.Identity, deps=[e_mm])
                    ev_uev[tt] = e_ev
                    sl = gsl[:, tt * 512:(tt + 1) * 512]
                    e_p = tt_(V, sl, sl, usl[:, tt * 512:(tt + 1) * 512],
                              ALU.mult, [ev_uev[tt], ev_gev[tt]])
                    if layer == 1:
                        e_p, _ = mx_quant(sl, sl, 512, 8.0, 7.5, iq, [e_p])
                    e_tp = dma(Sc, pbuf.rearrange("p (hs w) -> p hs w", hs=4)[
                                   :, :, tt * 128:(tt + 1) * 128],
                               sl.rearrange("p (hs q) -> p hs q", q=128),
                               [e_p, ev_dn[-1] if ev_dn else None],
                               transpose=True, flow=L + "pt" + str(hc % 2))
                    ev_p[tt] = e_tp
                st_d["use"] = ev_dn[-1] if ev_dn else None
                evd = prep("d", hc, wdf_d if layer == 0 else None,
                           None if layer == 0 else q1d_out,
                           wldd, wsg_d, wdfm[hc % 2], NF, 512, st_d,
                           fsuf=str(hc % 2))
                if pend_down is not None:
                    dn_evd, dn_evp, dn_buf, dn_w = pend_down
                    down_block(dn_evd, dn_evp, dn_buf, dn_w)
                pend_down = (evd, ev_p, pbuf, wdfm[hc % 2])
                prev_ptr = ev_p[-1]
            dn_evd, dn_evp, dn_buf, dn_w = pend_down
            down_block(dn_evd, dn_evp, dn_buf, dn_w)
            if any((L + nm) in taps for nm in
                   ("dwdfm0", "dwdfm1", "dpfm0", "dpfm1", "dgsl", "dusl")):
                p.barrier()
                for nm, buf, w in (("dwdfm0", wdfm[0], 4 * D),
                                   ("dwdfm1", wdfm[1], 4 * D),
                                   ("dpfm0", pfm[0], 4 * T),
                                   ("dpfm1", pfm[1], 4 * T),
                                   ("dgsl", gsl, NT * 512),
                                   ("dusl", usl, NT * 512)):
                    if tap(L + nm, (128, w), BF16) is not None:
                        dma(Sc, tap_t[L + nm], buf[:], flow="tap")
                p.barrier()

        ph5 = ExitStack()
        xq_fm5 = sb(ph5, "xq_fm5", [128, NF * T], BF16)
        e_rld = dma(Sy, xq_fm5[:], sc_xq, flow="r5")
        ffn_pass(0, xq_fm5, [e_rld], ph5)
        ph5.close()
        p.barrier()
        if tap("x1", (T, D), F32) is not None:
            for t in range(NT):
                dma(Sc, rtile(tap_t["x1"])[t], xres[:, t * D:(t + 1) * D], flow="tap")
            p.barrier()

        ph67 = ExitStack()
        xq_fm6 = sb(ph67, "xq_fm6", [128, NF * T], BF16,
                    is_transpose_dest=True)
        ph6 = ExitStack()
        xn_fm6 = sb(ph6, "xn_fm6", [128, NF * T], BF16, is_transpose_dest=True)
        s6 = {
            "A": sb(ph6, "s6A", [128, D], F32),
            "R": sb(ph6, "s6R", [128, D], F32),
            "I": sb(ph6, "s6I", [128, D], I32),
            "S1": sb(ph6, "s6S1", [128, D], BF16),
            "S2": sb(ph6, "s6S2", [128, D], BF16),
            "R2": sb(ph6, "s6R2", [128, D], BF16),
        }
        s6X = sb(ph6, "s6X", [128, D], F32)
        s6BF = sb(ph6, "s6BF", [128, D], BF16)
        s6Q = sb(ph6, "s6Q", [128, D], BF16)

        ev_xnfm2 = []
        ev_xqfm2 = []
        pv_eyq = pv_tq6 = pv_tp6 = None
        for t in range(NT):
            xt = xres[:, t * D:(t + 1) * D]
            base = t * 24
            rc, nbc, e9 = ln_cols(xt, base, 1e-5, (), s6BF, pv_tp6)
            e_xn = act_(s6X[:], xt, AF.Identity, bias=nbc, scale=rc,
                        deps=[e9, pv_eyq])
            e_bf = cp_(V, s6BF[:], s6X[:], [e_xn, pv_tp6])
            e_tp = dma(Sy, xn_fm6.rearrange("p (f w) -> p f w", f=NF)[
                           :, :, t * 128:(t + 1) * 128],
                       s6BF.rearrange("p (f q) -> p f q", q=128), [e_bf],
                       transpose=True, flow="t3")
            pv_tp6 = e_tp
            ev_xnfm2.append(e_tp)
            eo, ey = mx_quant(s6X[:], s6Q, D, 8.0, 7.5, s6, [e_xn],
                              out_free=[pv_tq6])
            pv_eyq = ey
            e_tq = dma(Sy, xq_fm6.rearrange("p (f w) -> p f w", f=NF)[
                           :, :, t * 128:(t + 1) * 128],
                       s6Q.rearrange("p (f q) -> p f q", q=128), [eo],
                       transpose=True, flow="t4")
            pv_tq6 = e_tq
            ev_xqfm2.append(e_tq)
        if tap("qx1", (128, NF * T), BF16) is not None:
            dma(Sc, tap_t["qx1"], xq_fm6[:], ev_xqfm2, flow="tap")
        e_spxn = dma(Sc, sc_xn, xn_fm6[:], ev_xnfm2, flow="sp2")
        ph6.close()
        p.barrier()

        ph7 = ExitStack()
        ffn_pass(1, xq_fm6, ev_xqfm2, ph7)
        ph7.close()
        ph67.close()
        p.barrier()

        ph8 = ExitStack()
        qkvt = sb(ph8, "qkvt", [128, NT * 3072], BF16)
        a8 = sb(ph8, "a8", [128, 1024], F32)
        c8 = sb(ph8, "c8", [128, 64], F32)
        wl8 = sb(ph8, "wl8", [128, 2 * D], F32)
        wb8 = sb(ph8, "wb8", [128, D], BF16)
        wfm8 = sb(ph8, "wfm8", [128, NF * 512], BF16, is_transpose_dest=True)
        afm8 = sb(ph8, "afm8", [128, NF * 128], BF16, is_transpose_dest=True)
        PS8 = [ps_(ph8, f"ps8_{i}", [128, 512], F32) for i in range(4)]

        ph8a = ExitStack()
        xn_fm8 = sb(ph8a, "xn_fm8", [128, NF * T], BF16)
        ev_xn8 = [dma(Sy, xn_fm8[:], sc_xn, flow="r8")]

        ev_qkv = []
        pv_cast8 = None
        pv_w8 = []
        for oc in range(6):
            evw = []
            for j in range(4):
                buf = wl8[:, (j % 2) * D:(j % 2) * D + D]
                e1 = dma(Sy, buf, rtile(wqkv_d)[oc * 4 + j],
                         [evw[-2] if len(evw) >= 2 else pv_cast8],
                         flow="w8" + str(j % 2))
                e2 = cp_(V, wb8[:], buf, [e1, evw[-1] if evw else pv_cast8])
                pv_cast8 = e2
                e3 = dma(Sy, wfm8.rearrange("p (f q) -> p f q", f=NF)[
                             :, :, j * 128:(j + 1) * 128],
                         wb8.rearrange("p (f q) -> p f q", q=128),
                         [e2] + (pv_w8 if j == 0 else []), transpose=True,
                         flow="w8t")
                evw.append(e3)
            pv_w8 = []
            for tt in range(NT):
                bank = PS8[tt % 2]
                dep0 = evw + ev_xn8 + \
                    ([ev_qkv[-2]] if len(ev_qkv) >= 2 else [])
                for kc in range(NF):
                    e_mm = mm_(bank[:],
                               xn_fm8[:, kc * T + tt * 128:
                                      kc * T + (tt + 1) * 128],
                               wfm8[:, kc * 512:(kc + 1) * 512],
                               start=(kc == 0), stop=(kc == NF - 1),
                               deps=dep0 if kc == 0 else ())
                e_ev = cp_(V, qkvt[:, tt * 3072 + oc * 512:
                                   tt * 3072 + (oc + 1) * 512],
                           bank[:], [e_mm])
                ev_qkv.append(e_ev)
                pv_w8.append(e_ev)
        ph8a.close()
        if tap("qkv", (T, 3072), BF16) is not None:
            for t in range(NT):
                dma(Sc, rtile(tap_t["qkv"])[t],
                    qkvt[:, t * 3072:(t + 1) * 3072], ev_qkv, flow="tap")

        ev_att2 = []
        pv_gp = None
        for tt in range(4, NT):
            src = qkvt[:, tt * 3072:(tt + 1) * 3072]
            s0 = qkvt[:, (tt - 4) * 3072:(tt - 3) * 3072]
            q1v3 = src.rearrange("p (g c) -> p g c", g=4)[:, :, 0:512] \
                .rearrange("p g (i q) -> p g i q", q=128)
            k0v = s0.rearrange("p (g c) -> p g c", g=4)[:, :, 512:640]
            k1v = src.rearrange("p (g c) -> p g c", g=4)[:, :, 512:640]
            for gp2 in range(2):
                e_m0 = p.op(Gp, lambda e, qv=q1v3[:, gp2 * 2:gp2 * 2 + 2],
                            kk=k0v[:, gp2 * 2:gp2 * 2 + 2],
                            o=a8.rearrange("p (g i q) -> p g i q", g=2, q=128):
                            e.tensor_tensor(
                                out=o, in0=qv,
                                in1=kk.unsqueeze(2).broadcast_to(
                                    (128, 2, 4, 128)),
                                op=ALU.mult), ev_qkv + [pv_gp])
                e_r0 = red_(c8[:, 16 + gp2 * 8:24 + gp2 * 8],
                            a8.rearrange("p (x q) -> p x q", q=128),
                            ALU.add, [e_m0])
                pv_gp = e_r0
            for gp2 in range(2):
                e_m1 = p.op(Gp, lambda e, qv=q1v3[:, gp2 * 2:gp2 * 2 + 2],
                            kk=k1v[:, gp2 * 2:gp2 * 2 + 2],
                            o=a8.rearrange("p (g i q) -> p g i q", g=2, q=128):
                            e.tensor_tensor(
                                out=o, in0=qv,
                                in1=kk.unsqueeze(2).broadcast_to(
                                    (128, 2, 4, 128)),
                                op=ALU.mult), [pv_gp])
                e_r1 = red_(c8[:, 32 + gp2 * 8:40 + gp2 * 8],
                            a8.rearrange("p (x q) -> p x q", q=128),
                            ALU.add, [e_m1])
                pv_gp = e_r1
            e_d = tt_(V, c8[:, 48:64], c8[:, 32:48], c8[:, 16:32],
                      ALU.subtract, [pv_gp])
            e_e = act_(c8[:, 48:64], c8[:, 48:64], AF.Exp, scale=cc(-SCL),
                       deps=[e_d, ccev(-SCL)])
            e_a = ts_(V, c8[:, 48:64], c8[:, 48:64], 1.0, ALU.add, deps=[e_e])
            e_p1 = recip_(c8[:, 48:64], c8[:, 48:64], [e_a])
            for g in range(4):
                e_t = tt_(V, a8[:, g * 128:(g + 1) * 128],
                          src[:, g * 768 + 640:g * 768 + 768],
                          s0[:, g * 768 + 640:g * 768 + 768], ALU.subtract,
                          [e_p1])
                for qi in range(4):
                    h_ = g * 4 + qi
                    dstc = src[:, g * 768 + qi * 128:g * 768 + qi * 128 + 128]
                    e_sc = ts_(V, a8[:, 512 + h_ % 4 * 128:
                                     512 + (h_ % 4 + 1) * 128],
                               a8[:, g * 128:(g + 1) * 128],
                               c8[:, 48 + h_:49 + h_], ALU.mult, deps=[e_t])
                    e_o = tt_(V, dstc,
                              a8[:, 512 + h_ % 4 * 128:
                                 512 + (h_ % 4 + 1) * 128],
                              s0[:, g * 768 + 640:g * 768 + 768], ALU.add,
                              [e_sc])
                    ev_att2.append(e_o)
            pv_gp = e_o
        ev_att0 = []
        for tt in range(4):
            src = qkvt[:, tt * 3072:(tt + 1) * 3072]
            for g in range(4):
                for qi in range(4):
                    hcol = g * 768 + qi * 128
                    ev_att0.append(cp_(V, src[:, hcol:hcol + 128],
                                       src[:, g * 768 + 640:g * 768 + 768],
                                       ev_qkv + [pv_gp]))
        ev_att = ev_att2 + ev_att0
        if tap("attn1", (T, D), BF16) is not None:
            at = tap_t["attn1"]
            for t in range(NT):
                for g in range(4):
                    dma(Sc, rtile(at)[t][:, g * 512:(g + 1) * 512],
                        qkvt[:, t * 3072 + g * 768:t * 3072 + g * 768 + 512],
                        ev_att, flow="tap")

        ev_pj = []
        pv_cast8 = None
        pv_w8 = []
        for oc in range(4):
            evw = []
            for j in range(4):
                buf = wl8[:, (j % 2) * D:(j % 2) * D + D]
                e1 = dma(Sy, buf, rtile(wproj_d)[oc * 4 + j],
                         [evw[-2] if len(evw) >= 2 else pv_cast8],
                         flow="w8" + str(j % 2))
                e2 = cp_(V, wb8[:], buf, [e1, evw[-1] if evw else pv_cast8])
                pv_cast8 = e2
                e3 = dma(Sy, wfm8.rearrange("p (f q) -> p f q", f=NF)[
                             :, :, j * 128:(j + 1) * 128],
                         wb8.rearrange("p (f q) -> p f q", q=128),
                         [e2] + (pv_w8 if j == 0 else []) +
                         (ev_qkv if oc == 0 and j == 0 else []),
                         transpose=True, flow="w8t")
                evw.append(e3)
            pv_w8 = []
            for tt in range(NT):
                e_tr2 = None
                for g in range(4):
                    attv = qkvt[:, tt * 3072 + g * 768:
                                tt * 3072 + g * 768 + 512].rearrange(
                        "p (i q) -> p i q", q=128)
                    e_tr2 = dma(Sy, afm8[:, g * 512:(g + 1) * 512].rearrange(
                                    "p (i q) -> p i q", q=128),
                                attv,
                                ev_att + [ev_pj[-1] if ev_pj else None],
                                transpose=True, flow="af")
                bank = PS8[2 + tt % 2]
                dep0 = evw + [e_tr2] + \
                    ([ev_pj[-2]] if len(ev_pj) >= 2 else [])
                for kc in range(NF):
                    e_mm = mm_(bank[:],
                               afm8[:, kc * 128:(kc + 1) * 128],
                               wfm8[:, kc * 512:(kc + 1) * 512],
                               start=(kc == 0), stop=(kc == NF - 1),
                               deps=dep0 if kc == 0 else ())
                e_ad = tt_(V, xres[:, tt * D + oc * 512:
                                   tt * D + (oc + 1) * 512],
                           xres[:, tt * D + oc * 512:
                                tt * D + (oc + 1) * 512],
                           bank[:], ALU.add, [e_mm, xres_last.get((tt, oc))])
                xres_last[(tt, oc)] = e_ad
                ev_pj.append(e_ad)
                pv_w8.append(e_ad)
        ph8.close()
        p.barrier()

        ev_st = []
        y_t = rtile(y_out)
        for t in range(NT):
            ev_st.append(dma(Sy, y_t[t], xres[:, t * D:(t + 1) * D], flow="yst"))
        all_done = list(ev_st) + [(f[0], f[1]) for f in p.flows.values()]
        p.op(Sy, lambda e: e.nop(), all_done, inc=1)
        p.op(Sc, lambda e: e.nop(), all_done, inc=1)

        with nc.Block() as block:
            p.finalize(block)

    return nc, sorted(tap_t.keys())


_cache = {}


def kernel(L0_attn_x, L0_attn_Wq, L0_attn_Wk, L0_attn_Wv, L0_attn_Wg,
           L0_attn_Wo, L0_ffn_W_g, L0_ffn_W_u, L0_ffn_W_d,
           L1_attn_weight_qkv, L1_attn_weight_proj, L1_ffn_w_gate,
           L1_ffn_w_up, L1_ffn_w_down, _taps=(), _return_raw=False):
    x = np.ascontiguousarray(np.asarray(L0_attn_x, dtype=np.float32))
    f32 = lambda a: np.ascontiguousarray(np.asarray(a, dtype=np.float32))
    wq, wk, wv, wg, wo = map(f32, (L0_attn_Wq, L0_attn_Wk, L0_attn_Wv,
                                   L0_attn_Wg, L0_attn_Wo))
    wgf, wuf, wdf = map(f32, (L0_ffn_W_g, L0_ffn_W_u, L0_ffn_W_d))
    wqkv, wproj = map(f32, (L1_attn_weight_qkv, L1_attn_weight_proj))
    w1g, w1u, w1d = map(f32, (L1_ffn_w_gate, L1_ffn_w_up, L1_ffn_w_down))

    tri = np.triu(np.ones((128, 128), np.float32)).astype(ml_dtypes.bfloat16)
    ident = np.eye(128, dtype=ml_dtypes.bfloat16)

    key = tuple(_taps)
    if key not in _cache:
        _cache[key] = build_program(_taps)
    nc, tap_names = _cache[key]

    in_maps = []
    for c in range(NCORES):
        xs = np.ascontiguousarray(x[:, c * SL:(c + 1) * SL, :].reshape(T, D))
        fm = np.zeros((128, NCORES), np.float32)
        fm[:, :c] = 1.0
        in_maps.append({
            "x": xs, "wq": wq, "wk": wk, "wv": wv, "wg": wg, "wo": wo,
            "wgf": wgf, "wuf": wuf, "wdf": wdf, "wqkv": wqkv, "wproj": wproj,
            "w1g_sh": np.ascontiguousarray(
                w1g[c * (HID // NCORES):(c + 1) * (HID // NCORES)]),
            "w1u_sh": np.ascontiguousarray(
                w1u[c * (HID // NCORES):(c + 1) * (HID // NCORES)]),
            "w1d_sh": np.ascontiguousarray(
                w1d[c * (D // NCORES):(c + 1) * (D // NCORES)]),
            "tri": tri, "ident": ident,
            "fmask": np.ascontiguousarray(fm),
            "fmaski": np.ascontiguousarray(1.0 - fm),
        })

    res = run_bass_kernel_spmd(nc, in_maps, core_ids=list(range(NCORES)))
    if _return_raw:
        return res
    out = np.empty((B, S, D), np.float32)
    for c in range(NCORES):
        out[:, c * SL:(c + 1) * SL, :] = np.asarray(
            res.results[c]["y"]).reshape(B, SL, D)
    return out


# version 07
# speedup vs baseline: 1.4616x; 1.0011x over previous
import math
import os
import numpy as np
import ml_dtypes
from contextlib import ExitStack

import concourse.bass as bass
import concourse.mybir as mybir
from concourse.bass_utils import run_bass_kernel_spmd

F32 = mybir.dt.float32
BF16 = mybir.dt.bfloat16
I32 = mybir.dt.int32
ALU = mybir.AluOpType
AF = mybir.ActivationFunctionType
AX = mybir.AxisListType

NCORES = 8
B = 2
S = 4096
D = 2048
SL = S // NCORES
T = B * SL
NT = T // 128
NF = D // 128
H = 16
DK = 128
HID = 8192
GN16 = 1.0 / 16.0
SCL = 1.0 / math.sqrt(DK)
LN2 = math.log(2.0)

H_LOOP = int(os.environ.get("K_HLOOP", "16"))
HC_LOOP = int(os.environ.get("K_HCLOOP", "16"))
RACEDET = os.environ.get("K_RACEDET", "0") == "1"

ENGS = ("sync", "scalar", "vector", "tensor", "gpsimd")
V, Sc, Te, Gp, Sy = "vector", "scalar", "tensor", "gpsimd", "sync"


class Stream:
    def __init__(self, name, sem):
        self.name = name
        self.sem = sem
        self.count = 0
        self.items = []
        self.waited = {}

    def covered(self, ev):
        sem, val = ev
        return self.waited.get(id(sem), -1) >= val


class Prog:
    def __init__(self, nc, stack):
        self.nc = nc
        self.stack = stack
        self.streams = {}
        for e in ENGS:
            sem = stack.enter_context(nc.semaphore(f"sem_{e}"))
            self.streams[e] = Stream(e, sem)
        self.cc_sem = stack.enter_context(nc.semaphore("sem_cc"))
        self.cc_count = 0
        self.flows = {}

    def flow(self, name):
        if name not in self.flows:
            sem = self.stack.enter_context(self.nc.semaphore("f_" + name))
            self.flows[name] = [sem, 0]
        return self.flows[name]

    def op(self, eng, emit, deps=(), inc=1, inc_sem=None):
        st = self.streams[eng]
        need = {}
        for ev in deps:
            if ev is None:
                continue
            if not st.covered(ev):
                sem, val = ev
                k = id(sem)
                if k not in need or need[k][1] < val:
                    need[k] = (sem, val)
        waits = []
        for sem, val in need.values():
            waits.append((sem, val))
            st.waited[id(sem)] = max(st.waited.get(id(sem), -1), val)
        if inc_sem is None:
            st.count += inc
            st.items.append((waits, emit, (st.sem, inc)))
            return (st.sem, st.count)
        st.items.append((waits, emit, (inc_sem[0], inc)))
        return (inc_sem[0], inc_sem[1])

    def collective(self, kind, alu, ins, outs, deps=()):
        self.cc_count += 1

        def emit(e):
            return e.collective_compute(kind, alu,
                                        replica_groups=[list(range(NCORES))],
                                        ins=ins, outs=outs)
        return self.op(Gp, emit, deps, inc=1, inc_sem=(self.cc_sem, self.cc_count))

    def barrier(self, include_cc=True):
        evs = [(st.sem, st.count) for st in self.streams.values() if st.count > 0]
        evs += [(f[0], f[1]) for f in self.flows.values() if f[1] > 0]
        if include_cc and self.cc_count:
            evs.append((self.cc_sem, self.cc_count))
        for st in self.streams.values():
            for ev in evs:
                if not st.covered(ev):
                    sem, val = ev
                    st.items.append(([(sem, val)], None, None))
                    st.waited[id(sem)] = max(st.waited.get(id(sem), -1), val)

    def finalize(self, block):
        def mk(st):
            def body(eng):
                for waits, emit, inc in st.items:
                    for sem, val in waits:
                        eng.wait_ge(sem, val)
                    if emit is not None:
                        ins = emit(eng)
                        if inc is not None:
                            ins.then_inc(inc[0], inc[1])
            return body
        for e in ENGS:
            getattr(block, e)(mk(self.streams[e]))


def build_program(taps=()):
    nc = bass.Bass("TRN2", target_bir_lowering=False,
                   detect_race_conditions=RACEDET)

    def din(name, shape, dt=F32):
        return nc.dram_tensor(name, list(shape), dt, kind="ExternalInput").ap()

    x_in = din("x", (T, D))
    wq_d = din("wq", (D, D)); wk_d = din("wk", (D, D)); wv_d = din("wv", (D, D))
    wg_d = din("wg", (D, D)); wo_d = din("wo", (D, D))
    wgf_d = din("wgf", (HID, D)); wuf_d = din("wuf", (HID, D))
    wdf_d = din("wdf", (D, HID))
    wqkv_d = din("wqkv", (3072, D)); wproj_d = din("wproj", (D, D))
    w1g_d = din("w1g_sh", (HID // NCORES, D))
    w1u_d = din("w1u_sh", (HID // NCORES, D))
    w1d_d = din("w1d_sh", (D // NCORES, HID))
    tri_d = din("tri", (128, 128), BF16)
    ident_d = din("ident", (128, 128), BF16)
    fmask_d = din("fmask", (128, NCORES))
    fmaski_d = din("fmaski", (128, NCORES))

    y_out = nc.dram_tensor("y", [T, D], F32, kind="ExternalOutput").ap()
    tap_t = {}

    def tap(name, shape, dt):
        if name in taps:
            tap_t[name] = nc.dram_tensor("tap_" + name, list(shape), dt,
                                         kind="ExternalOutput").ap()
            return tap_t[name]
        return None

    ccg_in = nc.dram_tensor("ccg_in", [2 * H, 128, 129], F32).ap()
    ccg_out = nc.dram_tensor("ccg_out", [NCORES, 2 * H, 128, 129], F32).ap()
    q1g_in = nc.dram_tensor("q1g_in", [HID // NCORES, D], BF16).ap()
    q1g_out = nc.dram_tensor("q1g_out", [HID, D], BF16).ap()
    q1u_in = nc.dram_tensor("q1u_in", [HID // NCORES, D], BF16).ap()
    q1u_out = nc.dram_tensor("q1u_out", [HID, D], BF16).ap()
    q1d_in = nc.dram_tensor("q1d_in", [D // NCORES, HID], BF16).ap()
    q1d_out = nc.dram_tensor("q1d_out", [D, HID], BF16).ap()
    sc_q = nc.dram_tensor("sc_q", [H, DK, T], BF16).ap()
    sc_o = nc.dram_tensor("sc_o", [H, DK, T], BF16).ap()
    sc_xn = nc.dram_tensor("sc_xn", [128, NF * T], BF16).ap()
    sc_xq = nc.dram_tensor("sc_xq", [128, NF * T], BF16).ap()

    gstack = ExitStack()
    with gstack:
        p = Prog(nc, gstack)

        def sb(st, *a, **k):
            return st.enter_context(nc.sbuf_tensor(*a, **k))

        def ps_(st, *a, **k):
            return st.enter_context(nc.psum_tensor(*a, **k))

        xres = sb(gstack, "xres", [128, NT * D], F32)
        trit = sb(gstack, "trit", [128, 128], BF16)
        idt = sb(gstack, "idt", [128, 128], BF16)
        cols = sb(gstack, "cols", [128, 256], F32)
        sinv_cols = sb(gstack, "sinv_cols", [128, NT], F32)
        acols = sb(gstack, "acols", [128, H * 8], F32)
        fmsk = sb(gstack, "fmsk", [128, NCORES], F32)
        fmski = sb(gstack, "fmski", [128, NCORES], F32)
        zcol = sb(gstack, "zcol", [128, 1], F32)
        mxc = sb(gstack, "mxc", [128, 3 * 64], F32)
        xres_last = {}
        ccol = sb(gstack, "ccol", [128, 16], F32)
        CONSTS = {}

        def ts_(eng, out, in0, s1, op0, s2=None, op1=None, deps=(), accum=None):
            if op1 is None:
                return p.op(eng, lambda e: e.tensor_scalar(
                    out=out, in0=in0, scalar1=s1, scalar2=None, op0=op0,
                    accum_out=accum), deps)
            return p.op(eng, lambda e: e.tensor_scalar(
                out=out, in0=in0, scalar1=s1, scalar2=s2, op0=op0, op1=op1,
                accum_out=accum), deps)

        def tt_(eng, out, in0, in1, alu, deps=()):
            return p.op(eng, lambda e: e.tensor_tensor(out=out, in0=in0,
                                                       in1=in1, op=alu), deps)

        def stt_(eng, out, in0, scalar, in1, op0, op1, deps=()):
            return p.op(eng, lambda e: e.scalar_tensor_tensor(
                out=out, in0=in0, scalar=scalar, in1=in1, op0=op0, op1=op1),
                deps)

        def act_(out, in_, func, bias=0.0, scale=1.0, accum=None, deps=()):
            return p.op(Sc, lambda e: e.activation(out=out, in_=in_, func=func,
                                                   bias=bias, scale=scale,
                                                   accum_out=accum), deps)

        def cp_(eng, out, in_, deps=()):
            return p.op(eng, lambda e: e.tensor_copy(out=out, in_=in_), deps)

        def red_(out, in_, alu, deps=(), absval=None, negate=None):
            return p.op(V, lambda e: e.tensor_reduce(
                out=out, in_=in_, axis=AX.X, op=alu,
                apply_absolute_value=absval, negate=negate), deps)

        def mm_(out, lhsT, rhs, start, stop, deps=()):
            return p.op(Te, lambda e: e.matmul(out=out, lhsT=lhsT, rhs=rhs,
                                               start=start, stop=stop), deps)

        def tr_(out, in_, deps=()):
            return p.op(Te, lambda e: e.transpose(out=out, in_=in_,
                                                  identity=idt[:]), deps)

        def scan_(out, d0, deps=()):
            return p.op(V, lambda e: e.tensor_tensor_scan(
                out=out, data0=d0, data1=d0, initial=0.0, op0=ALU.add,
                op1=ALU.bypass), deps)

        def ms_(eng, ap_, val, deps=()):
            return p.op(eng, lambda e: e.memset(ap_, val), deps)

        def recip_(out, in_, deps=()):
            return p.op(V, lambda e: e.reciprocal(out=out, in_=in_), deps)

        def dma(eng, out, in_, deps=(), transpose=False, flow="ld"):
            f = p.flow(flow)
            f[1] += 16
            return p.op(eng, lambda e: e.dma_start(out=out, in_=in_,
                                                   transpose=transpose), deps,
                        inc=16, inc_sem=(f[0], f[1]))

        def rtile(apx):
            return apx.rearrange("(t p) c -> t p c", p=128)

        def cc(v):
            if v not in CONSTS:
                i = len(CONSTS)
                ev = ms_(V, ccol[:, i:i + 1], float(v))
                CONSTS[v] = (ccol[:, i:i + 1], ev)
            return CONSTS[v][0]

        def ccev(v):
            cc(v)
            return CONSTS[v][1]

        def repl32(dst_flat, src_cols, nb, deps):
            d3 = dst_flat.rearrange("p (b e) -> p b e", e=32)
            ev = cp_(V, d3[:, :, 0:1],
                     src_cols.rearrange("p (b o) -> p b o", o=1), deps)
            k = 1
            while k < 32:
                kk = min(k, 32 - k)
                ev = cp_(V, d3[:, :, k:k + kk], d3[:, :, 0:kk], [ev])
                k += kk
            return ev

        def mx_scales(src_blocks, nb, iw, deps):
            amx = mxc[:, 0:nb]
            lg = mxc[:, 64:64 + nb]
            fg = mxc[:, 128:128 + nb]
            e1 = red_(amx, src_blocks, ALU.max, deps, absval=True)
            e2 = ts_(V, amx, amx, 1e-30, ALU.max, deps=[e1])
            e3 = act_(lg, amx, AF.Ln, deps=[e2])
            e4 = ts_(V, lg, lg, 1.0 / LN2, ALU.mult, 1e-9, ALU.add, deps=[e3])
            e5 = cp_(V, iw[:, 0:nb], lg, [e4])
            e6 = stt_(V, fg, iw[:, 0:nb], 0.0, lg, ALU.bypass, ALU.is_gt, [e5])
            e7 = tt_(V, lg, iw[:, 0:nb], fg, ALU.subtract, [e6])
            e8 = ts_(V, lg, lg, -2.0, ALU.add, deps=[e7])
            e9 = ts_(V, lg, lg, 127.0, ALU.min, -127.0, ALU.max, deps=[e8])
            er = act_(fg, lg, AF.Exp, scale=cc(-LN2), deps=[e9, ccev(-LN2)])
            ers = act_(lg, lg, AF.Exp, scale=cc(LN2), deps=[er, ccev(LN2)])
            return fg, lg, er, ers

        def mx_quant(in_f32, out_bf, W, base_inv, maxn, scr, deps,
                     out_free=(), in_more=()):
            nb = W // 32
            fg, lg, er, ers = mx_scales(
                in_f32.rearrange("p (b e) -> p b e", e=32), nb, scr["I"],
                list(deps))
            erp = repl32(scr["R"][:, 0:W], fg, nb, [er])
            ey = tt_(V, scr["A"][:, 0:W], in_f32, scr["R"][:, 0:W], ALU.mult,
                     [erp] + list(deps) + list(in_more))
            ersp = repl32(scr["R2"][:, 0:W], lg, nb, [ers])
            ea = act_(scr["S1"][:, 0:W], scr["A"][:, 0:W], AF.Abs, deps=[ey])
            es1 = ts_(V, scr["S2"][:, 0:W], scr["S1"][:, 0:W], 2.0, ALU.is_ge,
                      deps=[ea])
            es2 = ts_(V, scr["S1"][:, 0:W], scr["S1"][:, 0:W], 4.0, ALU.is_ge,
                      deps=[ea, es1])
            eu1 = ts_(V, out_bf[:, 0:W], scr["S2"][:, 0:W], -0.5, ALU.mult,
                      1.0, ALU.add, deps=[es1] + list(out_free))
            ev2 = tt_(V, scr["A"][:, 0:W], scr["A"][:, 0:W], out_bf[:, 0:W],
                      ALU.mult, [eu1])
            eu2 = ts_(V, out_bf[:, 0:W], scr["S1"][:, 0:W], -0.5, ALU.mult,
                      1.0, ALU.add, deps=[es2, ev2])
            ev3 = tt_(V, scr["A"][:, 0:W], scr["A"][:, 0:W], out_bf[:, 0:W],
                      ALU.mult, [eu2])
            ev4 = ts_(V, scr["A"][:, 0:W], scr["A"][:, 0:W], base_inv, ALU.mult,
                      0.5, ALU.add, deps=[ev3])
            eI = cp_(V, scr["I"][:, 0:W], scr["A"][:, 0:W], [ev4])
            eG = stt_(V, scr["R"][:, 0:W], scr["I"][:, 0:W], 0.0,
                      scr["A"][:, 0:W], ALU.bypass, ALU.is_gt, [eI])
            eK = tt_(V, out_bf[:, 0:W], scr["I"][:, 0:W], scr["R"][:, 0:W],
                     ALU.subtract, [eG])
            ew1 = ts_(V, scr["S2"][:, 0:W], scr["S2"][:, 0:W], 1.0, ALU.add,
                      deps=[eK])
            ew2 = stt_(V, scr["S1"][:, 0:W], scr["S1"][:, 0:W], 1.0,
                       scr["S2"][:, 0:W], ALU.add, ALU.mult, [ew1])
            ew3 = ts_(V, scr["S1"][:, 0:W], scr["S1"][:, 0:W], 1.0 / base_inv,
                      ALU.mult, deps=[ew2])
            eq = tt_(V, out_bf[:, 0:W], out_bf[:, 0:W], scr["S1"][:, 0:W],
                     ALU.mult, [ew3])
            ecl = ts_(V, out_bf[:, 0:W], out_bf[:, 0:W], maxn, ALU.min, -maxn,
                      ALU.max, deps=[eq])
            eo = tt_(V, out_bf[:, 0:W], out_bf[:, 0:W], scr["R2"][:, 0:W],
                     ALU.mult, [ecl, ersp])
            return eo, ey

        def mx_quant_gp(in_f32, out_bf, W, base_inv, maxn, scr, deps,
                        out_free=()):
            nb = W // 32
            fg, lg, er, ers = mx_scales(
                in_f32.rearrange("p (b e) -> p b e", e=32), nb, scr["I"],
                list(deps))
            erp = repl32g(scr["R"][:, 0:W], fg, nb, [er])
            ey = tt_(Gp, scr["A"][:, 0:W], in_f32, scr["R"][:, 0:W], ALU.mult,
                     [erp] + list(deps))
            ersp = repl32g(scr["R2"][:, 0:W], lg, nb, [ers])
            esq = tt_(Gp, scr["R"][:, 0:W], scr["A"][:, 0:W],
                      scr["A"][:, 0:W], ALU.mult, [ey])
            es1 = ts_(Gp, scr["S2"][:, 0:W], scr["R"][:, 0:W], 4.0, ALU.is_ge,
                      deps=[esq])
            es2 = ts_(Gp, scr["S1"][:, 0:W], scr["R"][:, 0:W], 16.0, ALU.is_ge,
                      deps=[esq, es1])
            eu1 = ts_(Gp, out_bf[:, 0:W], scr["S2"][:, 0:W], -0.5, ALU.mult,
                      1.0, ALU.add, deps=[es1] + list(out_free))
            ev2 = tt_(Gp, scr["A"][:, 0:W], scr["A"][:, 0:W], out_bf[:, 0:W],
                      ALU.mult, [eu1])
            eu2 = ts_(Gp, out_bf[:, 0:W], scr["S1"][:, 0:W], -0.5, ALU.mult,
                      1.0, ALU.add, deps=[es2, ev2])
            ev3 = tt_(Gp, scr["A"][:, 0:W], scr["A"][:, 0:W], out_bf[:, 0:W],
                      ALU.mult, [eu2])
            ev4 = ts_(Gp, scr["A"][:, 0:W], scr["A"][:, 0:W], base_inv,
                      ALU.mult, 0.5, ALU.add, deps=[ev3])
            eI = cp_(Gp, scr["I"][:, 0:W], scr["A"][:, 0:W], [ev4])
            eF = cp_(Gp, scr["R"][:, 0:W], scr["I"][:, 0:W], [eI])
            eD = tt_(Gp, scr["A"][:, 0:W], scr["R"][:, 0:W], scr["A"][:, 0:W],
                     ALU.subtract, [eF])
            eG = ts_(Gp, scr["A"][:, 0:W], scr["A"][:, 0:W], 0.0, ALU.is_gt,
                     deps=[eD])
            eK = tt_(Gp, out_bf[:, 0:W], scr["R"][:, 0:W], scr["A"][:, 0:W],
                     ALU.subtract, [eG])
            ew1 = ts_(Gp, scr["S2"][:, 0:W], scr["S2"][:, 0:W], 1.0, ALU.add,
                      deps=[eK])
            ea1 = ts_(Gp, scr["S1"][:, 0:W], scr["S1"][:, 0:W], 1.0, ALU.add,
                      deps=[eK])
            ew2 = tt_(Gp, scr["S1"][:, 0:W], scr["S1"][:, 0:W],
                      scr["S2"][:, 0:W], ALU.mult, [ew1, ea1])
            ew3 = ts_(Gp, scr["S1"][:, 0:W], scr["S1"][:, 0:W], 1.0 / base_inv,
                      ALU.mult, deps=[ew2])
            eq = tt_(Gp, out_bf[:, 0:W], out_bf[:, 0:W], scr["S1"][:, 0:W],
                     ALU.mult, [ew3])
            ecl = ts_(Gp, out_bf[:, 0:W], out_bf[:, 0:W], maxn, ALU.min, -maxn,
                      ALU.max, deps=[eq])
            eo = tt_(Gp, out_bf[:, 0:W], out_bf[:, 0:W], scr["R2"][:, 0:W],
                     ALU.mult, [ecl, ersp])
            return eo, ey

        def repl32g(dst_flat, src_cols, nb, deps):
            d3 = dst_flat.rearrange("p (b e) -> p b e", e=32)
            ev = cp_(Gp, d3[:, :, 0:1],
                     src_cols.rearrange("p (b o) -> p b o", o=1), deps)
            k = 1
            while k < 32:
                kk = min(k, 32 - k)
                ev = cp_(Gp, d3[:, :, k:k + kk], d3[:, :, 0:kk], [ev])
                k += kk
            return ev

        ev_tri = dma(Sy, trit[:], tri_d, flow="cst")
        ev_id = dma(Sy, idt[:], ident_d, flow="cst")
        ev_fm = dma(Sy, fmsk[:], fmask_d, flow="cst")
        ev_fmi = dma(Sy, fmski[:], fmaski_d, flow="cst")
        ev_cst = ev_fmi
        ev_z = ms_(V, zcol[:], 0.0)
        evx = []
        x_t = rtile(x_in)
        for t in range(NT):
            evx.append(dma(Sy, xres[:, t * D:(t + 1) * D], x_t[t]))
        ev_xall = evx[-1]

        ph12 = ExitStack()
        xn_fm = sb(ph12, "xn_fm", [128, NF * T], BF16, is_transpose_dest=True)
        pool = ExitStack()
        pA = sb(pool, "pA", [128, D], F32)
        pB = sb(pool, "pB", [128, D], F32)
        pC = sb(pool, "pC", [128, D], F32)
        pI = sb(pool, "pI", [128, D], I32)
        pBF = sb(pool, "pBF", [128, D], BF16)
        pBF2 = sb(pool, "pBF2", [128, D], BF16)
        pR2 = sb(pool, "pR2", [128, D], BF16)
        pO = sb(pool, "pO", [128, D], BF16)
        SCR = {"A": pA, "R": pC, "I": pI, "S1": pBF, "S2": pBF2, "R2": pR2}

        ph1 = ExitStack()
        xq_fm1 = sb(ph1, "xq_fm1", [128, NF * T], BF16, is_transpose_dest=True)

        def ln_cols(src_tile, base, eps, deps, dump, ddep):
            c = lambda i: cols[:, base + i:base + i + 1]
            e1 = red_(c(0), src_tile, ALU.add, deps)
            e2 = act_(dump[:], src_tile, AF.Square, accum=c(1),
                      deps=list(deps) + [ddep])
            e3 = ts_(V, c(2), c(0), 1.0 / D, ALU.mult, deps=[e1])
            e4 = ts_(V, c(3), c(1), 1.0 / D, ALU.mult, deps=[e2])
            e5 = tt_(V, c(4), c(2), c(2), ALU.mult, [e3])
            e6 = tt_(V, c(4), c(3), c(4), ALU.subtract, [e4, e5])
            e7 = act_(c(5), c(4), AF.Sqrt, bias=cc(eps), deps=[e6, ccev(eps)])
            e8 = recip_(c(5), c(5), [e7])
            e9 = stt_(V, c(6), c(2), -1.0, c(5), ALU.mult, ALU.mult, [e8])
            return c(5), c(6), e9

        ev_xnfm = []
        ev_xqfm1 = []
        pv_saf = pv_g = pv_tq = pv_bf = None
        for t in range(NT):
            xt = xres[:, t * D:(t + 1) * D]
            base = t * 24
            rc, nbc, e9 = ln_cols(xt, base, 1e-5, [ev_xall], pBF2, pv_tq)
            e_xn = act_(pA[:], xt, AF.Identity, bias=nbc, scale=rc,
                        deps=[e9, ev_xall, pv_saf])
            e_bf = cp_(V, pBF[:], pA[:], [e_xn, pv_tq])
            e_tp = dma(Sy, xn_fm.rearrange("p (f w) -> p f w", f=NF)[
                           :, :, t * 128:(t + 1) * 128],
                       pBF.rearrange("p (f q) -> p f q", q=128), [e_bf],
                       transpose=True, flow="t1")
            ev_xnfm.append(e_tp)
            rc2, nbc2, e9b = ln_cols(pA[:], base + 8, 1e-8, [e_xn], pBF2, None)
            pv_saf = e9b
            e_xnr = act_(pB[:], pA[:], AF.Identity, bias=nbc2, scale=rc2,
                         deps=[e9b, e_xn, pv_g])
            cmax = cols[:, base + 16:base + 17]
            csc = cols[:, base + 17:base + 18]
            e_mx = red_(cmax, pB[:], ALU.max, [e_xnr], absval=True)
            e_r = recip_(csc, cmax, [e_mx])
            e_s = ts_(V, csc, csc, 127.0, ALU.mult, deps=[e_r])
            e_si = recip_(sinv_cols[:, t:t + 1], csc, [e_s])
            e_v = ts_(V, pB[:], pB[:], csc, ALU.mult, 0.5, ALU.add,
                      deps=[e_s, e_xnr])
            e_i = cp_(V, pI[:], pB[:], [e_v])
            e_g = stt_(V, pC[:], pI[:], 0.0, pB[:], ALU.bypass, ALU.is_gt,
                       [e_i])
            pv_g = e_g
            e_k = tt_(V, pBF[:], pI[:], pC[:], ALU.subtract, [e_g, e_tp])
            e_cl = ts_(V, pBF[:], pBF[:], 127.0, ALU.min, -128.0, ALU.max,
                       deps=[e_k])
            e_tq = dma(Sy, xq_fm1.rearrange("p (f w) -> p f w", f=NF)[
                           :, :, t * 128:(t + 1) * 128],
                       pBF.rearrange("p (f q) -> p f q", q=128), [e_cl],
                       transpose=True, flow="t2")
            pv_tq = e_tq
            ev_xqfm1.append(e_tq)
        e_spxq = dma(Sc, sc_xq, xq_fm1[:], ev_xqfm1, flow="sp1")
        ph1.close()

        if tap("xnfm", (128, NF * T), BF16) is not None:
            dma(Sc, tap_t["xnfm"], xn_fm[:], ev_xnfm, flow="tap")
        if tap("sinv", (128, NT), F32) is not None:
            dma(Sc, tap_t["sinv"], sinv_cols[:], [e_si], flow="tap")

        phw = ExitStack()
        pL = sb(phw, "pL", [128, D], F32)
        ph1_done = [e_spxq, ev_xnfm[-1], e_si]
        ev_qg, ev_qu, ev_qd = [], [], []
        pv_ey = None
        pv_store = None
        for (src, dst, evlist) in ((w1g_d, q1g_in, ev_qg),
                                   (w1u_d, q1u_in, ev_qu)):
            src_t = rtile(src)
            dst_t = rtile(dst)
            for t in range(HID // NCORES // 128):
                el = dma(Sy, pL[:], src_t[t], [pv_ey] + ph1_done, flow="wlA")
                eo, pv_ey = mx_quant(pL[:], pO, D, 2.0, 6.0, SCR,
                                     [el] + ph1_done, [pv_store])
                pv_store = dma(Sc, dst_t[t], pO[:], [eo], flow="st" + dst_t.tensor.name)
                evlist.append(pv_store)
        src_t = rtile(w1d_d)
        dst_t = rtile(q1d_in)
        for t in range(D // NCORES // 128):
            for cchunk in range(4):
                el = dma(Sy, pL[:], src_t[t][:, cchunk * D:(cchunk + 1) * D],
                         [pv_ey] + ph1_done, flow="wlA")
                eo, pv_ey = mx_quant(pL[:], pO, D, 2.0, 6.0, SCR,
                                     [el] + ph1_done, [pv_store])
                pv_store = dma(Sc, dst_t[t][:, cchunk * D:(cchunk + 1) * D],
                               pO[:], [eo], flow="stD")
            ev_qd.append(pv_store)
        cc_qg = p.collective("AllGather", ALU.bypass, [q1g_in.opt()],
                             [q1g_out.opt()], ev_qg)
        cc_qu = p.collective("AllGather", ALU.bypass, [q1u_in.opt()],
                             [q1u_out.opt()], ev_qu)
        cc_qd = p.collective("AllGather", ALU.bypass, [q1d_in.opt()],
                             [q1d_out.opt()], ev_qd)
        if tap("qwg", (HID, D), BF16) is not None:
            dma(Sc, tap_t["qwg"], q1g_out, [cc_qg], flow="tap")
        phw.close()
        pool.close()
        p.barrier(include_cc=False)

        ph2 = ExitStack()
        gkl = sb(ph2, "gkl", [128, T], F32)
        An = sb(ph2, "An", [128, T], F32)
        gt1 = sb(ph2, "gt1", [128, 512], F32)
        qh = sb(ph2, "qh", [128, T], BF16)
        vh = sb(ph2, "vh", [128, T], BF16)
        oh = sb(ph2, "oh", [128, T], BF16)
        ktil = sb(ph2, "ktil", [128, 128], BF16)
        khat = sb(ph2, "khat", [128, 128], BF16)
        kth = sb(ph2, "kth", [128, 128], BF16)
        eqt = sb(ph2, "eqt", [128, 128], BF16)
        ekt = sb(ph2, "ekt", [128, 128], BF16)
        pbar = sb(ph2, "pbar", [128, 128], BF16)
        sbf = sb(ph2, "sbf", [128, 128], BF16)
        ssb = sb(ph2, "ssb", [128, 2 * 129], F32)
        wsl = sb(ph2, "wsl", [128, NF * 128], F32)
        wslb = [sb(ph2, f"wslb{i}", [128, NF * 128], BF16) for i in range(2)]
        gcolA = sb(ph2, "gcolA", [128, 8], F32)
        gcolB = sb(ph2, "gcolB", [128, 8], F32)
        gcolD = sb(ph2, "gcolD", [128, 8], F32)
        PS = [ps_(ph2, f"ps{i}", [128, 512], F32) for i in range(7)]
        PT = ps_(ph2, "pst", [128, 128], BF16)

        ev_ccg_w = []
        prev_tail = None
        prev_cast = None
        prev_sq = prev_so = None
        prev_ccg = None
        prev_dc = [None] * 8
        prev_scan_sread = None
        pv_kth = None
        for h in range(H_LOOP):
            def wslice(wd, buf, extra=()):
                nonlocal prev_cast
                e = dma(Sy, wsl.rearrange("p (f q) -> p f q", q=128),
                        wd.rearrange("(f p) o -> p f o", p=128)[
                            :, :, h * 128:(h + 1) * 128],
                        [prev_cast], flow="wsl")
                e2 = cp_(V, buf[:], wsl[:], [e] + list(extra))
                prev_cast = e2
                return e2

            e_wg = wslice(wg_d, wslb[0], [prev_tail])
            ev_scan = [None, None]
            for half in range(2):
                for kc in range(NF):
                    e_mm = mm_(PS[half][:],
                               wslb[0][:, kc * 128:(kc + 1) * 128],
                               xn_fm[:, kc * T + half * 512:
                                     kc * T + (half + 1) * 512],
                               start=(kc == 0), stop=(kc == NF - 1),
                               deps=([e_wg, prev_tail] + ev_xnfm)
                               if kc == 0 else ())
                e_c = ts_(V, gt1[:], PS[half][:], -87.0, ALU.max,
                          deps=[e_mm, prev_scan_sread])
                e_e = act_(gt1[:], gt1[:], AF.Exp, scale=cc(-1.0),
                           deps=[e_c, ccev(-1.0)])
                e_u = ts_(V, gt1[:], gt1[:], 1.0, ALU.add, deps=[e_e])
                e_l = act_(gkl[:, half * 512:(half + 1) * 512], gt1[:], AF.Ln,
                           deps=[e_u])
                ev_scan[half] = scan_(An[:, half * 512:(half + 1) * 512],
                                      gkl[:, half * 512:(half + 1) * 512],
                                      [e_l])
            ev_bc = [None] * 8
            for c in range(8):
                aoff_src = zcol[:] if c % 4 == 0 else \
                    An[:, (c // 4) * 512 + (c % 4) * 128 - 1:
                       (c // 4) * 512 + (c % 4) * 128]
                e_b1 = ts_(V, gcolA[:, c:c + 1], aoff_src, GN16, ALU.mult,
                           deps=[ev_scan[c // 4], ev_z, prev_dc[c]])
                e_b2 = ts_(V, gcolB[:, c:c + 1], aoff_src, -GN16, ALU.mult,
                           deps=[ev_scan[c // 4], prev_dc[c]])
                ev_bc[c] = (e_b1, e_b2)
            ev_ac = []
            for c in range(8):
                dst = acols[:, h * 8 + c:h * 8 + c + 1]
                if c % 4 == 0:
                    ev_ac.append(cp_(V, dst, zcol[:], [ev_z]))
                else:
                    cs = (c // 4) * 512 + (c % 4) * 128
                    ev_ac.append(cp_(V, dst, An[:, cs - 1:cs],
                                     [ev_scan[c // 4]]))
            e_wq = wslice(wq_d, wslb[1], [prev_tail])
            for half in range(2):
                for kc in range(NF):
                    e_mmq = mm_(PS[2 + half][:],
                                wslb[1][:, kc * 128:(kc + 1) * 128],
                                xn_fm[:, kc * T + half * 512:
                                      kc * T + (half + 1) * 512],
                                start=(kc == 0), stop=(kc == NF - 1),
                                deps=[e_wq, prev_tail] if kc == 0 else ())
            ev_qmm = e_mmq
            e_wk = wslice(wk_d, wslb[0], [e_mm])
            for half in range(2):
                for kc in range(NF):
                    e_mmk = mm_(PS[4 + half][:],
                                wslb[0][:, kc * 128:(kc + 1) * 128],
                                xn_fm[:, kc * T + half * 512:
                                      kc * T + (half + 1) * 512],
                                start=(kc == 0), stop=(kc == NF - 1),
                                deps=[e_wk, prev_tail] if kc == 0 else ())
            ev_kmm = e_mmk
            e_wv = wslice(wv_d, wslb[1], [ev_qmm])
            ev_vev = [None] * NT
            for tt in range(NT):
                sl6 = PS[6][:, (tt % 4) * 128:(tt % 4) * 128 + 128]
                dep0 = [e_wv, prev_tail]
                if tt > 0:
                    dep0.append(ev_vev[tt - 1])
                for kc in range(NF):
                    e_mmv = mm_(sl6,
                                xn_fm[:, kc * T + tt * 128:
                                      kc * T + (tt + 1) * 128],
                                wslb[1][:, kc * 128:(kc + 1) * 128],
                                start=(kc == 0), stop=(kc == NF - 1),
                                deps=dep0 if kc == 0 else ())
                ev_vev[tt] = cp_(V, vh[:, tt * 128:(tt + 1) * 128], sl6,
                                 [e_mmv])
            e_z0 = ms_(V, ssb[:], 0.0, [prev_ccg])
            ev_oev = [None] * 8
            e_qt_last = None
            pv_P = pv_tr = pv_D = pv_o1 = pv_oi = None
            pv_qt = pv_kt = None
            e_su = None
            e_dcr = None
            for c in range(8):
                b = c // 4
                cs = c * 128
                first_of_b = (c % 4 == 0)
                e_b1, e_b2 = ev_bc[c]
                e_eq = act_(eqt[:], An[:, cs:cs + 128], AF.Exp, scale=cc(-GN16),
                            bias=gcolA[:, c:c + 1],
                            deps=[e_b1, ev_scan[b], ccev(-GN16), pv_qt])
                e_ek = act_(ekt[:], An[:, cs:cs + 128], AF.Exp, scale=cc(GN16),
                            bias=gcolB[:, c:c + 1],
                            deps=[e_b2, ev_scan[b], ccev(GN16), pv_kt])
                qps = PS[2 + b][:, (c % 4) * 128:(c % 4) * 128 + 128]
                e_qt = stt_(V, qh[:, cs:cs + 128], eqt[:], SCL, qps,
                            ALU.mult, ALU.mult,
                            [e_eq, ev_qmm, prev_sq if c == 0 else None])
                e_qt_last = e_qt
                pv_qt = e_qt
                kps = PS[4 + b][:, (c % 4) * 128:(c % 4) * 128 + 128]
                e_kt = tt_(V, ktil[:], kps, ekt[:], ALU.mult,
                           [e_ek, ev_kmm, pv_P])
                pv_kt = e_kt
                e_dc = act_(gcolD[:, c:c + 1], An[:, cs + 127:cs + 128],
                            AF.Exp, scale=cc(-GN16), bias=gcolA[:, c:c + 1],
                            deps=[e_b1, ev_scan[b], prev_dc[c]])
                prev_dc[c] = e_dc
                e_kh = ts_(V, khat[:], ktil[:], gcolD[:, c:c + 1], ALU.mult,
                           deps=[e_kt, e_dc, pv_tr])
                e_tr = tr_(PT[:], khat[:],
                           [e_kh, ev_cst] + ([pv_kth] if pv_kth else []))
                pv_tr = e_tr
                e_kth = cp_(V, kth[:], PT[:], [e_tr, pv_D])
                pv_kth = e_kth
                e_P = mm_(PS[0][:, 0:128], ktil[:], qh[:, cs:cs + 128],
                          start=True, stop=True,
                          deps=[e_qt, e_kt,
                                ev_oev[c - 1] if c else prev_tail])
                pv_P = e_P
                e_pb = tt_(V, pbar[:], PS[0][:, 0:128], trit[:], ALU.mult,
                           [e_P, ev_cst, pv_o1])
                e_sb = cp_(V, sbf[:], ssb[:, b * 129:b * 129 + 128],
                           [e_z0, pv_oi, e_su])
                ops = PS[1][:, (c % 4) * 128:(c % 4) * 128 + 128]
                e_o1 = mm_(ops, vh[:, cs:cs + 128], pbar[:],
                           start=True, stop=first_of_b,
                           deps=[e_pb, ev_vev[c],
                                 ev_oev[c - 1] if c else prev_tail])
                pv_o1 = e_o1
                if not first_of_b:
                    e_o1 = mm_(ops, sbf[:], qh[:, cs:cs + 128],
                               start=False, stop=True, deps=[e_sb, e_qt])
                    pv_oi = e_o1
                e_oe = cp_(V, oh[:, cs:cs + 128], ops,
                           [e_o1, prev_so if c == 0 else None])
                ev_oev[c] = e_oe
                e_D = mm_(PS[6][:, 128:256], kth[:], vh[:, cs:cs + 128],
                          start=True, stop=True,
                          deps=[e_kth, ev_vev[c],
                                e_su if c else ev_vev[NT - 1]])
                pv_D = e_D
                e_su = stt_(V, ssb[:, b * 129:b * 129 + 128],
                            ssb[:, b * 129:b * 129 + 128],
                            gcolD[:, c:c + 1], PS[6][:, 128:256],
                            ALU.mult, ALU.add, [e_D, e_sb, e_dc])
                if c % 4 == 3:
                    e_dcr = act_(ssb[:, b * 129 + 128:b * 129 + 129],
                                 An[:, b * 512 + 511:b * 512 + 512], AF.Exp,
                                 scale=cc(-GN16), deps=[ev_scan[b], e_su])
            prev_scan_sread = e_dcr
            prev_sq = dma(Sc, sc_q[h], qh[:], [e_qt_last], flow="sq")
            prev_so = dma(Sc, sc_o[h], oh[:], ev_oev, flow="so")
            e_c0 = dma(Sc, ccg_in[h], ssb[:, 0:129], [e_su, e_dcr], flow="ccw")
            e_c1 = dma(Sc, ccg_in[H + h], ssb[:, 129:258], [e_su, e_dcr], flow="ccw")
            prev_ccg = e_c1
            ev_ccg_w += [prev_sq, prev_so, e_c0, e_c1]
            prev_tail = ev_oev[7]

        if "qh2" in taps:
            tap("qh2", (128, T), BF16)
            dma(Sc, tap_t["qh2"], qh[:], [prev_sq], flow="tap")
        if "vh2" in taps:
            tap("vh2", (128, T), BF16)
            dma(Sc, tap_t["vh2"], vh[:], ev_vev, flow="tap")
        if "oh2" in taps:
            tap("oh2", (128, T), BF16)
            dma(Sc, tap_t["oh2"], oh[:], ev_oev, flow="tap")
        if "An2" in taps:
            tap("An2", (128, T), F32)
            dma(Sc, tap_t["An2"], An[:], [prev_scan_sread, ev_scan[0], ev_scan[1]], flow="tap")
        if "ssb2" in taps:
            tap("ssb2", (128, 258), F32)
            dma(Sc, tap_t["ssb2"], ssb[:], [e_su, e_dcr], flow="tap")

        cc_g = p.collective("AllGather", ALU.bypass, [ccg_in.opt()],
                            [ccg_out.opt()], ev_ccg_w)
        ph2.close()
        ph12.close()
        p.barrier()

        ph3 = ExitStack()
        sfold = sb(ph3, "sfold", [128, 2 * H * 128], F32)
        o_all = sb(ph3, "o_all", [128, H * T], BF16)
        qh3 = sb(ph3, "qh3", [128, T], BF16)
        oh3 = sb(ph3, "oh3", [128, T], BF16)
        sbf3 = sb(ph3, "sbf3", [128, 128], BF16)
        PS3 = [ps_(ph3, f"ps3_{i}", [128, 512], F32) for i in range(4)]

        ph3a = ExitStack()
        tjl = sb(ph3a, "tjl", [128, 2 * H * 129], F32)
        e_fold = ms_(V, sfold[:], 0.0)
        fold_q = [e_fold] * (2 * H)
        for j in range(NCORES):
            e_lj = dma(Sy, tjl.rearrange("p (q c) -> p q c", c=129),
                       ccg_out[j].rearrange("q p c -> p q c"),
                       [cc_g, e_fold], flow="tj")
            e_m1 = ts_(V, tjl[:], tjl[:], fmsk[:, j:j + 1], ALU.mult,
                       deps=[e_lj, ev_cst])
            e_m2 = ts_(V, tjl.rearrange("p (q c) -> p q c", c=129)[
                           :, :, 128:129],
                       tjl.rearrange("p (q c) -> p q c", c=129)[
                           :, :, 128:129],
                       fmski[:, j:j + 1], ALU.add, deps=[e_m1, ev_cst])
            for q in range(2 * H):
                e_fold = stt_(V, sfold[:, q * 128:(q + 1) * 128],
                              sfold[:, q * 128:(q + 1) * 128],
                              tjl[:, q * 129 + 128:q * 129 + 129],
                              tjl[:, q * 129:q * 129 + 128],
                              ALU.mult, ALU.add, [e_m2, fold_q[q]])
                fold_q[q] = e_fold
        ph3a.close()

        ev_oall = []
        if H_LOOP < H:
            ev_oall.append(ms_(V, o_all[:], 0.0))
        pv_cor = None
        for h in range(H_LOOP):
            e_lq = dma(Sy, qh3[:], sc_q[h],
                       [e_fold, ev_oall[-1] if ev_oall else None], flow="lq")
            e_lo = dma(Sy, oh3[:], sc_o[h],
                       [e_fold, ev_oall[-1] if ev_oall else None], flow="lo")
            for c in range(8):
                b = c // 4
                qp = b * H + h
                e_bk = act_(cols[:, 200:201],
                            acols[:, h * 8 + c:h * 8 + c + 1],
                            AF.Exp, scale=cc(-GN16),
                            deps=[e_lq, ev_oall[-1] if ev_oall else None])
                e_sk = ts_(V, sbf3[:], sfold[:, qp * 128:(qp + 1) * 128],
                           cols[:, 200:201], ALU.mult,
                           deps=[e_bk, e_fold, pv_cor])
                e_cor = mm_(PS3[0][:, (c % 4) * 128:(c % 4) * 128 + 128],
                            sbf3[:], qh3[:, c * 128:(c + 1) * 128],
                            start=True, stop=True,
                            deps=[e_sk, e_lq,
                                  ev_oall[-1] if ev_oall else None])
                pv_cor = e_cor
                e_up = tt_(V, o_all[:, (h * 8 + c) * 128:
                                    (h * 8 + c + 1) * 128],
                           oh3[:, c * 128:(c + 1) * 128],
                           PS3[0][:, (c % 4) * 128:(c % 4) * 128 + 128],
                           ALU.add, [e_cor, e_lo])
                ev_oall.append(e_up)
        if tap("o", (128, H * T), BF16) is not None:
            dma(Sc, tap_t["o"], o_all[:], ev_oall, flow="tap")

        ph3b = ExitStack()
        wof = sb(ph3b, "wof", [128, NF * 512], F32)
        wofb = sb(ph3b, "wofb", [128, NF * 512], BF16)
        ev_wo = []
        pv_cw = None
        for oc in range(4):
            e_lw = dma(Sy, wof.rearrange("p (f q) -> p f q", q=512),
                       wo_d.rearrange("(f p) o -> p f o", p=128)[
                           :, :, oc * 512:(oc + 1) * 512],
                       [pv_cw, e_fold], flow="wo")
            e_cw = cp_(V, wofb[:], wof[:],
                       [e_lw, ev_wo[-1] if ev_wo else None])
            pv_cw = e_cw
            for tt in range(NT):
                bank = PS3[1 + tt % 2]
                dep0 = [e_cw] + ev_oall + \
                    ([ev_wo[-2]] if len(ev_wo) >= 2 else [])
                for hh in range(H):
                    e_mm = mm_(bank[:],
                               o_all[:, (hh * 8 + tt) * 128:
                                     (hh * 8 + tt + 1) * 128],
                               wofb[:, hh * 512:(hh + 1) * 512],
                               start=(hh == 0), stop=(hh == H - 1),
                               deps=dep0 if hh == 0 else ())
                e_ad = tt_(V, xres[:, tt * D + oc * 512:
                                   tt * D + (oc + 1) * 512],
                           xres[:, tt * D + oc * 512:
                                tt * D + (oc + 1) * 512],
                           bank[:], ALU.add, [e_mm, xres_last.get((tt, oc))])
                xres_last[(tt, oc)] = e_ad
                ev_wo.append(e_ad)
        ph3b.close()
        ph3.close()
        p.barrier()

        def ffn_pass(layer, xq_b, ev_act_ready, stk):
            L = f"l{layer}"
            if layer == 0:
                wld = sb(stk, L + "wld", [128, 1024], F32)
                wldd = sb(stk, L + "wldd", [128, 512], F32)
            else:
                wld = wldd = None
            wld_state = {}
            if layer == 0:
                wsg_g = sb(stk, L + "wsg_gu", [128, 1024], BF16)
                wsg_u = wsg_g
                wsg_d = sb(stk, L + "wsg_d", [128, 512], BF16)
            else:
                wsg_g = wsg_u = wsg_d = None
            wfm_g = sb(stk, L + "wfm_g", [128, NF * 512], BF16,
                       is_transpose_dest=True)
            wfm_u = sb(stk, L + "wfm_u", [128, NF * 512], BF16,
                       is_transpose_dest=True)
            wdfm = [sb(stk, L + f"wdfm{i}", [128, 4 * D], BF16,
                       is_transpose_dest=True) for i in range(2)]
            gsl = sb(stk, L + "gsl", [128, NT * 512], BF16)
            usl = sb(stk, L + "usl", [128, NT * 512], BF16)
            pfm = [sb(stk, L + f"pfm{i}", [128, 4 * T], BF16,
                      is_transpose_dest=True) for i in range(2)]
            PS5 = [ps_(stk, L + f"ps5_{i}", [128, 512], F32) for i in range(8)]
            if layer == 1:
                iq = {
                    "A": sb(stk, "iqA", [128, 512], F32),
                    "R": sb(stk, "iqR", [128, 512], F32),
                    "I": sb(stk, "iqI", [128, 512], I32),
                    "S1": sb(stk, "iqS1", [128, 512], BF16),
                    "S2": sb(stk, "iqS2", [128, 512], BF16),
                    "R2": sb(stk, "iqR2", [128, 512], BF16),
                }

            NHC_ = HC_LOOP

            def prep(stage, hc, wdram, quant_src, wld_ap, wsg_ap, dst_fm,
                     ntile, wchunk, state, fsuf=""):
                evs = []
                halves = 2 if (quant_src is None and wchunk == D) else 1
                wc = wchunk // halves
                for jj in range(ntile * halves):
                    j, hh = jj // halves, jj % halves
                    if wchunk == D:
                        srcap = rtile(wdram if quant_src is None else
                                      quant_src)[hc * ntile + j]
                        if halves == 2:
                            srcap = srcap[:, hh * 1024:(hh + 1) * 1024]
                    else:
                        srcap = rtile(wdram if quant_src is None else
                                      quant_src)[j][:, hc * 512:(hc + 1) * 512]
                    if quant_src is None:
                        lstate = wld_state if wld_ap is not wldd else state
                        e1 = dma(Sy, wld_ap[:, 0:wc], srcap,
                                 [lstate.get("sign")], flow=L + stage + "w")
                        e2 = act_(wsg_ap[:, 0:wc], wld_ap[:, 0:wc],
                                  AF.Sign,
                                  deps=[e1, evs[-1] if evs else
                                        state.get("tr"),
                                        lstate.get("trr")])
                        state["sign"] = e2
                        lstate["sign"] = e2
                    else:
                        nblk = wc // 128
                        fv = dst_fm.rearrange("p (f w) -> p f w",
                                              f=wchunk // 128)
                        e3 = dma(Sy, fv[:, hh * nblk:(hh + 1) * nblk,
                                        j * 128:(j + 1) * 128],
                                 srcap.rearrange("p (f q) -> p f q", q=128),
                                 [state.get("use")] if jj == 0 else (),
                                 transpose=True, flow=L + stage + "t" + fsuf)
                        evs.append(e3)
                        continue
                    nblk = wc // 128
                    fv = dst_fm.rearrange("p (f w) -> p f w", f=wchunk // 128)
                    e3 = dma(Sc, fv[:, hh * nblk:(hh + 1) * nblk,
                                    j * 128:(j + 1) * 128],
                             wsg_ap[:, 0:wc].rearrange(
                                 "p (f q) -> p f q", q=128),
                             [e2] + ([state.get("use")] if jj == 0 else []),
                             transpose=True, flow=L + stage + "t" + fsuf)
                    evs.append(e3)
                    if wld_ap is not wldd:
                        lstate["trr"] = e3
                state["tr"] = evs[-1]
                return evs

            def down_block(dn_evd, dn_evp, buf, dn_w):
                for tt in range(NT):
                    for oc in range(4):
                        bank = PS5[4 + (tt * 4 + oc) % 4]
                        dep0 = dn_evd + dn_evp +                             ([ev_dn[-4]] if len(ev_dn) >= 4 else [])
                        for ks in range(4):
                            e_mm = mm_(bank[:],
                                       buf[:, ks * T + tt * 128:
                                           ks * T + (tt + 1) * 128],
                                       dn_w[:, ks * D + oc * 512:
                                            ks * D + (oc + 1) * 512],
                                       start=(ks == 0), stop=(ks == 3),
                                       deps=dep0 if ks == 0 else ())
                        e_ad = tt_(V, xres[:, tt * D + oc * 512:
                                           tt * D + (oc + 1) * 512],
                                   xres[:, tt * D + oc * 512:
                                        tt * D + (oc + 1) * 512],
                                   bank[:], ALU.add,
                                   [e_mm, xres_last.get((tt, oc))])
                        xres_last[(tt, oc)] = e_ad
                        ev_dn.append(e_ad)

            st_g = {}
            st_u = {}
            st_d = {}
            ev_gev = [None] * NT
            ev_uev = [None] * NT
            ev_dn = []
            pend_down = None
            prev_ptr = None

            for hc in range(NHC_):
                st_g["use"] = ev_gev[-1]
                evg = prep("g", hc, wgf_d if layer == 0 else None,
                           None if layer == 0 else q1g_out,
                           wld if layer == 0 else None,
                           wsg_g, wfm_g, 4, D, st_g)
                st_u["use"] = ev_uev[-1]
                evu = prep("u", hc, wuf_d if layer == 0 else None,
                           None if layer == 0 else q1u_out,
                           wld if layer == 0 else None,
                           wsg_u, wfm_u, 4, D, st_u)
                st_d["use"] = ev_dn[-1] if ev_dn else None
                evd = prep("d", hc, wdf_d if layer == 0 else None,
                           None if layer == 0 else q1d_out,
                           wldd, wsg_d, wdfm[hc % 2], NF, 512, st_d,
                           fsuf=str(hc % 2))
                if pend_down is not None:
                    dn_evd, dn_evp, dn_buf, dn_w = pend_down
                    down_block(dn_evd, dn_evp, dn_buf, dn_w)
                for tt in range(NT):
                    bank = PS5[tt % 2]
                    dep0 = evg + list(ev_act_ready) +                         [ev_gev[tt], ev_gev[tt - 2] if tt >= 2 else None]
                    for kc in range(NF):
                        e_mm = mm_(bank[:],
                                   xq_b[:, kc * T + tt * 128:
                                        kc * T + (tt + 1) * 128],
                                   wfm_g[:, kc * 512:(kc + 1) * 512],
                                   start=(kc == 0), stop=(kc == NF - 1),
                                   deps=dep0 if kc == 0 else ())
                    if layer == 0:
                        e_ev = act_(gsl[:, tt * 512:(tt + 1) * 512], bank[:],
                                    AF.Silu, scale=sinv_cols[:, tt:tt + 1],
                                    deps=[e_mm, prev_ptr])
                    else:
                        e_ev = act_(gsl[:, tt * 512:(tt + 1) * 512], bank[:],
                                    AF.Silu, deps=[e_mm, prev_ptr])
                    ev_gev[tt] = e_ev
                ev_p = [None] * NT
                pbuf = pfm[hc % 2]
                for tt in range(NT):
                    bank = PS5[2 + tt % 2]
                    dep0 = evu + [ev_uev[tt],
                                  ev_uev[tt - 2] if tt >= 2 else None]
                    for kc in range(NF):
                        e_mm = mm_(bank[:],
                                   xq_b[:, kc * T + tt * 128:
                                        kc * T + (tt + 1) * 128],
                                   wfm_u[:, kc * 512:(kc + 1) * 512],
                                   start=(kc == 0), stop=(kc == NF - 1),
                                   deps=dep0 if kc == 0 else ())
                    if layer == 0:
                        e_ev = act_(usl[:, tt * 512:(tt + 1) * 512], bank[:],
                                    AF.Identity,
                                    scale=sinv_cols[:, tt:tt + 1],
                                    deps=[e_mm])
                    else:
                        e_ev = act_(usl[:, tt * 512:(tt + 1) * 512], bank[:],
                                    AF.Identity, deps=[e_mm])
                    ev_uev[tt] = e_ev
                    sl = gsl[:, tt * 512:(tt + 1) * 512]
                    e_p = tt_(V, sl, sl, usl[:, tt * 512:(tt + 1) * 512],
                              ALU.mult, [ev_uev[tt], ev_gev[tt]])
                    if layer == 1:
                        e_p, _ = mx_quant(sl, sl, 512, 8.0, 7.5, iq, [e_p])
                    e_tp = dma(Sc, pbuf.rearrange("p (hs w) -> p hs w", hs=4)[
                                   :, :, tt * 128:(tt + 1) * 128],
                               sl.rearrange("p (hs q) -> p hs q", q=128),
                               [e_p, ev_dn[-1] if ev_dn else None],
                               transpose=True, flow=L + "pt" + str(hc % 2))
                    ev_p[tt] = e_tp
                pend_down = (evd, ev_p, pbuf, wdfm[hc % 2])
                prev_ptr = ev_p[-1]
            dn_evd, dn_evp, dn_buf, dn_w = pend_down
            down_block(dn_evd, dn_evp, dn_buf, dn_w)
            if any((L + nm) in taps for nm in
                   ("dwdfm0", "dwdfm1", "dpfm0", "dpfm1", "dgsl", "dusl")):
                p.barrier()
                for nm, buf, w in (("dwdfm0", wdfm[0], 4 * D),
                                   ("dwdfm1", wdfm[1], 4 * D),
                                   ("dpfm0", pfm[0], 4 * T),
                                   ("dpfm1", pfm[1], 4 * T),
                                   ("dgsl", gsl, NT * 512),
                                   ("dusl", usl, NT * 512)):
                    if tap(L + nm, (128, w), BF16) is not None:
                        dma(Sc, tap_t[L + nm], buf[:], flow="tap")
                p.barrier()

        ph5 = ExitStack()
        xq_fm5 = sb(ph5, "xq_fm5", [128, NF * T], BF16)
        e_rld = dma(Sy, xq_fm5[:], sc_xq, flow="r5")
        ffn_pass(0, xq_fm5, [e_rld], ph5)
        ph5.close()
        p.barrier()
        if tap("x1", (T, D), F32) is not None:
            for t in range(NT):
                dma(Sc, rtile(tap_t["x1"])[t], xres[:, t * D:(t + 1) * D], flow="tap")
            p.barrier()

        ph67 = ExitStack()
        xq_fm6 = sb(ph67, "xq_fm6", [128, NF * T], BF16,
                    is_transpose_dest=True)
        ph6 = ExitStack()
        xn_fm6 = sb(ph6, "xn_fm6", [128, NF * T], BF16, is_transpose_dest=True)
        s6 = {
            "A": sb(ph6, "s6A", [128, D], F32),
            "R": sb(ph6, "s6R", [128, D], F32),
            "I": sb(ph6, "s6I", [128, D], I32),
            "S1": sb(ph6, "s6S1", [128, D], BF16),
            "S2": sb(ph6, "s6S2", [128, D], BF16),
            "R2": sb(ph6, "s6R2", [128, D], BF16),
        }
        s6X = sb(ph6, "s6X", [128, D], F32)
        s6BF = sb(ph6, "s6BF", [128, D], BF16)
        s6Q = sb(ph6, "s6Q", [128, D], BF16)

        ev_xnfm2 = []
        ev_xqfm2 = []
        pv_eyq = pv_tq6 = pv_tp6 = None
        for t in range(NT):
            xt = xres[:, t * D:(t + 1) * D]
            base = t * 24
            rc, nbc, e9 = ln_cols(xt, base, 1e-5, (), s6BF, pv_tp6)
            e_xn = act_(s6X[:], xt, AF.Identity, bias=nbc, scale=rc,
                        deps=[e9, pv_eyq])
            e_bf = cp_(V, s6BF[:], s6X[:], [e_xn, pv_tp6])
            e_tp = dma(Sy, xn_fm6.rearrange("p (f w) -> p f w", f=NF)[
                           :, :, t * 128:(t + 1) * 128],
                       s6BF.rearrange("p (f q) -> p f q", q=128), [e_bf],
                       transpose=True, flow="t3")
            pv_tp6 = e_tp
            ev_xnfm2.append(e_tp)
            eo, ey = mx_quant(s6X[:], s6Q, D, 8.0, 7.5, s6, [e_xn],
                              out_free=[pv_tq6])
            pv_eyq = ey
            e_tq = dma(Sy, xq_fm6.rearrange("p (f w) -> p f w", f=NF)[
                           :, :, t * 128:(t + 1) * 128],
                       s6Q.rearrange("p (f q) -> p f q", q=128), [eo],
                       transpose=True, flow="t4")
            pv_tq6 = e_tq
            ev_xqfm2.append(e_tq)
        if tap("qx1", (128, NF * T), BF16) is not None:
            dma(Sc, tap_t["qx1"], xq_fm6[:], ev_xqfm2, flow="tap")
        e_spxn = dma(Sc, sc_xn, xn_fm6[:], ev_xnfm2, flow="sp2")
        ph6.close()
        p.barrier()

        ph7 = ExitStack()
        ffn_pass(1, xq_fm6, ev_xqfm2, ph7)
        ph7.close()
        ph67.close()
        p.barrier()

        ph8 = ExitStack()
        qkvt = sb(ph8, "qkvt", [128, NT * 3072], BF16)
        a8 = sb(ph8, "a8", [128, 1024], F32)
        c8 = sb(ph8, "c8", [128, 64], F32)
        wl8 = sb(ph8, "wl8", [128, 2 * D], F32)
        wb8 = sb(ph8, "wb8", [128, D], BF16)
        wfm8 = sb(ph8, "wfm8", [128, NF * 512], BF16, is_transpose_dest=True)
        afm8 = sb(ph8, "afm8", [128, NF * 128], BF16, is_transpose_dest=True)
        PS8 = [ps_(ph8, f"ps8_{i}", [128, 512], F32) for i in range(4)]

        ph8a = ExitStack()
        xn_fm8 = sb(ph8a, "xn_fm8", [128, NF * T], BF16)
        ev_xn8 = [dma(Sy, xn_fm8[:], sc_xn, flow="r8")]

        ev_qkv = []
        pv_cast8 = None
        pv_w8 = []
        for oc in range(6):
            evw = []
            for j in range(4):
                buf = wl8[:, (j % 2) * D:(j % 2) * D + D]
                e1 = dma(Sy, buf, rtile(wqkv_d)[oc * 4 + j],
                         [evw[-2] if len(evw) >= 2 else pv_cast8],
                         flow="w8" + str(j % 2))
                e2 = cp_(V, wb8[:], buf, [e1, evw[-1] if evw else pv_cast8])
                pv_cast8 = e2
                e3 = dma(Sy, wfm8.rearrange("p (f q) -> p f q", f=NF)[
                             :, :, j * 128:(j + 1) * 128],
                         wb8.rearrange("p (f q) -> p f q", q=128),
                         [e2] + (pv_w8 if j == 0 else []), transpose=True,
                         flow="w8t")
                evw.append(e3)
            pv_w8 = []
            for tt in range(NT):
                bank = PS8[tt % 2]
                dep0 = evw + ev_xn8 + \
                    ([ev_qkv[-2]] if len(ev_qkv) >= 2 else [])
                for kc in range(NF):
                    e_mm = mm_(bank[:],
                               xn_fm8[:, kc * T + tt * 128:
                                      kc * T + (tt + 1) * 128],
                               wfm8[:, kc * 512:(kc + 1) * 512],
                               start=(kc == 0), stop=(kc == NF - 1),
                               deps=dep0 if kc == 0 else ())
                e_ev = cp_(V, qkvt[:, tt * 3072 + oc * 512:
                                   tt * 3072 + (oc + 1) * 512],
                           bank[:], [e_mm])
                ev_qkv.append(e_ev)
                pv_w8.append(e_ev)
        ph8a.close()
        if tap("qkv", (T, 3072), BF16) is not None:
            for t in range(NT):
                dma(Sc, rtile(tap_t["qkv"])[t],
                    qkvt[:, t * 3072:(t + 1) * 3072], ev_qkv, flow="tap")

        ev_att2 = []
        pv_gp = None
        for tt in range(4, NT):
            src = qkvt[:, tt * 3072:(tt + 1) * 3072]
            s0 = qkvt[:, (tt - 4) * 3072:(tt - 3) * 3072]
            q1v3 = src.rearrange("p (g c) -> p g c", g=4)[:, :, 0:512] \
                .rearrange("p g (i q) -> p g i q", q=128)
            k0v = s0.rearrange("p (g c) -> p g c", g=4)[:, :, 512:640]
            k1v = src.rearrange("p (g c) -> p g c", g=4)[:, :, 512:640]
            for gp2 in range(2):
                e_m0 = p.op(Gp, lambda e, qv=q1v3[:, gp2 * 2:gp2 * 2 + 2],
                            kk=k0v[:, gp2 * 2:gp2 * 2 + 2],
                            o=a8.rearrange("p (g i q) -> p g i q", g=2, q=128):
                            e.tensor_tensor(
                                out=o, in0=qv,
                                in1=kk.unsqueeze(2).broadcast_to(
                                    (128, 2, 4, 128)),
                                op=ALU.mult), ev_qkv + [pv_gp])
                e_r0 = red_(c8[:, 16 + gp2 * 8:24 + gp2 * 8],
                            a8.rearrange("p (x q) -> p x q", q=128),
                            ALU.add, [e_m0])
                pv_gp = e_r0
            for gp2 in range(2):
                e_m1 = p.op(Gp, lambda e, qv=q1v3[:, gp2 * 2:gp2 * 2 + 2],
                            kk=k1v[:, gp2 * 2:gp2 * 2 + 2],
                            o=a8.rearrange("p (g i q) -> p g i q", g=2, q=128):
                            e.tensor_tensor(
                                out=o, in0=qv,
                                in1=kk.unsqueeze(2).broadcast_to(
                                    (128, 2, 4, 128)),
                                op=ALU.mult), [pv_gp])
                e_r1 = red_(c8[:, 32 + gp2 * 8:40 + gp2 * 8],
                            a8.rearrange("p (x q) -> p x q", q=128),
                            ALU.add, [e_m1])
                pv_gp = e_r1
            e_d = tt_(V, c8[:, 48:64], c8[:, 32:48], c8[:, 16:32],
                      ALU.subtract, [pv_gp])
            e_e = act_(c8[:, 48:64], c8[:, 48:64], AF.Exp, scale=cc(-SCL),
                       deps=[e_d, ccev(-SCL)])
            e_a = ts_(V, c8[:, 48:64], c8[:, 48:64], 1.0, ALU.add, deps=[e_e])
            e_p1 = recip_(c8[:, 48:64], c8[:, 48:64], [e_a])
            for g in range(4):
                e_t = tt_(V, a8[:, g * 128:(g + 1) * 128],
                          src[:, g * 768 + 640:g * 768 + 768],
                          s0[:, g * 768 + 640:g * 768 + 768], ALU.subtract,
                          [e_p1])
                for qi in range(4):
                    h_ = g * 4 + qi
                    dstc = src[:, g * 768 + qi * 128:g * 768 + qi * 128 + 128]
                    e_sc = ts_(V, a8[:, 512 + h_ % 4 * 128:
                                     512 + (h_ % 4 + 1) * 128],
                               a8[:, g * 128:(g + 1) * 128],
                               c8[:, 48 + h_:49 + h_], ALU.mult, deps=[e_t])
                    e_o = tt_(V, dstc,
                              a8[:, 512 + h_ % 4 * 128:
                                 512 + (h_ % 4 + 1) * 128],
                              s0[:, g * 768 + 640:g * 768 + 768], ALU.add,
                              [e_sc])
                    ev_att2.append(e_o)
            pv_gp = e_o
        ev_att0 = []
        for tt in range(4):
            src = qkvt[:, tt * 3072:(tt + 1) * 3072]
            for g in range(4):
                for qi in range(4):
                    hcol = g * 768 + qi * 128
                    ev_att0.append(cp_(V, src[:, hcol:hcol + 128],
                                       src[:, g * 768 + 640:g * 768 + 768],
                                       ev_qkv + [pv_gp]))
        ev_att = ev_att2 + ev_att0
        if tap("attn1", (T, D), BF16) is not None:
            at = tap_t["attn1"]
            for t in range(NT):
                for g in range(4):
                    dma(Sc, rtile(at)[t][:, g * 512:(g + 1) * 512],
                        qkvt[:, t * 3072 + g * 768:t * 3072 + g * 768 + 512],
                        ev_att, flow="tap")

        ev_pj = []
        pv_cast8 = None
        pv_w8 = []
        for oc in range(4):
            evw = []
            for j in range(4):
                buf = wl8[:, (j % 2) * D:(j % 2) * D + D]
                e1 = dma(Sy, buf, rtile(wproj_d)[oc * 4 + j],
                         [evw[-2] if len(evw) >= 2 else pv_cast8],
                         flow="w8" + str(j % 2))
                e2 = cp_(V, wb8[:], buf, [e1, evw[-1] if evw else pv_cast8])
                pv_cast8 = e2
                e3 = dma(Sy, wfm8.rearrange("p (f q) -> p f q", f=NF)[
                             :, :, j * 128:(j + 1) * 128],
                         wb8.rearrange("p (f q) -> p f q", q=128),
                         [e2] + (pv_w8 if j == 0 else []) +
                         (ev_qkv if oc == 0 and j == 0 else []),
                         transpose=True, flow="w8t")
                evw.append(e3)
            pv_w8 = []
            for tt in range(NT):
                e_tr2 = None
                for g in range(4):
                    attv = qkvt[:, tt * 3072 + g * 768:
                                tt * 3072 + g * 768 + 512].rearrange(
                        "p (i q) -> p i q", q=128)
                    e_tr2 = dma(Sy, afm8[:, g * 512:(g + 1) * 512].rearrange(
                                    "p (i q) -> p i q", q=128),
                                attv,
                                ev_att + [ev_pj[-1] if ev_pj else None],
                                transpose=True, flow="af")
                bank = PS8[2 + tt % 2]
                dep0 = evw + [e_tr2] + \
                    ([ev_pj[-2]] if len(ev_pj) >= 2 else [])
                for kc in range(NF):
                    e_mm = mm_(bank[:],
                               afm8[:, kc * 128:(kc + 1) * 128],
                               wfm8[:, kc * 512:(kc + 1) * 512],
                               start=(kc == 0), stop=(kc == NF - 1),
                               deps=dep0 if kc == 0 else ())
                e_ad = tt_(V, xres[:, tt * D + oc * 512:
                                   tt * D + (oc + 1) * 512],
                           xres[:, tt * D + oc * 512:
                                tt * D + (oc + 1) * 512],
                           bank[:], ALU.add, [e_mm, xres_last.get((tt, oc))])
                xres_last[(tt, oc)] = e_ad
                ev_pj.append(e_ad)
                pv_w8.append(e_ad)
        ph8.close()
        p.barrier()

        ev_st = []
        y_t = rtile(y_out)
        for t in range(NT):
            ev_st.append(dma(Sy, y_t[t], xres[:, t * D:(t + 1) * D], flow="yst"))
        all_done = list(ev_st) + [(f[0], f[1]) for f in p.flows.values()]
        p.op(Sy, lambda e: e.nop(), all_done, inc=1)
        p.op(Sc, lambda e: e.nop(), all_done, inc=1)

        with nc.Block() as block:
            p.finalize(block)

    return nc, sorted(tap_t.keys())


_cache = {}


def kernel(L0_attn_x, L0_attn_Wq, L0_attn_Wk, L0_attn_Wv, L0_attn_Wg,
           L0_attn_Wo, L0_ffn_W_g, L0_ffn_W_u, L0_ffn_W_d,
           L1_attn_weight_qkv, L1_attn_weight_proj, L1_ffn_w_gate,
           L1_ffn_w_up, L1_ffn_w_down, _taps=(), _return_raw=False):
    x = np.ascontiguousarray(np.asarray(L0_attn_x, dtype=np.float32))
    f32 = lambda a: np.ascontiguousarray(np.asarray(a, dtype=np.float32))
    wq, wk, wv, wg, wo = map(f32, (L0_attn_Wq, L0_attn_Wk, L0_attn_Wv,
                                   L0_attn_Wg, L0_attn_Wo))
    wgf, wuf, wdf = map(f32, (L0_ffn_W_g, L0_ffn_W_u, L0_ffn_W_d))
    wqkv, wproj = map(f32, (L1_attn_weight_qkv, L1_attn_weight_proj))
    w1g, w1u, w1d = map(f32, (L1_ffn_w_gate, L1_ffn_w_up, L1_ffn_w_down))

    tri = np.triu(np.ones((128, 128), np.float32)).astype(ml_dtypes.bfloat16)
    ident = np.eye(128, dtype=ml_dtypes.bfloat16)

    key = tuple(_taps)
    if key not in _cache:
        _cache[key] = build_program(_taps)
    nc, tap_names = _cache[key]

    in_maps = []
    for c in range(NCORES):
        xs = np.ascontiguousarray(x[:, c * SL:(c + 1) * SL, :].reshape(T, D))
        fm = np.zeros((128, NCORES), np.float32)
        fm[:, :c] = 1.0
        in_maps.append({
            "x": xs, "wq": wq, "wk": wk, "wv": wv, "wg": wg, "wo": wo,
            "wgf": wgf, "wuf": wuf, "wdf": wdf, "wqkv": wqkv, "wproj": wproj,
            "w1g_sh": np.ascontiguousarray(
                w1g[c * (HID // NCORES):(c + 1) * (HID // NCORES)]),
            "w1u_sh": np.ascontiguousarray(
                w1u[c * (HID // NCORES):(c + 1) * (HID // NCORES)]),
            "w1d_sh": np.ascontiguousarray(
                w1d[c * (D // NCORES):(c + 1) * (D // NCORES)]),
            "tri": tri, "ident": ident,
            "fmask": np.ascontiguousarray(fm),
            "fmaski": np.ascontiguousarray(1.0 - fm),
        })

    res = run_bass_kernel_spmd(nc, in_maps, core_ids=list(range(NCORES)))
    if _return_raw:
        return res
    out = np.empty((B, S, D), np.float32)
    for c in range(NCORES):
        out[:, c * SL:(c + 1) * SL, :] = np.asarray(
            res.results[c]["y"]).reshape(B, SL, D)
    return out


# version 08
# speedup vs baseline: 1.6950x; 1.1597x over previous
import math
import os
import numpy as np
import ml_dtypes
from contextlib import ExitStack

import concourse.bass as bass
import concourse.mybir as mybir
from concourse.bass_utils import run_bass_kernel_spmd

F32 = mybir.dt.float32
BF16 = mybir.dt.bfloat16
I32 = mybir.dt.int32
ALU = mybir.AluOpType
AF = mybir.ActivationFunctionType
AX = mybir.AxisListType

NCORES = 8
B = 2
S = 4096
D = 2048
SL = S // NCORES
T = B * SL
NT = T // 128
NF = D // 128
H = 16
DK = 128
HID = 8192
GN16 = 1.0 / 16.0
SCL = 1.0 / math.sqrt(DK)
LN2 = math.log(2.0)

H_LOOP = int(os.environ.get("K_HLOOP", "16"))
HC_LOOP = int(os.environ.get("K_HCLOOP", "16"))
RACEDET = os.environ.get("K_RACEDET", "0") == "1"

ENGS = ("sync", "scalar", "vector", "tensor", "gpsimd")
V, Sc, Te, Gp, Sy = "vector", "scalar", "tensor", "gpsimd", "sync"


class Stream:
    def __init__(self, name, sem):
        self.name = name
        self.sem = sem
        self.count = 0
        self.items = []
        self.waited = {}

    def covered(self, ev):
        sem, val = ev
        return self.waited.get(id(sem), -1) >= val


class Prog:
    def __init__(self, nc, stack):
        self.nc = nc
        self.stack = stack
        self.streams = {}
        for e in ENGS:
            sem = stack.enter_context(nc.semaphore(f"sem_{e}"))
            self.streams[e] = Stream(e, sem)
        self.cc_sem = stack.enter_context(nc.semaphore("sem_cc"))
        self.cc_count = 0
        self.flows = {}

    def flow(self, name):
        if name not in self.flows:
            sem = self.stack.enter_context(self.nc.semaphore("f_" + name))
            self.flows[name] = [sem, 0]
        return self.flows[name]

    def op(self, eng, emit, deps=(), inc=1, inc_sem=None):
        st = self.streams[eng]
        need = {}
        for ev in deps:
            if ev is None:
                continue
            if not st.covered(ev):
                sem, val = ev
                k = id(sem)
                if k not in need or need[k][1] < val:
                    need[k] = (sem, val)
        waits = []
        for sem, val in need.values():
            waits.append((sem, val))
            st.waited[id(sem)] = max(st.waited.get(id(sem), -1), val)
        if inc_sem is None:
            st.count += inc
            st.items.append((waits, emit, (st.sem, inc)))
            return (st.sem, st.count)
        st.items.append((waits, emit, (inc_sem[0], inc)))
        return (inc_sem[0], inc_sem[1])

    def collective(self, kind, alu, ins, outs, deps=()):
        self.cc_count += 1

        def emit(e):
            return e.collective_compute(kind, alu,
                                        replica_groups=[list(range(NCORES))],
                                        ins=ins, outs=outs)
        return self.op(Gp, emit, deps, inc=1, inc_sem=(self.cc_sem, self.cc_count))

    def barrier(self, include_cc=True):
        evs = [(st.sem, st.count) for st in self.streams.values() if st.count > 0]
        evs += [(f[0], f[1]) for f in self.flows.values() if f[1] > 0]
        if include_cc and self.cc_count:
            evs.append((self.cc_sem, self.cc_count))
        for st in self.streams.values():
            for ev in evs:
                if not st.covered(ev):
                    sem, val = ev
                    st.items.append(([(sem, val)], None, None))
                    st.waited[id(sem)] = max(st.waited.get(id(sem), -1), val)

    def finalize(self, block):
        def mk(st):
            def body(eng):
                for waits, emit, inc in st.items:
                    for sem, val in waits:
                        eng.wait_ge(sem, val)
                    if emit is not None:
                        ins = emit(eng)
                        if inc is not None:
                            ins.then_inc(inc[0], inc[1])
            return body
        for e in ENGS:
            getattr(block, e)(mk(self.streams[e]))


def build_program(taps=()):
    nc = bass.Bass("TRN2", target_bir_lowering=False,
                   detect_race_conditions=RACEDET)

    def din(name, shape, dt=F32):
        return nc.dram_tensor(name, list(shape), dt, kind="ExternalInput").ap()

    x_in = din("x", (T, D))
    wq_d = din("wq", (D, D)); wk_d = din("wk", (D, D)); wv_d = din("wv", (D, D))
    wg_d = din("wg", (D, D)); wo_d = din("wo", (D, D))
    w0g_d = din("w0g_sh", (HID // NCORES, D))
    w0u_d = din("w0u_sh", (HID // NCORES, D))
    w0d_d = din("w0d_sh", (D, HID // NCORES))
    wqkv_d = din("wqkv", (3072, D)); wproj_d = din("wproj", (D, D))
    w1g_d = din("w1g_sh", (HID // NCORES, D))
    w1u_d = din("w1u_sh", (HID // NCORES, D))
    w1d_d = din("w1d_sh", (D // NCORES, HID))
    tri_d = din("tri", (128, 128), BF16)
    ident_d = din("ident", (128, 128), BF16)
    fmask_d = din("fmask", (128, NCORES))
    fmaski_d = din("fmaski", (128, NCORES))

    y_out = nc.dram_tensor("y", [T, D], F32, kind="ExternalOutput").ap()
    tap_t = {}

    def tap(name, shape, dt):
        if name in taps:
            tap_t[name] = nc.dram_tensor("tap_" + name, list(shape), dt,
                                         kind="ExternalOutput").ap()
            return tap_t[name]
        return None

    ccg_in = nc.dram_tensor("ccg_in", [2 * H, 128, 129], F32).ap()
    ccg_out = nc.dram_tensor("ccg_out", [NCORES, 2 * H, 128, 129], F32).ap()
    q0g_in = nc.dram_tensor("q0g_in", [D, HID // NCORES], BF16).ap()
    q0g_out = nc.dram_tensor("q0g_out", [NCORES * D, HID // NCORES], BF16).ap()
    q0u_in = nc.dram_tensor("q0u_in", [D, HID // NCORES], BF16).ap()
    q0u_out = nc.dram_tensor("q0u_out", [NCORES * D, HID // NCORES], BF16).ap()
    q0d_in = nc.dram_tensor("q0d_in", [HID // NCORES, D], BF16).ap()
    q0d_out = nc.dram_tensor("q0d_out", [HID, D], BF16).ap()
    q1g_in = nc.dram_tensor("q1g_in", [HID // NCORES, D], BF16).ap()
    q1g_out = nc.dram_tensor("q1g_out", [HID, D], BF16).ap()
    q1u_in = nc.dram_tensor("q1u_in", [HID // NCORES, D], BF16).ap()
    q1u_out = nc.dram_tensor("q1u_out", [HID, D], BF16).ap()
    q1d_in = nc.dram_tensor("q1d_in", [D // NCORES, HID], BF16).ap()
    q1d_out = nc.dram_tensor("q1d_out", [D, HID], BF16).ap()
    sc_q = nc.dram_tensor("sc_q", [H, DK, T], BF16).ap()
    sc_o = nc.dram_tensor("sc_o", [H, DK, T], BF16).ap()
    sc_xn = nc.dram_tensor("sc_xn", [128, NF * T], BF16).ap()
    sc_xq = nc.dram_tensor("sc_xq", [128, NF * T], BF16).ap()

    gstack = ExitStack()
    with gstack:
        p = Prog(nc, gstack)

        def sb(st, *a, **k):
            return st.enter_context(nc.sbuf_tensor(*a, **k))

        def ps_(st, *a, **k):
            return st.enter_context(nc.psum_tensor(*a, **k))

        xres = sb(gstack, "xres", [128, NT * D], F32)
        trit = sb(gstack, "trit", [128, 128], BF16)
        idt = sb(gstack, "idt", [128, 128], BF16)
        cols = sb(gstack, "cols", [128, 256], F32)
        sinv_cols = sb(gstack, "sinv_cols", [128, NT], F32)
        acols = sb(gstack, "acols", [128, H * 8], F32)
        fmsk = sb(gstack, "fmsk", [128, NCORES], F32)
        fmski = sb(gstack, "fmski", [128, NCORES], F32)
        zcol = sb(gstack, "zcol", [128, 1], F32)
        mxc = sb(gstack, "mxc", [128, 3 * 64], F32)
        xres_last = {}
        ccol = sb(gstack, "ccol", [128, 16], F32)
        CONSTS = {}

        def ts_(eng, out, in0, s1, op0, s2=None, op1=None, deps=(), accum=None):
            if op1 is None:
                return p.op(eng, lambda e: e.tensor_scalar(
                    out=out, in0=in0, scalar1=s1, scalar2=None, op0=op0,
                    accum_out=accum), deps)
            return p.op(eng, lambda e: e.tensor_scalar(
                out=out, in0=in0, scalar1=s1, scalar2=s2, op0=op0, op1=op1,
                accum_out=accum), deps)

        def tt_(eng, out, in0, in1, alu, deps=()):
            return p.op(eng, lambda e: e.tensor_tensor(out=out, in0=in0,
                                                       in1=in1, op=alu), deps)

        def stt_(eng, out, in0, scalar, in1, op0, op1, deps=()):
            return p.op(eng, lambda e: e.scalar_tensor_tensor(
                out=out, in0=in0, scalar=scalar, in1=in1, op0=op0, op1=op1),
                deps)

        def act_(out, in_, func, bias=0.0, scale=1.0, accum=None, deps=()):
            return p.op(Sc, lambda e: e.activation(out=out, in_=in_, func=func,
                                                   bias=bias, scale=scale,
                                                   accum_out=accum), deps)

        def cp_(eng, out, in_, deps=()):
            return p.op(eng, lambda e: e.tensor_copy(out=out, in_=in_), deps)

        def red_(out, in_, alu, deps=(), absval=None, negate=None):
            return p.op(V, lambda e: e.tensor_reduce(
                out=out, in_=in_, axis=AX.X, op=alu,
                apply_absolute_value=absval, negate=negate), deps)

        def mm_(out, lhsT, rhs, start, stop, deps=()):
            return p.op(Te, lambda e: e.matmul(out=out, lhsT=lhsT, rhs=rhs,
                                               start=start, stop=stop), deps)

        def tr_(out, in_, deps=()):
            return p.op(Te, lambda e: e.transpose(out=out, in_=in_,
                                                  identity=idt[:]), deps)

        def scan_(out, d0, deps=()):
            return p.op(V, lambda e: e.tensor_tensor_scan(
                out=out, data0=d0, data1=d0, initial=0.0, op0=ALU.add,
                op1=ALU.bypass), deps)

        def ms_(eng, ap_, val, deps=()):
            return p.op(eng, lambda e: e.memset(ap_, val), deps)

        def recip_(out, in_, deps=()):
            return p.op(V, lambda e: e.reciprocal(out=out, in_=in_), deps)

        def dma(eng, out, in_, deps=(), transpose=False, flow="ld"):
            f = p.flow(flow)
            f[1] += 16
            return p.op(eng, lambda e: e.dma_start(out=out, in_=in_,
                                                   transpose=transpose), deps,
                        inc=16, inc_sem=(f[0], f[1]))

        def rtile(apx):
            return apx.rearrange("(t p) c -> t p c", p=128)

        def cc(v):
            if v not in CONSTS:
                i = len(CONSTS)
                ev = ms_(V, ccol[:, i:i + 1], float(v))
                CONSTS[v] = (ccol[:, i:i + 1], ev)
            return CONSTS[v][0]

        def ccev(v):
            cc(v)
            return CONSTS[v][1]

        def repl32(dst_flat, src_cols, nb, deps):
            d3 = dst_flat.rearrange("p (b e) -> p b e", e=32)
            ev = cp_(V, d3[:, :, 0:1],
                     src_cols.rearrange("p (b o) -> p b o", o=1), deps)
            k = 1
            while k < 32:
                kk = min(k, 32 - k)
                ev = cp_(V, d3[:, :, k:k + kk], d3[:, :, 0:kk], [ev])
                k += kk
            return ev

        def mx_scales(src_blocks, nb, iw, deps):
            amx = mxc[:, 0:nb]
            lg = mxc[:, 64:64 + nb]
            fg = mxc[:, 128:128 + nb]
            e1 = red_(amx, src_blocks, ALU.max, deps, absval=True)
            e2 = ts_(V, amx, amx, 1e-30, ALU.max, deps=[e1])
            e3 = act_(lg, amx, AF.Ln, deps=[e2])
            e4 = ts_(V, lg, lg, 1.0 / LN2, ALU.mult, 1e-9, ALU.add, deps=[e3])
            e5 = cp_(V, iw[:, 0:nb], lg, [e4])
            e6 = stt_(V, fg, iw[:, 0:nb], 0.0, lg, ALU.bypass, ALU.is_gt, [e5])
            e7 = tt_(V, lg, iw[:, 0:nb], fg, ALU.subtract, [e6])
            e8 = ts_(V, lg, lg, -2.0, ALU.add, deps=[e7])
            e9 = ts_(V, lg, lg, 127.0, ALU.min, -127.0, ALU.max, deps=[e8])
            er = act_(fg, lg, AF.Exp, scale=cc(-LN2), deps=[e9, ccev(-LN2)])
            ers = act_(lg, lg, AF.Exp, scale=cc(LN2), deps=[er, ccev(LN2)])
            return fg, lg, er, ers

        def mx_quant(in_f32, out_bf, W, base_inv, maxn, scr, deps,
                     out_free=(), in_more=()):
            nb = W // 32
            fg, lg, er, ers = mx_scales(
                in_f32.rearrange("p (b e) -> p b e", e=32), nb, scr["I"],
                list(deps))
            erp = repl32(scr["R"][:, 0:W], fg, nb, [er])
            ey = tt_(V, scr["A"][:, 0:W], in_f32, scr["R"][:, 0:W], ALU.mult,
                     [erp] + list(deps) + list(in_more))
            ersp = repl32(scr["R2"][:, 0:W], lg, nb, [ers])
            ea = act_(scr["S1"][:, 0:W], scr["A"][:, 0:W], AF.Abs, deps=[ey])
            es1 = ts_(V, scr["S2"][:, 0:W], scr["S1"][:, 0:W], 2.0, ALU.is_ge,
                      deps=[ea])
            es2 = ts_(V, scr["S1"][:, 0:W], scr["S1"][:, 0:W], 4.0, ALU.is_ge,
                      deps=[ea, es1])
            eu1 = ts_(V, out_bf[:, 0:W], scr["S2"][:, 0:W], -0.5, ALU.mult,
                      1.0, ALU.add, deps=[es1] + list(out_free))
            ev2 = tt_(V, scr["A"][:, 0:W], scr["A"][:, 0:W], out_bf[:, 0:W],
                      ALU.mult, [eu1])
            eu2 = ts_(V, out_bf[:, 0:W], scr["S1"][:, 0:W], -0.5, ALU.mult,
                      1.0, ALU.add, deps=[es2, ev2])
            ev3 = tt_(V, scr["A"][:, 0:W], scr["A"][:, 0:W], out_bf[:, 0:W],
                      ALU.mult, [eu2])
            ev4 = ts_(V, scr["A"][:, 0:W], scr["A"][:, 0:W], base_inv, ALU.mult,
                      0.5, ALU.add, deps=[ev3])
            eI = cp_(V, scr["I"][:, 0:W], scr["A"][:, 0:W], [ev4])
            eG = stt_(V, scr["R"][:, 0:W], scr["I"][:, 0:W], 0.0,
                      scr["A"][:, 0:W], ALU.bypass, ALU.is_gt, [eI])
            eK = tt_(V, out_bf[:, 0:W], scr["I"][:, 0:W], scr["R"][:, 0:W],
                     ALU.subtract, [eG])
            ew1 = ts_(V, scr["S2"][:, 0:W], scr["S2"][:, 0:W], 1.0, ALU.add,
                      deps=[eK])
            ew2 = stt_(V, scr["S1"][:, 0:W], scr["S1"][:, 0:W], 1.0,
                       scr["S2"][:, 0:W], ALU.add, ALU.mult, [ew1])
            ew3 = ts_(V, scr["S1"][:, 0:W], scr["S1"][:, 0:W], 1.0 / base_inv,
                      ALU.mult, deps=[ew2])
            eq = tt_(V, out_bf[:, 0:W], out_bf[:, 0:W], scr["S1"][:, 0:W],
                     ALU.mult, [ew3])
            ecl = ts_(V, out_bf[:, 0:W], out_bf[:, 0:W], maxn, ALU.min, -maxn,
                      ALU.max, deps=[eq])
            eo = tt_(V, out_bf[:, 0:W], out_bf[:, 0:W], scr["R2"][:, 0:W],
                     ALU.mult, [ecl, ersp])
            return eo, ey

        def mx_quant_gp(in_f32, out_bf, W, base_inv, maxn, scr, deps,
                        out_free=()):
            nb = W // 32
            fg, lg, er, ers = mx_scales(
                in_f32.rearrange("p (b e) -> p b e", e=32), nb, scr["I"],
                list(deps))
            erp = repl32g(scr["R"][:, 0:W], fg, nb, [er])
            ey = tt_(Gp, scr["A"][:, 0:W], in_f32, scr["R"][:, 0:W], ALU.mult,
                     [erp] + list(deps))
            ersp = repl32g(scr["R2"][:, 0:W], lg, nb, [ers])
            esq = tt_(Gp, scr["R"][:, 0:W], scr["A"][:, 0:W],
                      scr["A"][:, 0:W], ALU.mult, [ey])
            es1 = ts_(Gp, scr["S2"][:, 0:W], scr["R"][:, 0:W], 4.0, ALU.is_ge,
                      deps=[esq])
            es2 = ts_(Gp, scr["S1"][:, 0:W], scr["R"][:, 0:W], 16.0, ALU.is_ge,
                      deps=[esq, es1])
            eu1 = ts_(Gp, out_bf[:, 0:W], scr["S2"][:, 0:W], -0.5, ALU.mult,
                      1.0, ALU.add, deps=[es1] + list(out_free))
            ev2 = tt_(Gp, scr["A"][:, 0:W], scr["A"][:, 0:W], out_bf[:, 0:W],
                      ALU.mult, [eu1])
            eu2 = ts_(Gp, out_bf[:, 0:W], scr["S1"][:, 0:W], -0.5, ALU.mult,
                      1.0, ALU.add, deps=[es2, ev2])
            ev3 = tt_(Gp, scr["A"][:, 0:W], scr["A"][:, 0:W], out_bf[:, 0:W],
                      ALU.mult, [eu2])
            ev4 = ts_(Gp, scr["A"][:, 0:W], scr["A"][:, 0:W], base_inv,
                      ALU.mult, 0.5, ALU.add, deps=[ev3])
            eI = cp_(Gp, scr["I"][:, 0:W], scr["A"][:, 0:W], [ev4])
            eF = cp_(Gp, scr["R"][:, 0:W], scr["I"][:, 0:W], [eI])
            eD = tt_(Gp, scr["A"][:, 0:W], scr["R"][:, 0:W], scr["A"][:, 0:W],
                     ALU.subtract, [eF])
            eG = ts_(Gp, scr["A"][:, 0:W], scr["A"][:, 0:W], 0.0, ALU.is_gt,
                     deps=[eD])
            eK = tt_(Gp, out_bf[:, 0:W], scr["R"][:, 0:W], scr["A"][:, 0:W],
                     ALU.subtract, [eG])
            ew1 = ts_(Gp, scr["S2"][:, 0:W], scr["S2"][:, 0:W], 1.0, ALU.add,
                      deps=[eK])
            ea1 = ts_(Gp, scr["S1"][:, 0:W], scr["S1"][:, 0:W], 1.0, ALU.add,
                      deps=[eK])
            ew2 = tt_(Gp, scr["S1"][:, 0:W], scr["S1"][:, 0:W],
                      scr["S2"][:, 0:W], ALU.mult, [ew1, ea1])
            ew3 = ts_(Gp, scr["S1"][:, 0:W], scr["S1"][:, 0:W], 1.0 / base_inv,
                      ALU.mult, deps=[ew2])
            eq = tt_(Gp, out_bf[:, 0:W], out_bf[:, 0:W], scr["S1"][:, 0:W],
                     ALU.mult, [ew3])
            ecl = ts_(Gp, out_bf[:, 0:W], out_bf[:, 0:W], maxn, ALU.min, -maxn,
                      ALU.max, deps=[eq])
            eo = tt_(Gp, out_bf[:, 0:W], out_bf[:, 0:W], scr["R2"][:, 0:W],
                     ALU.mult, [ecl, ersp])
            return eo, ey

        def repl32g(dst_flat, src_cols, nb, deps):
            d3 = dst_flat.rearrange("p (b e) -> p b e", e=32)
            ev = cp_(Gp, d3[:, :, 0:1],
                     src_cols.rearrange("p (b o) -> p b o", o=1), deps)
            k = 1
            while k < 32:
                kk = min(k, 32 - k)
                ev = cp_(Gp, d3[:, :, k:k + kk], d3[:, :, 0:kk], [ev])
                k += kk
            return ev

        ev_tri = dma(Sy, trit[:], tri_d, flow="cst")
        ev_id = dma(Sy, idt[:], ident_d, flow="cst")
        ev_fm = dma(Sy, fmsk[:], fmask_d, flow="cst")
        ev_fmi = dma(Sy, fmski[:], fmaski_d, flow="cst")
        ev_cst = ev_fmi
        ev_z = ms_(V, zcol[:], 0.0)
        evx = []
        x_t = rtile(x_in)
        for t in range(NT):
            evx.append(dma(Sy, xres[:, t * D:(t + 1) * D], x_t[t]))
        ev_xall = evx[-1]

        ph12 = ExitStack()
        xn_fm = sb(ph12, "xn_fm", [128, NF * T], BF16, is_transpose_dest=True)
        pool = ExitStack()
        pA = sb(pool, "pA", [128, D], F32)
        pB = sb(pool, "pB", [128, D], F32)
        pC = sb(pool, "pC", [128, D], F32)
        pI = sb(pool, "pI", [128, D], I32)
        pBF = sb(pool, "pBF", [128, D], BF16)
        pBF2 = sb(pool, "pBF2", [128, D], BF16)
        pR2 = sb(pool, "pR2", [128, D], BF16)
        pO = sb(pool, "pO", [128, D], BF16)
        SCR = {"A": pA, "R": pC, "I": pI, "S1": pBF, "S2": pBF2, "R2": pR2}
        pL0 = [sb(pool, "pL0a", [128, D], F32),
               sb(pool, "pL0b", [128, D], F32)]
        pS0 = sb(pool, "pS0", [128, D], BF16)
        pTD = sb(pool, "pTD", [128, D], BF16, is_transpose_dest=True)

        ph1 = ExitStack()
        xq_fm1 = sb(ph1, "xq_fm1", [128, NF * T], BF16, is_transpose_dest=True)

        def ln_cols(src_tile, base, eps, deps, dump, ddep):
            c = lambda i: cols[:, base + i:base + i + 1]
            e1 = red_(c(0), src_tile, ALU.add, deps)
            e2 = act_(dump[:], src_tile, AF.Square, accum=c(1),
                      deps=list(deps) + [ddep])
            e3 = ts_(V, c(2), c(0), 1.0 / D, ALU.mult, deps=[e1])
            e4 = ts_(V, c(3), c(1), 1.0 / D, ALU.mult, deps=[e2])
            e5 = tt_(V, c(4), c(2), c(2), ALU.mult, [e3])
            e6 = tt_(V, c(4), c(3), c(4), ALU.subtract, [e4, e5])
            e7 = act_(c(5), c(4), AF.Sqrt, bias=cc(eps), deps=[e6, ccev(eps)])
            e8 = recip_(c(5), c(5), [e7])
            e9 = stt_(V, c(6), c(2), -1.0, c(5), ALU.mult, ALU.mult, [e8])
            return c(5), c(6), e9

        ev_xnfm = []
        ev_xqfm1 = []
        pv_saf = pv_g = pv_tq = pv_bf = None
        for t in range(NT):
            xt = xres[:, t * D:(t + 1) * D]
            base = t * 24
            rc, nbc, e9 = ln_cols(xt, base, 1e-5, [ev_xall], pBF2, pv_tq)
            e_xn = act_(pA[:], xt, AF.Identity, bias=nbc, scale=rc,
                        deps=[e9, ev_xall, pv_saf])
            e_bf = cp_(V, pBF[:], pA[:], [e_xn, pv_tq])
            e_tp = dma(Sy, xn_fm.rearrange("p (f w) -> p f w", f=NF)[
                           :, :, t * 128:(t + 1) * 128],
                       pBF.rearrange("p (f q) -> p f q", q=128), [e_bf],
                       transpose=True, flow="t1")
            ev_xnfm.append(e_tp)
            rc2, nbc2, e9b = ln_cols(pA[:], base + 8, 1e-8, [e_xn], pBF2, None)
            pv_saf = e9b
            e_xnr = act_(pB[:], pA[:], AF.Identity, bias=nbc2, scale=rc2,
                         deps=[e9b, e_xn, pv_g])
            cmax = cols[:, base + 16:base + 17]
            csc = cols[:, base + 17:base + 18]
            e_mx = red_(cmax, pB[:], ALU.max, [e_xnr], absval=True)
            e_r = recip_(csc, cmax, [e_mx])
            e_s = ts_(V, csc, csc, 127.0, ALU.mult, deps=[e_r])
            e_si = recip_(sinv_cols[:, t:t + 1], csc, [e_s])
            e_v = ts_(V, pB[:], pB[:], csc, ALU.mult, 0.5, ALU.add,
                      deps=[e_s, e_xnr])
            e_i = cp_(V, pI[:], pB[:], [e_v])
            e_g = stt_(V, pC[:], pI[:], 0.0, pB[:], ALU.bypass, ALU.is_gt,
                       [e_i])
            pv_g = e_g
            e_k = tt_(V, pBF[:], pI[:], pC[:], ALU.subtract, [e_g, e_tp])
            e_cl = ts_(V, pBF[:], pBF[:], 127.0, ALU.min, -128.0, ALU.max,
                       deps=[e_k])
            e_tq = dma(Sy, xq_fm1.rearrange("p (f w) -> p f w", f=NF)[
                           :, :, t * 128:(t + 1) * 128],
                       pBF.rearrange("p (f q) -> p f q", q=128), [e_cl],
                       transpose=True, flow="t2")
            pv_tq = e_tq
            ev_xqfm1.append(e_tq)
        e_spxq = dma(Sc, sc_xq, xq_fm1[:], ev_xqfm1, flow="sp1")
        ph1.close()

        if tap("xnfm", (128, NF * T), BF16) is not None:
            dma(Sc, tap_t["xnfm"], xn_fm[:], ev_xnfm, flow="tap")
        if tap("sinv", (128, NT), F32) is not None:
            dma(Sc, tap_t["sinv"], sinv_cols[:], [e_si], flow="tap")

        ev_w0 = {}
        sg_prev = [None, None]
        pv_t0 = pv_st0 = None
        for key, src, dstq, outw in (("g", w0g_d, q0g_in, q0g_out),
                                     ("u", w0u_d, q0u_in, q0u_out),
                                     ("d", w0d_d, q0d_in, q0d_out)):
            wc = D if key != "d" else HID // NCORES
            nb = wc // 128
            nt = (HID // NCORES // 128) if key != "d" else (D // 128)
            stores = []
            for t in range(nt):
                buf = pL0[t % 2]
                el = dma(Sy, buf[:, 0:wc], rtile(src)[t], [sg_prev[t % 2]],
                         flow="w0l" + str(t % 2))
                e_sg = act_(pS0[:, 0:wc], buf[:, 0:wc], AF.Sign,
                            deps=[el, pv_t0])
                sg_prev[t % 2] = e_sg
                pv_t0 = dma(Sc, pTD.rearrange("p (f w) -> p f w", w=128)[
                                :, 0:nb, :],
                            pS0[:, 0:wc].rearrange("p (f q) -> p f q", q=128),
                            [e_sg, pv_st0], transpose=True, flow="w0t")
                pv_st0 = dma(Sy, dstq.rearrange("(f p) w -> p f w", p=128)[
                                 :, :, t * 128:(t + 1) * 128],
                             pTD.rearrange("p (f w) -> p f w", w=128)[
                                 :, 0:nb, :],
                             [pv_t0], flow="w0s")
                stores.append(pv_st0)
            ev_w0[key] = p.collective("AllGather", ALU.bypass, [dstq.opt()],
                                      [outw.opt()], stores)

        phw = ExitStack()
        pL = sb(phw, "pL", [128, D], F32)
        ph1_done = [e_spxq, ev_xnfm[-1], e_si]
        ev_qg, ev_qu, ev_qd = [], [], []
        pv_ey = None
        pv_store = None
        for (src, dst, evlist) in ((w1g_d, q1g_in, ev_qg),
                                   (w1u_d, q1u_in, ev_qu)):
            src_t = rtile(src)
            dst_t = rtile(dst)
            for t in range(HID // NCORES // 128):
                el = dma(Sy, pL[:], src_t[t], [pv_ey] + ph1_done, flow="wlA")
                eo, pv_ey = mx_quant(pL[:], pO, D, 2.0, 6.0, SCR,
                                     [el] + ph1_done, [pv_store])
                pv_store = dma(Sc, dst_t[t], pO[:], [eo], flow="st" + dst_t.tensor.name)
                evlist.append(pv_store)
        src_t = rtile(w1d_d)
        dst_t = rtile(q1d_in)
        for t in range(D // NCORES // 128):
            for cchunk in range(4):
                el = dma(Sy, pL[:], src_t[t][:, cchunk * D:(cchunk + 1) * D],
                         [pv_ey] + ph1_done, flow="wlA")
                eo, pv_ey = mx_quant(pL[:], pO, D, 2.0, 6.0, SCR,
                                     [el] + ph1_done, [pv_store])
                pv_store = dma(Sc, dst_t[t][:, cchunk * D:(cchunk + 1) * D],
                               pO[:], [eo], flow="stD")
            ev_qd.append(pv_store)
        cc_qg = p.collective("AllGather", ALU.bypass, [q1g_in.opt()],
                             [q1g_out.opt()], ev_qg)
        cc_qu = p.collective("AllGather", ALU.bypass, [q1u_in.opt()],
                             [q1u_out.opt()], ev_qu)
        cc_qd = p.collective("AllGather", ALU.bypass, [q1d_in.opt()],
                             [q1d_out.opt()], ev_qd)
        if tap("qwg", (HID, D), BF16) is not None:
            dma(Sc, tap_t["qwg"], q1g_out, [cc_qg], flow="tap")
        phw.close()
        pool.close()
        p.barrier(include_cc=False)

        ph2 = ExitStack()
        gkl = sb(ph2, "gkl", [128, T], F32)
        An = sb(ph2, "An", [128, T], F32)
        gt1 = sb(ph2, "gt1", [128, 512], F32)
        qh = sb(ph2, "qh", [128, T], BF16)
        vh = sb(ph2, "vh", [128, T], BF16)
        oh = sb(ph2, "oh", [128, T], BF16)
        ktil = sb(ph2, "ktil", [128, 128], BF16)
        khat = sb(ph2, "khat", [128, 128], BF16)
        kth = sb(ph2, "kth", [128, 128], BF16)
        eqt = sb(ph2, "eqt", [128, 128], BF16)
        ekt = sb(ph2, "ekt", [128, 128], BF16)
        pbar = sb(ph2, "pbar", [128, 128], BF16)
        sbf = sb(ph2, "sbf", [128, 128], BF16)
        ssb = sb(ph2, "ssb", [128, 2 * 129], F32)
        wsl = sb(ph2, "wsl", [128, NF * 128], F32)
        wslb = [sb(ph2, f"wslb{i}", [128, NF * 128], BF16) for i in range(2)]
        gcolA = sb(ph2, "gcolA", [128, 8], F32)
        gcolB = sb(ph2, "gcolB", [128, 8], F32)
        gcolD = sb(ph2, "gcolD", [128, 8], F32)
        PS = [ps_(ph2, f"ps{i}", [128, 512], F32) for i in range(7)]
        PT = ps_(ph2, "pst", [128, 128], BF16)

        ev_ccg_w = []
        prev_tail = None
        prev_cast = None
        prev_sq = prev_so = None
        prev_ccg = None
        prev_dc = [None] * 8
        prev_scan_sread = None
        pv_kth = None
        for h in range(H_LOOP):
            def wslice(wd, buf, extra=()):
                nonlocal prev_cast
                e = dma(Sy, wsl.rearrange("p (f q) -> p f q", q=128),
                        wd.rearrange("(f p) o -> p f o", p=128)[
                            :, :, h * 128:(h + 1) * 128],
                        [prev_cast], flow="wsl")
                e2 = cp_(V, buf[:], wsl[:], [e] + list(extra))
                prev_cast = e2
                return e2

            e_wg = wslice(wg_d, wslb[0], [prev_tail])
            ev_scan = [None, None]
            for half in range(2):
                for kc in range(NF):
                    e_mm = mm_(PS[half][:],
                               wslb[0][:, kc * 128:(kc + 1) * 128],
                               xn_fm[:, kc * T + half * 512:
                                     kc * T + (half + 1) * 512],
                               start=(kc == 0), stop=(kc == NF - 1),
                               deps=([e_wg, prev_tail] + ev_xnfm)
                               if kc == 0 else ())
                e_c = ts_(V, gt1[:], PS[half][:], -87.0, ALU.max,
                          deps=[e_mm, prev_scan_sread])
                e_e = act_(gt1[:], gt1[:], AF.Exp, scale=cc(-1.0),
                           deps=[e_c, ccev(-1.0)])
                e_u = ts_(V, gt1[:], gt1[:], 1.0, ALU.add, deps=[e_e])
                e_l = act_(gkl[:, half * 512:(half + 1) * 512], gt1[:], AF.Ln,
                           deps=[e_u])
                ev_scan[half] = scan_(An[:, half * 512:(half + 1) * 512],
                                      gkl[:, half * 512:(half + 1) * 512],
                                      [e_l])
            ev_bc = [None] * 8
            for c in range(8):
                aoff_src = zcol[:] if c % 4 == 0 else \
                    An[:, (c // 4) * 512 + (c % 4) * 128 - 1:
                       (c // 4) * 512 + (c % 4) * 128]
                e_b1 = ts_(V, gcolA[:, c:c + 1], aoff_src, GN16, ALU.mult,
                           deps=[ev_scan[c // 4], ev_z, prev_dc[c]])
                e_b2 = ts_(V, gcolB[:, c:c + 1], aoff_src, -GN16, ALU.mult,
                           deps=[ev_scan[c // 4], prev_dc[c]])
                ev_bc[c] = (e_b1, e_b2)
            ev_ac = []
            for c in range(8):
                dst = acols[:, h * 8 + c:h * 8 + c + 1]
                if c % 4 == 0:
                    ev_ac.append(cp_(V, dst, zcol[:], [ev_z]))
                else:
                    cs = (c // 4) * 512 + (c % 4) * 128
                    ev_ac.append(cp_(V, dst, An[:, cs - 1:cs],
                                     [ev_scan[c // 4]]))
            e_wq = wslice(wq_d, wslb[1], [prev_tail])
            for half in range(2):
                for kc in range(NF):
                    e_mmq = mm_(PS[2 + half][:],
                                wslb[1][:, kc * 128:(kc + 1) * 128],
                                xn_fm[:, kc * T + half * 512:
                                      kc * T + (half + 1) * 512],
                                start=(kc == 0), stop=(kc == NF - 1),
                                deps=[e_wq, prev_tail] if kc == 0 else ())
            ev_qmm = e_mmq
            e_wk = wslice(wk_d, wslb[0], [e_mm])
            for half in range(2):
                for kc in range(NF):
                    e_mmk = mm_(PS[4 + half][:],
                                wslb[0][:, kc * 128:(kc + 1) * 128],
                                xn_fm[:, kc * T + half * 512:
                                      kc * T + (half + 1) * 512],
                                start=(kc == 0), stop=(kc == NF - 1),
                                deps=[e_wk, prev_tail] if kc == 0 else ())
            ev_kmm = e_mmk
            e_wv = wslice(wv_d, wslb[1], [ev_qmm])
            ev_vev = [None] * NT
            for tt in range(NT):
                sl6 = PS[6][:, (tt % 4) * 128:(tt % 4) * 128 + 128]
                dep0 = [e_wv, prev_tail]
                if tt > 0:
                    dep0.append(ev_vev[tt - 1])
                for kc in range(NF):
                    e_mmv = mm_(sl6,
                                xn_fm[:, kc * T + tt * 128:
                                      kc * T + (tt + 1) * 128],
                                wslb[1][:, kc * 128:(kc + 1) * 128],
                                start=(kc == 0), stop=(kc == NF - 1),
                                deps=dep0 if kc == 0 else ())
                ev_vev[tt] = cp_(V, vh[:, tt * 128:(tt + 1) * 128], sl6,
                                 [e_mmv])
            e_z0 = ms_(V, ssb[:], 0.0, [prev_ccg])
            ev_oev = [None] * 8
            e_qt_last = None
            pv_P = pv_tr = pv_D = pv_o1 = pv_oi = None
            pv_qt = pv_kt = None
            e_su = None
            e_dcr = None
            for c in range(8):
                b = c // 4
                cs = c * 128
                first_of_b = (c % 4 == 0)
                e_b1, e_b2 = ev_bc[c]
                e_eq = act_(eqt[:], An[:, cs:cs + 128], AF.Exp, scale=cc(-GN16),
                            bias=gcolA[:, c:c + 1],
                            deps=[e_b1, ev_scan[b], ccev(-GN16), pv_qt])
                e_ek = act_(ekt[:], An[:, cs:cs + 128], AF.Exp, scale=cc(GN16),
                            bias=gcolB[:, c:c + 1],
                            deps=[e_b2, ev_scan[b], ccev(GN16), pv_kt])
                qps = PS[2 + b][:, (c % 4) * 128:(c % 4) * 128 + 128]
                e_qt = stt_(V, qh[:, cs:cs + 128], eqt[:], SCL, qps,
                            ALU.mult, ALU.mult,
                            [e_eq, ev_qmm, prev_sq if c == 0 else None])
                e_qt_last = e_qt
                pv_qt = e_qt
                kps = PS[4 + b][:, (c % 4) * 128:(c % 4) * 128 + 128]
                e_kt = tt_(V, ktil[:], kps, ekt[:], ALU.mult,
                           [e_ek, ev_kmm, pv_P])
                pv_kt = e_kt
                e_dc = act_(gcolD[:, c:c + 1], An[:, cs + 127:cs + 128],
                            AF.Exp, scale=cc(-GN16), bias=gcolA[:, c:c + 1],
                            deps=[e_b1, ev_scan[b], prev_dc[c]])
                prev_dc[c] = e_dc
                e_kh = ts_(V, khat[:], ktil[:], gcolD[:, c:c + 1], ALU.mult,
                           deps=[e_kt, e_dc, pv_tr])
                e_tr = tr_(PT[:], khat[:],
                           [e_kh, ev_cst] + ([pv_kth] if pv_kth else []))
                pv_tr = e_tr
                e_kth = cp_(V, kth[:], PT[:], [e_tr, pv_D])
                pv_kth = e_kth
                e_P = mm_(PS[0][:, 0:128], ktil[:], qh[:, cs:cs + 128],
                          start=True, stop=True,
                          deps=[e_qt, e_kt,
                                ev_oev[c - 1] if c else prev_tail])
                pv_P = e_P
                e_pb = tt_(V, pbar[:], PS[0][:, 0:128], trit[:], ALU.mult,
                           [e_P, ev_cst, pv_o1])
                e_sb = cp_(V, sbf[:], ssb[:, b * 129:b * 129 + 128],
                           [e_z0, pv_oi, e_su])
                ops = PS[1][:, (c % 4) * 128:(c % 4) * 128 + 128]
                e_o1 = mm_(ops, vh[:, cs:cs + 128], pbar[:],
                           start=True, stop=first_of_b,
                           deps=[e_pb, ev_vev[c],
                                 ev_oev[c - 1] if c else prev_tail])
                pv_o1 = e_o1
                if not first_of_b:
                    e_o1 = mm_(ops, sbf[:], qh[:, cs:cs + 128],
                               start=False, stop=True, deps=[e_sb, e_qt])
                    pv_oi = e_o1
                e_oe = cp_(V, oh[:, cs:cs + 128], ops,
                           [e_o1, prev_so if c == 0 else None])
                ev_oev[c] = e_oe
                e_D = mm_(PS[6][:, 128:256], kth[:], vh[:, cs:cs + 128],
                          start=True, stop=True,
                          deps=[e_kth, ev_vev[c],
                                e_su if c else ev_vev[NT - 1]])
                pv_D = e_D
                e_su = stt_(V, ssb[:, b * 129:b * 129 + 128],
                            ssb[:, b * 129:b * 129 + 128],
                            gcolD[:, c:c + 1], PS[6][:, 128:256],
                            ALU.mult, ALU.add, [e_D, e_sb, e_dc])
                if c % 4 == 3:
                    e_dcr = act_(ssb[:, b * 129 + 128:b * 129 + 129],
                                 An[:, b * 512 + 511:b * 512 + 512], AF.Exp,
                                 scale=cc(-GN16), deps=[ev_scan[b], e_su])
            prev_scan_sread = e_dcr
            prev_sq = dma(Sc, sc_q[h], qh[:], [e_qt_last], flow="sq")
            prev_so = dma(Sc, sc_o[h], oh[:], ev_oev, flow="so")
            e_c0 = dma(Sc, ccg_in[h], ssb[:, 0:129], [e_su, e_dcr], flow="ccw")
            e_c1 = dma(Sc, ccg_in[H + h], ssb[:, 129:258], [e_su, e_dcr], flow="ccw")
            prev_ccg = e_c1
            ev_ccg_w += [prev_sq, prev_so, e_c0, e_c1]
            prev_tail = ev_oev[7]

        if "qh2" in taps:
            tap("qh2", (128, T), BF16)
            dma(Sc, tap_t["qh2"], qh[:], [prev_sq], flow="tap")
        if "vh2" in taps:
            tap("vh2", (128, T), BF16)
            dma(Sc, tap_t["vh2"], vh[:], ev_vev, flow="tap")
        if "oh2" in taps:
            tap("oh2", (128, T), BF16)
            dma(Sc, tap_t["oh2"], oh[:], ev_oev, flow="tap")
        if "An2" in taps:
            tap("An2", (128, T), F32)
            dma(Sc, tap_t["An2"], An[:], [prev_scan_sread, ev_scan[0], ev_scan[1]], flow="tap")
        if "ssb2" in taps:
            tap("ssb2", (128, 258), F32)
            dma(Sc, tap_t["ssb2"], ssb[:], [e_su, e_dcr], flow="tap")

        cc_g = p.collective("AllGather", ALU.bypass, [ccg_in.opt()],
                            [ccg_out.opt()], ev_ccg_w)
        ph2.close()
        ph12.close()
        p.barrier()

        ph3 = ExitStack()
        sfold = sb(ph3, "sfold", [128, 2 * H * 128], F32)
        o_all = sb(ph3, "o_all", [128, H * T], BF16)
        qh3 = sb(ph3, "qh3", [128, T], BF16)
        oh3 = sb(ph3, "oh3", [128, T], BF16)
        sbf3 = sb(ph3, "sbf3", [128, 128], BF16)
        PS3 = [ps_(ph3, f"ps3_{i}", [128, 512], F32) for i in range(4)]

        ph3a = ExitStack()
        tjl = sb(ph3a, "tjl", [128, 2 * H * 129], F32)
        e_fold = ms_(V, sfold[:], 0.0)
        fold_q = [e_fold] * (2 * H)
        for j in range(NCORES):
            e_lj = dma(Sy, tjl.rearrange("p (q c) -> p q c", c=129),
                       ccg_out[j].rearrange("q p c -> p q c"),
                       [cc_g, e_fold], flow="tj")
            e_m1 = ts_(V, tjl[:], tjl[:], fmsk[:, j:j + 1], ALU.mult,
                       deps=[e_lj, ev_cst])
            e_m2 = ts_(V, tjl.rearrange("p (q c) -> p q c", c=129)[
                           :, :, 128:129],
                       tjl.rearrange("p (q c) -> p q c", c=129)[
                           :, :, 128:129],
                       fmski[:, j:j + 1], ALU.add, deps=[e_m1, ev_cst])
            for q in range(2 * H):
                e_fold = stt_(V, sfold[:, q * 128:(q + 1) * 128],
                              sfold[:, q * 128:(q + 1) * 128],
                              tjl[:, q * 129 + 128:q * 129 + 129],
                              tjl[:, q * 129:q * 129 + 128],
                              ALU.mult, ALU.add, [e_m2, fold_q[q]])
                fold_q[q] = e_fold
        ph3a.close()

        ev_oall = []
        if H_LOOP < H:
            ev_oall.append(ms_(V, o_all[:], 0.0))
        pv_cor = None
        for h in range(H_LOOP):
            e_lq = dma(Sy, qh3[:], sc_q[h],
                       [e_fold, ev_oall[-1] if ev_oall else None], flow="lq")
            e_lo = dma(Sy, oh3[:], sc_o[h],
                       [e_fold, ev_oall[-1] if ev_oall else None], flow="lo")
            for c in range(8):
                b = c // 4
                qp = b * H + h
                e_bk = act_(cols[:, 200:201],
                            acols[:, h * 8 + c:h * 8 + c + 1],
                            AF.Exp, scale=cc(-GN16),
                            deps=[e_lq, ev_oall[-1] if ev_oall else None])
                e_sk = ts_(V, sbf3[:], sfold[:, qp * 128:(qp + 1) * 128],
                           cols[:, 200:201], ALU.mult,
                           deps=[e_bk, e_fold, pv_cor])
                e_cor = mm_(PS3[0][:, (c % 4) * 128:(c % 4) * 128 + 128],
                            sbf3[:], qh3[:, c * 128:(c + 1) * 128],
                            start=True, stop=True,
                            deps=[e_sk, e_lq,
                                  ev_oall[-1] if ev_oall else None])
                pv_cor = e_cor
                e_up = tt_(V, o_all[:, (h * 8 + c) * 128:
                                    (h * 8 + c + 1) * 128],
                           oh3[:, c * 128:(c + 1) * 128],
                           PS3[0][:, (c % 4) * 128:(c % 4) * 128 + 128],
                           ALU.add, [e_cor, e_lo])
                ev_oall.append(e_up)
        if tap("o", (128, H * T), BF16) is not None:
            dma(Sc, tap_t["o"], o_all[:], ev_oall, flow="tap")

        ph3b = ExitStack()
        wof = sb(ph3b, "wof", [128, NF * 512], F32)
        wofb = sb(ph3b, "wofb", [128, NF * 512], BF16)
        ev_wo = []
        pv_cw = None
        for oc in range(4):
            e_lw = dma(Sy, wof.rearrange("p (f q) -> p f q", q=512),
                       wo_d.rearrange("(f p) o -> p f o", p=128)[
                           :, :, oc * 512:(oc + 1) * 512],
                       [pv_cw, e_fold], flow="wo")
            e_cw = cp_(V, wofb[:], wof[:],
                       [e_lw, ev_wo[-1] if ev_wo else None])
            pv_cw = e_cw
            for tt in range(NT):
                bank = PS3[1 + tt % 2]
                dep0 = [e_cw] + ev_oall + \
                    ([ev_wo[-2]] if len(ev_wo) >= 2 else [])
                for hh in range(H):
                    e_mm = mm_(bank[:],
                               o_all[:, (hh * 8 + tt) * 128:
                                     (hh * 8 + tt + 1) * 128],
                               wofb[:, hh * 512:(hh + 1) * 512],
                               start=(hh == 0), stop=(hh == H - 1),
                               deps=dep0 if hh == 0 else ())
                e_ad = tt_(V, xres[:, tt * D + oc * 512:
                                   tt * D + (oc + 1) * 512],
                           xres[:, tt * D + oc * 512:
                                tt * D + (oc + 1) * 512],
                           bank[:], ALU.add, [e_mm, xres_last.get((tt, oc))])
                xres_last[(tt, oc)] = e_ad
                ev_wo.append(e_ad)
        ph3b.close()
        ph3.close()
        p.barrier()

        def ffn_pass(layer, xq_b, ev_act_ready, stk):
            L = f"l{layer}"
            wld = wldd = None
            wsg_g = wsg_u = wsg_d = None
            wfm_g = sb(stk, L + "wfm_g", [128, NF * 512], BF16,
                       is_transpose_dest=True)
            wfm_u = sb(stk, L + "wfm_u", [128, NF * 512], BF16,
                       is_transpose_dest=True)
            wdfm = [sb(stk, L + f"wdfm{i}", [128, 4 * D], BF16,
                       is_transpose_dest=True) for i in range(2)]
            gsl = sb(stk, L + "gsl", [128, NT * 512], BF16)
            usl = sb(stk, L + "usl", [128, NT * 512], BF16)
            pfm = [sb(stk, L + f"pfm{i}", [128, 4 * T], BF16,
                      is_transpose_dest=True) for i in range(2)]
            PS5 = [ps_(stk, L + f"ps5_{i}", [128, 512], F32) for i in range(8)]
            if layer == 1:
                iq = {
                    "A": sb(stk, "iqA", [128, 512], F32),
                    "R": sb(stk, "iqR", [128, 512], F32),
                    "I": sb(stk, "iqI", [128, 512], I32),
                    "S1": sb(stk, "iqS1", [128, 512], BF16),
                    "S2": sb(stk, "iqS2", [128, 512], BF16),
                    "R2": sb(stk, "iqR2", [128, 512], BF16),
                }

            NHC_ = HC_LOOP

            def prep(stage, hc, wdram, quant_src, wld_ap, wsg_ap, dst_fm,
                     ntile, wchunk, state, fsuf=""):
                fv = dst_fm.rearrange("p (f w) -> p f w", f=wchunk // 128)
                if layer == 0:
                    if wchunk == D:
                        cb, off = hc // 2, (hc % 2) * 512
                        src = quant_src.rearrange("(a r) w -> a r w", r=D)[cb]
                        src = src.rearrange("(kc p) w -> p kc w", p=128)
                        src = src[:, :, off:off + 512]
                    else:
                        src = quant_src.rearrange("(n r) w -> n r w",
                                                  r=512)[hc]
                        src = src.rearrange("(ks p) w -> p ks w", p=128)
                    e3 = dma(Sy, fv[:], src,
                             [state.get("use"), ev_w0[stage]],
                             flow=L + stage + "ld" + fsuf)
                    state["tr"] = e3
                    return [e3]
                evs = []
                for j in range(ntile):
                    if wchunk == D:
                        srcap = rtile(quant_src)[hc * ntile + j]
                    else:
                        srcap = rtile(quant_src)[j][:, hc * 512:(hc + 1) * 512]
                    e3 = dma(Sy, fv[:, :, j * 128:(j + 1) * 128],
                             srcap.rearrange("p (f q) -> p f q", q=128),
                             [state.get("use")] if j == 0 else (),
                             transpose=True, flow=L + stage + "t" + fsuf)
                    evs.append(e3)
                state["tr"] = evs[-1]
                return evs

            def down_block(dn_evd, dn_evp, buf, dn_w):
                for tt in range(NT):
                    for oc in range(4):
                        bank = PS5[4 + (tt * 4 + oc) % 4]
                        dep0 = dn_evd + dn_evp +                             ([ev_dn[-4]] if len(ev_dn) >= 4 else [])
                        for ks in range(4):
                            e_mm = mm_(bank[:],
                                       buf[:, ks * T + tt * 128:
                                           ks * T + (tt + 1) * 128],
                                       dn_w[:, ks * D + oc * 512:
                                            ks * D + (oc + 1) * 512],
                                       start=(ks == 0), stop=(ks == 3),
                                       deps=dep0 if ks == 0 else ())
                        e_ad = tt_(V, xres[:, tt * D + oc * 512:
                                           tt * D + (oc + 1) * 512],
                                   xres[:, tt * D + oc * 512:
                                        tt * D + (oc + 1) * 512],
                                   bank[:], ALU.add,
                                   [e_mm, xres_last.get((tt, oc))])
                        xres_last[(tt, oc)] = e_ad
                        ev_dn.append(e_ad)

            st_g = {}
            st_u = {}
            st_d = {}
            ev_gev = [None] * NT
            ev_uev = [None] * NT
            ev_dn = []
            pend_down = None
            prev_ptr = None

            for hc in range(NHC_):
                st_g["use"] = ev_gev[-1]
                evg = prep("g", hc, None,
                           q0g_out if layer == 0 else q1g_out,
                           None, wsg_g, wfm_g, 4, D, st_g)
                st_u["use"] = ev_uev[-1]
                evu = prep("u", hc, None,
                           q0u_out if layer == 0 else q1u_out,
                           None, wsg_u, wfm_u, 4, D, st_u)
                for tt in range(NT):
                    bank = PS5[tt % 2]
                    dep0 = evg + list(ev_act_ready) +                         [ev_gev[tt], ev_gev[tt - 2] if tt >= 2 else None]
                    for kc in range(NF):
                        e_mm = mm_(bank[:],
                                   xq_b[:, kc * T + tt * 128:
                                        kc * T + (tt + 1) * 128],
                                   wfm_g[:, kc * 512:(kc + 1) * 512],
                                   start=(kc == 0), stop=(kc == NF - 1),
                                   deps=dep0 if kc == 0 else ())
                    if layer == 0:
                        e_ev = act_(gsl[:, tt * 512:(tt + 1) * 512], bank[:],
                                    AF.Silu, scale=sinv_cols[:, tt:tt + 1],
                                    deps=[e_mm, prev_ptr])
                    else:
                        e_ev = act_(gsl[:, tt * 512:(tt + 1) * 512], bank[:],
                                    AF.Silu, deps=[e_mm, prev_ptr])
                    ev_gev[tt] = e_ev
                ev_p = [None] * NT
                pbuf = pfm[hc % 2]
                for tt in range(NT):
                    bank = PS5[2 + tt % 2]
                    dep0 = evu + [ev_uev[tt],
                                  ev_uev[tt - 2] if tt >= 2 else None]
                    for kc in range(NF):
                        e_mm = mm_(bank[:],
                                   xq_b[:, kc * T + tt * 128:
                                        kc * T + (tt + 1) * 128],
                                   wfm_u[:, kc * 512:(kc + 1) * 512],
                                   start=(kc == 0), stop=(kc == NF - 1),
                                   deps=dep0 if kc == 0 else ())
                    if layer == 0:
                        e_ev = act_(usl[:, tt * 512:(tt + 1) * 512], bank[:],
                                    AF.Identity,
                                    scale=sinv_cols[:, tt:tt + 1],
                                    deps=[e_mm])
                    else:
                        e_ev = act_(usl[:, tt * 512:(tt + 1) * 512], bank[:],
                                    AF.Identity, deps=[e_mm])
                    ev_uev[tt] = e_ev
                    sl = gsl[:, tt * 512:(tt + 1) * 512]
                    e_p = tt_(V, sl, sl, usl[:, tt * 512:(tt + 1) * 512],
                              ALU.mult, [ev_uev[tt], ev_gev[tt]])
                    if layer == 1:
                        e_p, _ = mx_quant(sl, sl, 512, 8.0, 7.5, iq, [e_p])
                    e_tp = dma(Sc, pbuf.rearrange("p (hs w) -> p hs w", hs=4)[
                                   :, :, tt * 128:(tt + 1) * 128],
                               sl.rearrange("p (hs q) -> p hs q", q=128),
                               [e_p, ev_dn[-1] if ev_dn else None],
                               transpose=True, flow=L + "pt" + str(hc % 2))
                    ev_p[tt] = e_tp
                st_d["use"] = ev_dn[-1] if ev_dn else None
                evd = prep("d", hc, None,
                           q0d_out if layer == 0 else q1d_out,
                           wldd, wsg_d, wdfm[hc % 2], NF, 512, st_d,
                           fsuf=str(hc % 2))
                if pend_down is not None:
                    dn_evd, dn_evp, dn_buf, dn_w = pend_down
                    down_block(dn_evd, dn_evp, dn_buf, dn_w)
                pend_down = (evd, ev_p, pbuf, wdfm[hc % 2])
                prev_ptr = ev_p[-1]
            dn_evd, dn_evp, dn_buf, dn_w = pend_down
            down_block(dn_evd, dn_evp, dn_buf, dn_w)
            if any((L + nm) in taps for nm in
                   ("dwdfm0", "dwdfm1", "dpfm0", "dpfm1", "dgsl", "dusl")):
                p.barrier()
                for nm, buf, w in (("dwdfm0", wdfm[0], 4 * D),
                                   ("dwdfm1", wdfm[1], 4 * D),
                                   ("dpfm0", pfm[0], 4 * T),
                                   ("dpfm1", pfm[1], 4 * T),
                                   ("dgsl", gsl, NT * 512),
                                   ("dusl", usl, NT * 512)):
                    if tap(L + nm, (128, w), BF16) is not None:
                        dma(Sc, tap_t[L + nm], buf[:], flow="tap")
                p.barrier()

        ph5 = ExitStack()
        xq_fm5 = sb(ph5, "xq_fm5", [128, NF * T], BF16)
        e_rld = dma(Sy, xq_fm5[:], sc_xq, flow="r5")
        ffn_pass(0, xq_fm5, [e_rld], ph5)
        ph5.close()
        p.barrier()
        if tap("x1", (T, D), F32) is not None:
            for t in range(NT):
                dma(Sc, rtile(tap_t["x1"])[t], xres[:, t * D:(t + 1) * D], flow="tap")
            p.barrier()

        ph67 = ExitStack()
        xq_fm6 = sb(ph67, "xq_fm6", [128, NF * T], BF16,
                    is_transpose_dest=True)
        ph6 = ExitStack()
        xn_fm6 = sb(ph6, "xn_fm6", [128, NF * T], BF16, is_transpose_dest=True)
        s6 = {
            "A": sb(ph6, "s6A", [128, D], F32),
            "R": sb(ph6, "s6R", [128, D], F32),
            "I": sb(ph6, "s6I", [128, D], I32),
            "S1": sb(ph6, "s6S1", [128, D], BF16),
            "S2": sb(ph6, "s6S2", [128, D], BF16),
            "R2": sb(ph6, "s6R2", [128, D], BF16),
        }
        s6X = sb(ph6, "s6X", [128, D], F32)
        s6BF = sb(ph6, "s6BF", [128, D], BF16)
        s6Q = sb(ph6, "s6Q", [128, D], BF16)

        ev_xnfm2 = []
        ev_xqfm2 = []
        pv_eyq = pv_tq6 = pv_tp6 = None
        for t in range(NT):
            xt = xres[:, t * D:(t + 1) * D]
            base = t * 24
            rc, nbc, e9 = ln_cols(xt, base, 1e-5, (), s6BF, pv_tp6)
            e_xn = act_(s6X[:], xt, AF.Identity, bias=nbc, scale=rc,
                        deps=[e9, pv_eyq])
            e_bf = cp_(V, s6BF[:], s6X[:], [e_xn, pv_tp6])
            e_tp = dma(Sy, xn_fm6.rearrange("p (f w) -> p f w", f=NF)[
                           :, :, t * 128:(t + 1) * 128],
                       s6BF.rearrange("p (f q) -> p f q", q=128), [e_bf],
                       transpose=True, flow="t3")
            pv_tp6 = e_tp
            ev_xnfm2.append(e_tp)
            eo, ey = mx_quant(s6X[:], s6Q, D, 8.0, 7.5, s6, [e_xn],
                              out_free=[pv_tq6])
            pv_eyq = ey
            e_tq = dma(Sy, xq_fm6.rearrange("p (f w) -> p f w", f=NF)[
                           :, :, t * 128:(t + 1) * 128],
                       s6Q.rearrange("p (f q) -> p f q", q=128), [eo],
                       transpose=True, flow="t4")
            pv_tq6 = e_tq
            ev_xqfm2.append(e_tq)
        if tap("qx1", (128, NF * T), BF16) is not None:
            dma(Sc, tap_t["qx1"], xq_fm6[:], ev_xqfm2, flow="tap")
        e_spxn = dma(Sc, sc_xn, xn_fm6[:], ev_xnfm2, flow="sp2")
        ph6.close()
        p.barrier()

        ph7 = ExitStack()
        ffn_pass(1, xq_fm6, ev_xqfm2, ph7)
        ph7.close()
        ph67.close()
        p.barrier()

        ph8 = ExitStack()
        qkvt = sb(ph8, "qkvt", [128, NT * 3072], BF16)
        a8 = sb(ph8, "a8", [128, 1024], F32)
        c8 = sb(ph8, "c8", [128, 64], F32)
        wl8 = sb(ph8, "wl8", [128, 2 * D], F32)
        wb8 = sb(ph8, "wb8", [128, D], BF16)
        wfm8 = sb(ph8, "wfm8", [128, NF * 512], BF16, is_transpose_dest=True)
        afm8 = sb(ph8, "afm8", [128, NF * 128], BF16, is_transpose_dest=True)
        PS8 = [ps_(ph8, f"ps8_{i}", [128, 512], F32) for i in range(4)]

        ph8a = ExitStack()
        xn_fm8 = sb(ph8a, "xn_fm8", [128, NF * T], BF16)
        ev_xn8 = [dma(Sy, xn_fm8[:], sc_xn, flow="r8")]

        ev_qkv = []
        pv_cast8 = None
        pv_w8 = []
        for oc in range(6):
            evw = []
            for j in range(4):
                buf = wl8[:, (j % 2) * D:(j % 2) * D + D]
                e1 = dma(Sy, buf, rtile(wqkv_d)[oc * 4 + j],
                         [evw[-2] if len(evw) >= 2 else pv_cast8],
                         flow="w8" + str(j % 2))
                e2 = cp_(V, wb8[:], buf, [e1, evw[-1] if evw else pv_cast8])
                pv_cast8 = e2
                e3 = dma(Sy, wfm8.rearrange("p (f q) -> p f q", f=NF)[
                             :, :, j * 128:(j + 1) * 128],
                         wb8.rearrange("p (f q) -> p f q", q=128),
                         [e2] + (pv_w8 if j == 0 else []), transpose=True,
                         flow="w8t")
                evw.append(e3)
            pv_w8 = []
            for tt in range(NT):
                bank = PS8[tt % 2]
                dep0 = evw + ev_xn8 + \
                    ([ev_qkv[-2]] if len(ev_qkv) >= 2 else [])
                for kc in range(NF):
                    e_mm = mm_(bank[:],
                               xn_fm8[:, kc * T + tt * 128:
                                      kc * T + (tt + 1) * 128],
                               wfm8[:, kc * 512:(kc + 1) * 512],
                               start=(kc == 0), stop=(kc == NF - 1),
                               deps=dep0 if kc == 0 else ())
                e_ev = cp_(V, qkvt[:, tt * 3072 + oc * 512:
                                   tt * 3072 + (oc + 1) * 512],
                           bank[:], [e_mm])
                ev_qkv.append(e_ev)
                pv_w8.append(e_ev)
        ph8a.close()
        if tap("qkv", (T, 3072), BF16) is not None:
            for t in range(NT):
                dma(Sc, rtile(tap_t["qkv"])[t],
                    qkvt[:, t * 3072:(t + 1) * 3072], ev_qkv, flow="tap")

        ev_att2 = []
        pv_gp = None
        for tt in range(4, NT):
            src = qkvt[:, tt * 3072:(tt + 1) * 3072]
            s0 = qkvt[:, (tt - 4) * 3072:(tt - 3) * 3072]
            q1v3 = src.rearrange("p (g c) -> p g c", g=4)[:, :, 0:512] \
                .rearrange("p g (i q) -> p g i q", q=128)
            k0v = s0.rearrange("p (g c) -> p g c", g=4)[:, :, 512:640]
            k1v = src.rearrange("p (g c) -> p g c", g=4)[:, :, 512:640]
            for gp2 in range(2):
                e_m0 = p.op(Gp, lambda e, qv=q1v3[:, gp2 * 2:gp2 * 2 + 2],
                            kk=k0v[:, gp2 * 2:gp2 * 2 + 2],
                            o=a8.rearrange("p (g i q) -> p g i q", g=2, q=128):
                            e.tensor_tensor(
                                out=o, in0=qv,
                                in1=kk.unsqueeze(2).broadcast_to(
                                    (128, 2, 4, 128)),
                                op=ALU.mult), ev_qkv + [pv_gp])
                e_r0 = red_(c8[:, 16 + gp2 * 8:24 + gp2 * 8],
                            a8.rearrange("p (x q) -> p x q", q=128),
                            ALU.add, [e_m0])
                pv_gp = e_r0
            for gp2 in range(2):
                e_m1 = p.op(Gp, lambda e, qv=q1v3[:, gp2 * 2:gp2 * 2 + 2],
                            kk=k1v[:, gp2 * 2:gp2 * 2 + 2],
                            o=a8.rearrange("p (g i q) -> p g i q", g=2, q=128):
                            e.tensor_tensor(
                                out=o, in0=qv,
                                in1=kk.unsqueeze(2).broadcast_to(
                                    (128, 2, 4, 128)),
                                op=ALU.mult), [pv_gp])
                e_r1 = red_(c8[:, 32 + gp2 * 8:40 + gp2 * 8],
                            a8.rearrange("p (x q) -> p x q", q=128),
                            ALU.add, [e_m1])
                pv_gp = e_r1
            e_d = tt_(V, c8[:, 48:64], c8[:, 32:48], c8[:, 16:32],
                      ALU.subtract, [pv_gp])
            e_e = act_(c8[:, 48:64], c8[:, 48:64], AF.Exp, scale=cc(-SCL),
                       deps=[e_d, ccev(-SCL)])
            e_a = ts_(V, c8[:, 48:64], c8[:, 48:64], 1.0, ALU.add, deps=[e_e])
            e_p1 = recip_(c8[:, 48:64], c8[:, 48:64], [e_a])
            for g in range(4):
                e_t = tt_(V, a8[:, g * 128:(g + 1) * 128],
                          src[:, g * 768 + 640:g * 768 + 768],
                          s0[:, g * 768 + 640:g * 768 + 768], ALU.subtract,
                          [e_p1])
                for qi in range(4):
                    h_ = g * 4 + qi
                    dstc = src[:, g * 768 + qi * 128:g * 768 + qi * 128 + 128]
                    e_sc = ts_(V, a8[:, 512 + h_ % 4 * 128:
                                     512 + (h_ % 4 + 1) * 128],
                               a8[:, g * 128:(g + 1) * 128],
                               c8[:, 48 + h_:49 + h_], ALU.mult, deps=[e_t])
                    e_o = tt_(V, dstc,
                              a8[:, 512 + h_ % 4 * 128:
                                 512 + (h_ % 4 + 1) * 128],
                              s0[:, g * 768 + 640:g * 768 + 768], ALU.add,
                              [e_sc])
                    ev_att2.append(e_o)
            pv_gp = e_o
        ev_att0 = []
        for tt in range(4):
            src = qkvt[:, tt * 3072:(tt + 1) * 3072]
            for g in range(4):
                for qi in range(4):
                    hcol = g * 768 + qi * 128
                    ev_att0.append(cp_(V, src[:, hcol:hcol + 128],
                                       src[:, g * 768 + 640:g * 768 + 768],
                                       ev_qkv + [pv_gp]))
        ev_att = ev_att2 + ev_att0
        if tap("attn1", (T, D), BF16) is not None:
            at = tap_t["attn1"]
            for t in range(NT):
                for g in range(4):
                    dma(Sc, rtile(at)[t][:, g * 512:(g + 1) * 512],
                        qkvt[:, t * 3072 + g * 768:t * 3072 + g * 768 + 512],
                        ev_att, flow="tap")

        ev_pj = []
        pv_cast8 = None
        pv_w8 = []
        for oc in range(4):
            evw = []
            for j in range(4):
                buf = wl8[:, (j % 2) * D:(j % 2) * D + D]
                e1 = dma(Sy, buf, rtile(wproj_d)[oc * 4 + j],
                         [evw[-2] if len(evw) >= 2 else pv_cast8],
                         flow="w8" + str(j % 2))
                e2 = cp_(V, wb8[:], buf, [e1, evw[-1] if evw else pv_cast8])
                pv_cast8 = e2
                e3 = dma(Sy, wfm8.rearrange("p (f q) -> p f q", f=NF)[
                             :, :, j * 128:(j + 1) * 128],
                         wb8.rearrange("p (f q) -> p f q", q=128),
                         [e2] + (pv_w8 if j == 0 else []) +
                         (ev_qkv if oc == 0 and j == 0 else []),
                         transpose=True, flow="w8t")
                evw.append(e3)
            pv_w8 = []
            for tt in range(NT):
                e_tr2 = None
                for g in range(4):
                    attv = qkvt[:, tt * 3072 + g * 768:
                                tt * 3072 + g * 768 + 512].rearrange(
                        "p (i q) -> p i q", q=128)
                    e_tr2 = dma(Sy, afm8[:, g * 512:(g + 1) * 512].rearrange(
                                    "p (i q) -> p i q", q=128),
                                attv,
                                ev_att + [ev_pj[-1] if ev_pj else None],
                                transpose=True, flow="af")
                bank = PS8[2 + tt % 2]
                dep0 = evw + [e_tr2] + \
                    ([ev_pj[-2]] if len(ev_pj) >= 2 else [])
                for kc in range(NF):
                    e_mm = mm_(bank[:],
                               afm8[:, kc * 128:(kc + 1) * 128],
                               wfm8[:, kc * 512:(kc + 1) * 512],
                               start=(kc == 0), stop=(kc == NF - 1),
                               deps=dep0 if kc == 0 else ())
                e_ad = tt_(V, xres[:, tt * D + oc * 512:
                                   tt * D + (oc + 1) * 512],
                           xres[:, tt * D + oc * 512:
                                tt * D + (oc + 1) * 512],
                           bank[:], ALU.add, [e_mm, xres_last.get((tt, oc))])
                xres_last[(tt, oc)] = e_ad
                ev_pj.append(e_ad)
                pv_w8.append(e_ad)
        ph8.close()
        p.barrier()

        ev_st = []
        y_t = rtile(y_out)
        for t in range(NT):
            ev_st.append(dma(Sy, y_t[t], xres[:, t * D:(t + 1) * D], flow="yst"))
        all_done = list(ev_st) + [(f[0], f[1]) for f in p.flows.values()]
        p.op(Sy, lambda e: e.nop(), all_done, inc=1)
        p.op(Sc, lambda e: e.nop(), all_done, inc=1)

        with nc.Block() as block:
            p.finalize(block)

    return nc, sorted(tap_t.keys())


_cache = {}


def kernel(L0_attn_x, L0_attn_Wq, L0_attn_Wk, L0_attn_Wv, L0_attn_Wg,
           L0_attn_Wo, L0_ffn_W_g, L0_ffn_W_u, L0_ffn_W_d,
           L1_attn_weight_qkv, L1_attn_weight_proj, L1_ffn_w_gate,
           L1_ffn_w_up, L1_ffn_w_down, _taps=(), _return_raw=False):
    x = np.ascontiguousarray(np.asarray(L0_attn_x, dtype=np.float32))
    f32 = lambda a: np.ascontiguousarray(np.asarray(a, dtype=np.float32))
    wq, wk, wv, wg, wo = map(f32, (L0_attn_Wq, L0_attn_Wk, L0_attn_Wv,
                                   L0_attn_Wg, L0_attn_Wo))
    wgf, wuf, wdf = map(f32, (L0_ffn_W_g, L0_ffn_W_u, L0_ffn_W_d))
    wqkv, wproj = map(f32, (L1_attn_weight_qkv, L1_attn_weight_proj))
    w1g, w1u, w1d = map(f32, (L1_ffn_w_gate, L1_ffn_w_up, L1_ffn_w_down))

    tri = np.triu(np.ones((128, 128), np.float32)).astype(ml_dtypes.bfloat16)
    ident = np.eye(128, dtype=ml_dtypes.bfloat16)

    key = tuple(_taps)
    if key not in _cache:
        _cache[key] = build_program(_taps)
    nc, tap_names = _cache[key]

    in_maps = []
    for c in range(NCORES):
        xs = np.ascontiguousarray(x[:, c * SL:(c + 1) * SL, :].reshape(T, D))
        fm = np.zeros((128, NCORES), np.float32)
        fm[:, :c] = 1.0
        in_maps.append({
            "x": xs, "wq": wq, "wk": wk, "wv": wv, "wg": wg, "wo": wo,
            "w0g_sh": np.ascontiguousarray(
                wgf[c * (HID // NCORES):(c + 1) * (HID // NCORES)]),
            "w0u_sh": np.ascontiguousarray(
                wuf[c * (HID // NCORES):(c + 1) * (HID // NCORES)]),
            "w0d_sh": np.ascontiguousarray(
                wdf[:, c * (HID // NCORES):(c + 1) * (HID // NCORES)]),
            "wqkv": wqkv, "wproj": wproj,
            "w1g_sh": np.ascontiguousarray(
                w1g[c * (HID // NCORES):(c + 1) * (HID // NCORES)]),
            "w1u_sh": np.ascontiguousarray(
                w1u[c * (HID // NCORES):(c + 1) * (HID // NCORES)]),
            "w1d_sh": np.ascontiguousarray(
                w1d[c * (D // NCORES):(c + 1) * (D // NCORES)]),
            "tri": tri, "ident": ident,
            "fmask": np.ascontiguousarray(fm),
            "fmaski": np.ascontiguousarray(1.0 - fm),
        })

    res = run_bass_kernel_spmd(nc, in_maps, core_ids=list(range(NCORES)))
    if _return_raw:
        return res
    out = np.empty((B, S, D), np.float32)
    for c in range(NCORES):
        out[:, c * SL:(c + 1) * SL, :] = np.asarray(
            res.results[c]["y"]).reshape(B, SL, D)
    return out
